# version 29
# baseline (speedup 1.0000x reference)
import contextlib
import math
from types import SimpleNamespace

import numpy as np

import concourse.bass as bass
import concourse.mybir as mybir
from concourse.bass_utils import run_bass_kernel_spmd

F32 = mybir.dt.float32
BF16 = mybir.dt.bfloat16
U8 = mybir.dt.uint8
AX = mybir.AxisListType.X
OP = mybir.AluOpType
AF = mybir.ActivationFunctionType

RC = float(np.float32(1.5 * 2 ** 23))
EPS = float(np.finfo(np.float32).eps)
SQG = float(np.float32(math.sqrt(0.125)))
CLP = 1e-12
INV16 = float(np.float32(2.0 ** -16))

CFG_FULL = dict(TC=16, CC=8, IC=24, H=16, NCORES=8)
CFG_SMALL = dict(TC=2, CC=2, IC=4, H=4, NCORES=2)
CFG_MED = dict(TC=8, CC=2, IC=4, H=4, NCORES=2)
CFG_MED6 = dict(TC=16, CC=8, IC=8, H=16, NCORES=2)


DBG = dict(standalone_a4=False, standalone_a5=False, no_pproj=False)


class Sem:
    def __init__(self, nc, es, name):
        self.h = es.enter_context(nc.semaphore(name))
        self.v = 0

    def inc(self, inst, n=1):
        inst.then_inc(self.h, n)
        self.v += n
        return self.v


XN_MAX, TMP0 = 0, 1
S1, R1 = 2, 3
UP_MAX, GT_MAX = 4, 5
S2, R2 = 6, 7
S3, R3 = 8, 9
PR_MAX, S4, R4 = 10, 11, 12
OR_MAX, S5, R5 = 13, 14, 15
PR_TRUE, R4S = 16, 17
NRC3, NRC42 = 19, 20


def build_program(cfg, taps=()):
    P = SimpleNamespace(**cfg)
    P.T, P.C, P.I = P.TC * 128, P.CC * 128, P.IC * 128
    P.H2 = P.H // 2
    assert P.H == 2 * P.CC
    P.taps = set(taps)

    nc = bass.Bass("TRN2", target_bir_lowering=False, debug=False)
    P.nc = nc
    es = contextlib.ExitStack()
    P.es = es
    T, C, I, TC, CC, IC, H = P.T, P.C, P.I, P.TC, P.CC, P.IC, P.H

    D = SimpleNamespace()
    P.D = D
    D.x = nc.dram_tensor("x", [T, C], F32, kind="ExternalInput").ap()
    for w in ("wq", "wk", "wv", "wp"):
        setattr(D, w, nc.dram_tensor(w, [C, C], F32, kind="ExternalInput").ap())
    for b2 in ("bq", "bk", "bv", "bp"):
        setattr(D, b2, nc.dram_tensor(b2, [C], F32, kind="ExternalInput").ap())
    D.wg = nc.dram_tensor("wg", [I, C], F32, kind="ExternalInput").ap()
    D.wu = nc.dram_tensor("wu", [I, C], F32, kind="ExternalInput").ap()
    D.wd = nc.dram_tensor("wd", [C, I], F32, kind="ExternalInput").ap()
    D.y = nc.dram_tensor("y", [T, C], F32, kind="ExternalOutput").ap()
    D.s_up = nc.dram_tensor("s_up", [IC, 128, T], BF16, kind="Internal").ap()
    D.s_gate = nc.dram_tensor("s_gate", [IC, 128, T], BF16, kind="Internal").ap()
    D.s_prod = nc.dram_tensor("s_prod", [IC, 128, T], BF16, kind="Internal").ap()
    D.s_orw = nc.dram_tensor("s_orw", [IC, 128, T], BF16, kind="Internal").ap()
    D.s_wdt = nc.dram_tensor("s_wdt", [IC, 128, CC, 128], BF16, kind="Internal").ap()
    D.s_wpt = nc.dram_tensor("s_wpt", [CC, 128, C], BF16, kind="Internal").ap()
    D.s_nm = nc.dram_tensor("s_nm", [CC, 2], F32, kind="Internal").ap()
    D.cc = [(nc.dram_tensor(f"cci{i}", [1, 128], F32, kind="Internal").ap(),
             nc.dram_tensor(f"cco{i}", [1, 128], F32, kind="Internal").ap())
            for i in range(4)]
    P.tap_t = {}
    for name, shape, dt in (
        ("xnfm", [128, CC * T], BF16), ("q", [128, CC * T], BF16),
        ("k", [128, CC * T], BF16), ("v", [128, TC * H * 65], BF16),
        ("sml", [128, 32], F32), ("negm", [128, H], F32),
        ("ao", [128, P.H2 * T], BF16), ("aot", [128, TC * 128], BF16),
        ("ffo", [T, C], F32),
    ):
        if name in P.taps:
            P.tap_t[name] = nc.dram_tensor("tp_" + name, shape, dt,
                                           kind="ExternalOutput").ap()

    SB = lambda n, sh, dt: es.enter_context(nc.sbuf_tensor(n, sh, dt))
    PS = lambda n, sh, dt: es.enter_context(nc.psum_tensor(n, sh, dt))
    S = SimpleNamespace()
    P.S = S

    def bmax(*sizes):
        return (max(sizes) + 1) // 2

    ao_scratch = max(20 * C,
                     2 * C * CC + 14 * C,
                     24 * C + 4 * T,
                     9 * I,
                     4 * min(2048, T))
    vt_scratch = max(16 * T,
                     IC * 512 * 2 + 2 * C * 2 + 2 * 512 * 2 + 512 * 4)
    xn_scratch = max(CC * C * 2 + 2 * C * 4, 7 * I)
    S.xn = SB("xn_fm", [128, max(CC * T, (xn_scratch + 1) // 2)], BF16)
    S.q = SB("q_fm", [128, max(CC * T, (TC * C * 4 // 2 + 1) // 2)], BF16)
    S.k = SB("k_fm", [128, max(CC * T, (TC * C * 4 // 2 + 1) // 2)], BF16)
    S.v = SB("v_tm", [128, max(TC * H * 65, (vt_scratch + 1) // 2)], BF16)
    S.ao = SB("ao_fm", [128, max(P.H2 * T, (ao_scratch + 1) // 2)], BF16)
    S.pt = SB("pt_buf", [128, max(2048, 2 * C, T)], BF16)
    S.sav = SB("stage_av", [65, 2048], F32)
    S.aot = SB("ao_tm", [128, max(TC * 128, 2 * C)], BF16)
    S.xf = SB("xf32", [128, 2048], F32)
    S.ident_f = SB("ident_f", [128, 128], F32)
    S.ident_b = SB("ident_b", [128, 128], BF16)
    S.iota_a = SB("iota_a", [128, 128], F32)
    S.iota_b = SB("iota_b", [128, 1], F32)
    S.ones1 = SB("ones1", [1, 128], BF16)
    S.oblk = SB("oblk", [128, 2], BF16)
    S.bqt = SB("bqt", [128, CC], F32)
    S.bkt = SB("bkt", [128, CC], F32)
    S.bvr = SB("bvr", [1, C], BF16)
    S.bpr = SB("bpr", [1, C], BF16)
    S.negm = SB("negm", [128, H], F32)
    S.nq2 = SB("nq2", [2, CC + 1], F32)
    S.nk2 = SB("nk2", [2, CC + 1], F32)
    S.nm2 = SB("nm2", [2, CC], F32)
    S.sml = SB("sml", [128, 32], F32)
    S.row = SB("row", [1, 512], F32)
    S.wsc = SB("wsc", [128, 96], F32)
    S.wsc2 = SB("wsc2", [128, 96], F32)
    S.lnag = SB("lnag", [128, 16], F32)
    S.c_eps = SB("c_eps", [128, 1], F32)
    S.c_rc = SB("c_rc", [128, 1], F32)
    S.sig = SB("sig", [128, TC], F32)
    S.arena = SB("arena", [128, 6144], BF16)

    S.psa = PS("psa", [128, 1024], F32)
    S.psb = PS("psb", [128, 1024], F32)
    S.psc = PS("psc", [128, 1024], F32)
    S.psd = PS("psd", [128, 1024], F32)

    def view(base, byte_off, part, free, dt):
        esz = mybir.dt.size(dt)
        bb = base[0:part, :].bitcast(U8)
        assert byte_off % 4 == 0 and byte_off + free * esz <= bb.shape[1], \
            (byte_off, free, esz, bb.shape)
        return bb[:, byte_off:byte_off + free * esz].bitcast(dt)

    P.view = view

    for name in ("sL", "sG", "sU", "sT", "sV", "sS", "sC", "sO",
                 "sLa", "sLb", "sGa", "sGb", "sOa", "sOb", "s6T", "s6V", "sX",
                 "sY", "sW0", "sW1", "sW2", "sW3"):
        setattr(P, name, Sem(nc, es, name))
    P.sLp = [P.sLa, P.sLb]
    P.sGp = [P.sGa, P.sGb]
    P.sOp = [P.sOa, P.sOb]

    _b0_init(P)
    _b1_ln(P)
    _allreduce(P, 0, [XN_MAX])
    _scales(P, [(XN_MAX, S1, R1, 32767.0, CLP, None)])
    _b3_qkv(P)
    _b4_norms(P)
    _b67_mega(P)
    _scales(P, [(UP_MAX, S2, R2, 32767.0, CLP, None),
                (GT_MAX, S3, R3, 32767.0, CLP, None)])
    _mulcol(P, NRC3, S3, -RC)
    _b8_chain1(P)
    if DBG["standalone_a4"]:
        _allreduce(P, 2, [PR_TRUE])
    _scales(P, [(PR_TRUE, S4, R4, 32767.0, CLP, (R4S, INV16))])
    _b9_chain2(P)
    if DBG["standalone_a5"]:
        _allreduce(P, 3, [OR_MAX])
    _scales(P, [(OR_MAX, S5, R5, 32767.0, CLP, None)])
    _b10_down(P)
    _b12_taps(P)

    es.close()
    return nc


def _b0_init(P):
    nc, S, D = P.nc, P.S, P.D
    with nc.Block() as b:
        @b.gpsimd
        def _(g):
            g.iota(S.iota_a[:], [[1, 128]], channel_multiplier=0,
                   allow_small_or_imprecise_dtypes=True)
            g.iota(S.iota_b[:], [[0, 1]], channel_multiplier=1,
                   allow_small_or_imprecise_dtypes=True)
            g.dma_start(out=S.bvr[:], in_=D.bv.unsqueeze(0)).then_inc(P.sG.h, 16)
            g.dma_start(out=S.bpr[:], in_=D.bp.unsqueeze(0)).then_inc(P.sG.h, 16)
            P.sG.v += 32
            g.wait_ge(P.sG.h, P.sG.v)

        @b.sync
        def _(s):
            s.dma_start(out=S.bqt[:], in_=D.bq.rearrange("(c p) -> p c", p=128),
                        allow_slow_non_contiguous=True).then_inc(P.sL.h, 16)
            s.dma_start(out=S.bkt[:], in_=D.bk.rearrange("(c p) -> p c", p=128),
                        allow_slow_non_contiguous=True).then_inc(P.sL.h, 16)
            P.sL.v += 32
            s.wait_ge(P.sL.h, P.sL.v)

        @b.vector
        def _(v):
            v.memset(S.ones1[:], 1.0)
            v.memset(S.oblk[:], 0.0)
            v.drain()
            v.memset(S.oblk[0:64, 0:1], 1.0)
            v.memset(S.oblk[64:128, 1:2], 1.0)
            v.memset(bass.AP(S.v, 64, [[S.v.shape[1], 128], [P.H * 65, P.TC], [65, P.H]]), 1.0)
            v.memset(S.sml[:], 0.0)
            v.memset(S.row[:], 0.0)
            v.memset(S.negm[:], 0.0)
            v.memset(S.nq2[:], 0.0)
            v.memset(S.nk2[:], 0.0)
            v.memset(S.sig[:], 0.0)
            v.memset(S.lnag[:], 0.0)
            v.memset(S.c_eps[:], 1e-5)
            v.memset(S.c_rc[:], RC)

    with nc.Block() as b:
        @b.vector
        def _(v):
            v.tensor_scalar(S.ident_f[:], S.iota_a[:], S.iota_b[:], None, OP.is_equal)
            v.drain()
            v.tensor_copy(S.ident_b[:], S.ident_f[:])
            v.tensor_scalar_mul(S.bqt[:], S.bqt[:], SQG)
            v.tensor_scalar_mul(S.bkt[:], S.bkt[:], SQG)


def _b1_ln(P):
    nc, S, D = P.nc, P.S, P.D
    TC, CC, C, T = P.TC, P.CC, P.C, P.T
    bV, bS, bT = P.sV.v, P.sS.v, P.sT.v
    bLp = [P.sLa.v, P.sLb.v]
    xnt = lambda par: P.view(S.pt, par * 2 * C, 128, C, BF16)

    with nc.Block() as b:
        @b.sync
        def _(s):
            for tc in range(0, TC, 2):
                if tc >= 2:
                    s.wait_ge(P.sV.h, bV + 2 * (tc - 1))
                s.dma_start(out=S.xf[:, (tc % 2) * 1024:(tc % 2) * 1024 + C],
                            in_=D.x[tc * 128:(tc + 1) * 128, :]).then_inc(P.sLp[tc % 2].h, 16)
                P.sLp[tc % 2].v += 16

        @b.vector
        def _(v):
            for tc in range(TC):
                sl = S.xf[:, (tc % 2) * 1024:(tc % 2) * 1024 + C]
                v.wait_ge(P.sLp[tc % 2].h, bLp[tc % 2] + 16 * (tc // 2 + 1))
                for hi, h0 in enumerate(range(0, C, 512)):
                    v.bn_stats(S.lnag[:, hi * 6:hi * 6 + 6],
                               sl[:, h0:h0 + min(512, C - h0)])
                v.drain()
                v.bn_aggr(S.lnag[:, 12:14], S.lnag[:, 0:6 * ((C + 511) // 512)])
                v.drain()
                v.tensor_scalar_mul(S.lnag[:, 14:15], S.lnag[:, 12:13], -1.0)
                v.drain().then_inc(P.sV.h, 1)
                P.sV.v += 1
                v.wait_ge(P.sS.h, bS + 2 * tc + 1)
                v.reciprocal(S.lnag[:, 15:16], S.lnag[:, 15:16])
                v.drain()
                v.tensor_scalar(xnt(tc % 2), sl, S.lnag[:, 14:15], S.lnag[:, 15:16],
                                OP.add, OP.mult)
                v.drain()
                v.reduce_max(S.sml[:, TMP0:TMP0 + 1], xnt(tc % 2), axis=AX,
                             apply_absolute_value=True)
                v.drain()
                v.tensor_max(S.sml[:, XN_MAX:XN_MAX + 1], S.sml[:, XN_MAX:XN_MAX + 1],
                             S.sml[:, TMP0:TMP0 + 1])
                v.drain().then_inc(P.sV.h, 1)
                P.sV.v += 1

        @b.scalar
        def _(s):
            def xld(tc):
                if tc >= TC:
                    return
                s.dma_start(out=S.xf[:, (tc % 2) * 1024:(tc % 2) * 1024 + C],
                            in_=D.x[tc * 128:(tc + 1) * 128, :]).then_inc(P.sLp[tc % 2].h, 16)
                P.sLp[tc % 2].v += 16
            if TC > 1:
                xld(1)
            for tc in range(TC):
                s.wait_ge(P.sV.h, bV + 2 * tc + 1)
                s.activation(S.lnag[:, 15:16], S.lnag[:, 13:14], AF.Sqrt,
                             bias=S.c_eps[:]).then_inc(P.sS.h, 1)
                P.sS.v += 1
                s.wait_ge(P.sT.h, bT + tc + 1)
                pv = (S.psa if tc % 2 == 0 else S.psb)[:].bitcast(BF16)[:, 0:C]
                inst = None
                for j in range(CC):
                    inst = s.activation(
                        S.xn[:, j * T + tc * 128:j * T + (tc + 1) * 128],
                        pv[:, j * 128:(j + 1) * 128], AF.Identity)
                inst.then_inc(P.sS.h, 1)
                P.sS.v += 1
                if tc % 2 == 1:
                    xld(tc + 2)

        @b.tensor
        def _(t):
            for tc in range(TC):
                t.wait_ge(P.sV.h, bV + 2 * tc + 2)
                if tc >= 2:
                    t.wait_ge(P.sS.h, bS + 2 * (tc - 1))
                ps = (S.psa if tc % 2 == 0 else S.psb)[:].bitcast(BF16)
                for ic in range(CC):
                    inst = t.transpose(ps[:, ic * 128:(ic + 1) * 128],
                                xnt(tc % 2)[:, ic * 128:(ic + 1) * 128],
                                S.ident_b[:])
                inst.then_inc(P.sT.h, 1)
                P.sT.v += 1


def _allreduce(P, idx, cols):
    nc, S = P.nc, P.S
    cin, cout = P.D.cc[idx]
    with nc.Block() as b:
        @b.scalar
        def _(s):
            for j, col in enumerate(cols):
                s.dma_start(out=S.row[0:1, 128 + j * 128:256 + j * 128],
                            in_=S.sml[:, col:col + 1]).then_inc(P.sU.h, 16)
                P.sU.v += 16

        @b.vector
        def _(v):
            v.wait_ge(P.sU.h, P.sU.v)
            for j, col in enumerate(cols):
                v.reduce_max(S.row[0:1, j:j + 1],
                             S.row[0:1, 128 + j * 128:256 + j * 128], axis=AX)
            v.drain()
            v.drain().then_inc(P.sV.h, 1)
            P.sV.v += 1

        @b.gpsimd
        def _(g):
            g.wait_ge(P.sV.h, P.sV.v)
            g.dma_start(out=cin[:], in_=S.row[0:1, 0:128]).then_inc(P.sG.h, 16)
            P.sG.v += 16
            g.wait_ge(P.sG.h, P.sG.v)
            g.collective_compute(
                "AllReduce", OP.max, replica_groups=[list(range(P.NCORES))],
                ins=[cin[:]], outs=[cout[:]],
            ).then_inc(P.sC.h, 1)
            P.sC.v += 1
            g.wait_ge(P.sC.h, P.sC.v)
            for j, col in enumerate(cols):
                g.dma_start(out=S.sml[:, col:col + 1],
                            in_=cout[0:1, j:j + 1].broadcast_to([128, 1])).then_inc(P.sG.h, 16)
                P.sG.v += 16
            g.wait_ge(P.sG.h, P.sG.v)


def _scales(P, specs):
    nc, S = P.nc, P.S
    with nc.Block() as b:
        @b.vector
        def _(v):
            for g_col, s_col, r_col, qmax, clamp, extra in specs:
                v.tensor_scalar(S.sml[:, s_col:s_col + 1], S.sml[:, g_col:g_col + 1],
                                float(np.float32(1.0) / np.float32(qmax)), clamp,
                                OP.mult, OP.max)
                v.drain()
                v.reciprocal(S.sml[:, r_col:r_col + 1], S.sml[:, s_col:s_col + 1])
                v.drain()
                if extra is not None:
                    ec, em = extra
                    v.tensor_scalar_mul(S.sml[:, ec:ec + 1], S.sml[:, r_col:r_col + 1], em)
                    v.drain()


def _mulcol(P, dst, srcc, m):
    nc, S = P.nc, P.S
    with nc.Block() as b:
        @b.vector
        def _(v):
            v.tensor_scalar_mul(S.sml[:, dst:dst + 1], S.sml[:, srcc:srcc + 1], m)
            v.drain()


def _quant_row(P, v, wld_ap, wtmp_ap, wqc_ap, extra_scale):
    S = P.S
    v.reduce_max(S.wsc[:, 0:1], wld_ap, axis=AX, apply_absolute_value=True)
    v.drain()
    v.tensor_scalar(S.wsc[:, 1:2], S.wsc[:, 0:1], float(np.float32(1.0) / np.float32(127.0)),
                    EPS, OP.mult, OP.max)
    v.drain()
    v.reciprocal(S.wsc[:, 2:3], S.wsc[:, 1:2])
    if extra_scale is not None:
        v.drain()
        v.tensor_scalar_mul(S.wsc[:, 1:2], S.wsc[:, 1:2], extra_scale)
    v.drain()
    v.tensor_scalar(wtmp_ap, wld_ap, S.wsc[:, 2:3], RC, OP.mult, OP.add)
    v.drain()
    v.tensor_scalar(wqc_ap, wtmp_ap, RC, S.wsc[:, 1:2], OP.subtract, OP.mult)
    v.drain()


def _quant_blk(P, v, wld_ap, wtmp_ap, wtmp2_ap, wqc_ap, width):
    S = P.S
    nb = width // 32
    w3 = wld_ap.rearrange("p (b k) -> p b k", k=32)
    v.tensor_reduce(S.wsc[:, 0:nb].unsqueeze(2), w3, axis=AX, op=OP.max,
                    apply_absolute_value=True)
    v.drain()
    v.tensor_scalar(S.wsc[:, 0:nb], S.wsc[:, 0:nb], float(np.float32(1.0) / np.float32(7.0)),
                    CLP, OP.mult, OP.max)
    v.drain()
    v.reciprocal(S.wsc2[:, 0:nb], S.wsc[:, 0:nb])
    v.drain()
    v.tensor_tensor(wtmp_ap.rearrange("p (b k) -> p b k", k=32), w3,
                    S.wsc2[:, 0:nb].unsqueeze(2).broadcast_to([128, nb, 32]), OP.mult)
    v.drain()
    v.tensor_scalar(wtmp2_ap, wtmp_ap, RC, RC, OP.add, OP.subtract)
    v.drain()
    v.tensor_tensor(wqc_ap.rearrange("p (b k) -> p b k", k=32),
                    wtmp2_ap.rearrange("p (b k) -> p b k", k=32),
                    S.wsc[:, 0:nb].unsqueeze(2).broadcast_to([128, nb, 32]), OP.mult)
    v.drain()


def _xn_ap(P):
    return P.S.xn[:, 0:P.CC * P.T]


def _b3_qkv(P):
    nc, S, D, view = P.nc, P.S, P.D, P.view
    TC, CC, C, T, H = P.TC, P.CC, P.C, P.T, P.H
    nf = max(1, T // 1024)

    for wdram, dst, bias in ((D.wq, S.q, S.bqt), (D.wk, S.k, S.bkt)):
        wld = lambda p: view(S.ao, p * 4 * C, 128, C, F32)
        wtmp = view(S.ao, 8 * C, 128, C, F32)
        wqc = lambda p: view(S.ao, 12 * C + p * 2 * C, 128, C, BF16)
        wt = lambda p: view(S.ao, 16 * C + p * 2 * C, 128, C, BF16)
        bV, bS, bT = P.sV.v, P.sS.v, P.sT.v
        bLp = [P.sLa.v, P.sLb.v]
        v_quant = lambda oc: bV + oc + 1
        t_transp = lambda oc: bT + oc * (1 + nf) + 1
        t_fill = lambda oc, fi: bT + oc * (1 + nf) + 1 + fi + 1
        s_wt = lambda oc: bS + oc * (1 + nf) + 1
        s_fill = lambda oc, fi: bS + oc * (1 + nf) + 1 + fi + 1

        with nc.Block() as b:
            @b.sync
            def _(s):
                for oc in range(0, CC, 2):
                    if oc >= 2:
                        s.wait_ge(P.sV.h, v_quant(oc - 2))
                    s.dma_start(out=wld(oc % 2),
                                in_=wdram[oc * 128:(oc + 1) * 128, :]).then_inc(P.sLp[oc % 2].h, 16)
                    P.sLp[oc % 2].v += 16

            @b.vector
            def _(v):
                for oc in range(CC):
                    v.wait_ge(P.sLp[oc % 2].h, bLp[oc % 2] + 16 * (oc // 2 + 1))
                    if oc >= 2:
                        v.wait_ge(P.sT.h, t_transp(oc - 2))
                    _quant_row(P, v, wld(oc % 2), wtmp, wqc(oc % 2), SQG)
                    v.drain().then_inc(P.sV.h, 1)
                    P.sV.v += 1

            @b.tensor
            def _(t):
                for oc in range(CC):
                    t.wait_ge(P.sV.h, v_quant(oc))
                    if oc >= 2:
                        t.wait_ge(P.sS.h, s_wt(oc - 2))
                    ps = (S.psc if oc % 2 == 0 else S.psd)[:].bitcast(BF16)
                    for ic in range(CC):
                        inst = t.transpose(ps[:, ic * 128:(ic + 1) * 128],
                                    wqc(oc % 2)[:, ic * 128:(ic + 1) * 128],
                                    S.ident_b[:])
                    inst.then_inc(P.sT.h, 1)
                    P.sT.v += 1
                    t.wait_ge(P.sS.h, s_wt(oc))
                    for fi, t0 in enumerate(range(0, T, 1024)):
                        fw = min(1024, T - t0)
                        pm = S.psa if (oc * nf + fi) % 2 == 0 else S.psb
                        g = oc * nf + fi
                        if g >= 2:
                            po, pf = divmod(g - 2, nf)
                            t.wait_ge(P.sS.h, s_fill(po, pf))
                        for sl in range(0, fw, 512):
                            sw = min(512, fw - sl)
                            for ic in range(CC):
                                inst = t.matmul(pm[:, sl:sl + sw],
                                         wt(oc % 2)[:, ic * 128:(ic + 1) * 128],
                                         _xn_ap(P)[:, ic * T + t0 + sl:ic * T + t0 + sl + sw],
                                         start=(ic == 0), stop=(ic == CC - 1))
                        inst.then_inc(P.sT.h, 1)
                        P.sT.v += 1

            @b.scalar
            def _(s):
                def wload(oc):
                    if oc >= CC:
                        return
                    s.dma_start(out=wld(oc % 2),
                                in_=wdram[oc * 128:(oc + 1) * 128, :]
                                ).then_inc(P.sLp[oc % 2].h, 16)
                    P.sLp[oc % 2].v += 16
                if CC > 1:
                    wload(1)
                for oc in range(CC):
                    s.wait_ge(P.sT.h, t_transp(oc))
                    pt = (S.psc if oc % 2 == 0 else S.psd)[:].bitcast(BF16)[:, 0:C]
                    s.activation(wt(oc % 2), pt, AF.Identity).then_inc(P.sS.h, 1)
                    P.sS.v += 1
                    if oc % 2 == 1:
                        wload(oc + 2)
                    for fi, t0 in enumerate(range(0, T, 1024)):
                        fw = min(1024, T - t0)
                        pm = S.psa if (oc * nf + fi) % 2 == 0 else S.psb
                        s.wait_ge(P.sT.h, t_fill(oc, fi))
                        s.activation(dst[:, oc * T + t0:oc * T + t0 + fw], pm[:, 0:fw],
                                     AF.Identity, bias=bias[:, oc:oc + 1]).then_inc(P.sS.h, 1)
                        P.sS.v += 1

    wvt = view(S.ao, 0, 128, CC * C, BF16)
    wld1 = lambda p: view(S.ao, 2 * CC * C + p * 4 * C, 128, C, F32)
    wtmp1 = view(S.ao, 2 * CC * C + 8 * C, 128, C, F32)
    wqc1 = view(S.ao, 2 * CC * C + 12 * C, 128, C, BF16)
    bL, bV, bS, bT = P.sL.v, P.sV.v, P.sS.v, P.sT.v
    bU = P.sU.v
    vlsem = [P.sL, P.sU]
    bVL = [bL, bU]
    with nc.Block() as b:
        @b.sync
        def _(s):
            for oc in range(0, CC, 2):
                if oc >= 2:
                    s.wait_ge(P.sV.h, bV + oc - 1)
                s.dma_start(out=wld1(oc % 2),
                            in_=D.wv[oc * 128:(oc + 1) * 128, :]
                            ).then_inc(vlsem[oc % 2].h, 16)
                vlsem[oc % 2].v += 16

        @b.vector
        def _(v):
            for oc in range(CC):
                v.wait_ge(vlsem[oc % 2].h, bVL[oc % 2] + 16 * (oc // 2 + 1))
                if oc >= 1:
                    v.wait_ge(P.sT.h, bT + oc)
                _quant_row(P, v, wld1(oc % 2), wtmp1, wqc1, None)
                v.drain().then_inc(P.sV.h, 1)
                P.sV.v += 1

        @b.tensor
        def _(t):
            for oc in range(CC):
                t.wait_ge(P.sV.h, bV + oc + 1)
                if oc >= 2:
                    t.wait_ge(P.sS.h, bS + oc - 1)
                ps = (S.psc if oc % 2 == 0 else S.psd)[:].bitcast(BF16)
                for ic in range(CC):
                    inst = t.transpose(ps[:, ic * 128:(ic + 1) * 128],
                                wqc1[:, ic * 128:(ic + 1) * 128],
                                S.ident_b[:])
                inst.then_inc(P.sT.h, 1)
                P.sT.v += 1
            t.wait_ge(P.sS.h, bS + CC)
            for tc in range(TC):
                pm = S.psa if tc % 2 == 0 else S.psb
                if tc >= 2:
                    t.wait_ge(P.sS.h, bS + CC + tc - 1)
                for sl in range(0, C, 512):
                    sw = min(512, C - sl)
                    for ic in range(CC):
                        inst = t.matmul(pm[:, sl:sl + sw],
                                 _xn_ap(P)[:, ic * T + tc * 128:ic * T + (tc + 1) * 128],
                                 wvt[:, ic * C + sl:ic * C + sl + sw],
                                 start=(ic == 0), stop=False)
                    inst = t.matmul(pm[:, sl:sl + sw], S.ones1[:],
                             S.bvr[0:1, sl:sl + sw], start=False, stop=True)
                inst.then_inc(P.sT.h, 1)
                P.sT.v += 1

        @b.scalar
        def _(s):
            def vload(oc):
                if oc >= CC:
                    return
                s.dma_start(out=wld1(oc % 2),
                            in_=D.wv[oc * 128:(oc + 1) * 128, :]
                            ).then_inc(vlsem[oc % 2].h, 16)
                vlsem[oc % 2].v += 16
            if CC > 1:
                vload(1)
            for oc in range(CC):
                s.wait_ge(P.sT.h, bT + oc + 1)
                pt = (S.psc if oc % 2 == 0 else S.psd)[:].bitcast(BF16)[:, 0:C]
                inst = None
                for j in range(CC):
                    inst = s.activation(wvt[:, j * C + oc * 128:j * C + (oc + 1) * 128],
                                        pt[:, j * 128:(j + 1) * 128], AF.Identity)
                inst.then_inc(P.sS.h, 1)
                P.sS.v += 1
                if oc % 2 == 1:
                    vload(oc + 2)
            for tc in range(TC):
                s.wait_ge(P.sT.h, bT + CC + tc + 1)
                pm = S.psa if tc % 2 == 0 else S.psb
                out = S.v[:, tc * H * 65:(tc + 1) * H * 65].rearrange(
                    "p (h e) -> p h e", e=65)[:, :, 0:64]
                s.activation(out, pm[:, 0:C].rearrange("p (h d) -> p h d", d=64),
                             AF.Identity).then_inc(P.sS.h, 1)
                P.sS.v += 1


def _b4_norms(P):
    nc, S = P.nc, P.S
    TC, CC, C, T, H = P.TC, P.CC, P.C, P.T, P.H
    sq = lambda p: P.view(S.pt if p == 0 else S.aot, 0, 128, T, BF16)
    for src, acc in ((S.q, S.nq2), (S.k, S.nk2)):
        bV, bT, bS = P.sV.v, P.sT.v, P.sS.v
        nfill = (T + 511) // 512
        s_sq = lambda oc: bS + oc + 1
        v_red = lambda oc, fi: bV + oc * nfill + fi + 1
        t_fill = lambda oc, fi: bT + oc * nfill + fi + 1
        with nc.Block() as b:
            @b.scalar
            def _(s):
                for oc in range(CC):
                    if oc >= 2:
                        s.wait_ge(P.sT.h, t_fill(oc - 2, nfill - 1))
                    s.activation(sq(oc % 2), src[:, oc * T:(oc + 1) * T],
                                 AF.Square)
                    s.drain().then_inc(P.sS.h, 1)
                    P.sS.v += 1

            @b.tensor
            def _(t):
                for oc in range(CC):
                    t.wait_ge(P.sS.h, s_sq(oc))
                    for fi, t0 in enumerate(range(0, T, 512)):
                        sw = min(512, T - t0)
                        g = oc * nfill + fi
                        pm = S.psc if g % 2 == 0 else S.psd
                        if g >= 2:
                            po, pf = divmod(g - 2, nfill)
                            t.wait_ge(P.sV.h, v_red(po, pf))
                        inst = t.matmul(pm[0:2, 0:sw], S.oblk[:],
                                        sq(oc % 2)[:, t0:t0 + sw],
                                        start=True, stop=True)
                        inst.then_inc(P.sT.h, 1)
                        P.sT.v += 1

            @b.vector
            def _(v):
                for oc in range(CC):
                    for fi in range(nfill):
                        v.wait_ge(P.sT.h, t_fill(oc, fi))
                        pm = S.psc if (oc * nfill + fi) % 2 == 0 else S.psd
                        v.reduce_max(acc[:, CC:CC + 1],
                                     pm[0:2, 0:min(512, T - fi * 512)], axis=AX)
                        v.drain()
                        v.tensor_max(acc[:, oc:oc + 1], acc[:, oc:oc + 1],
                                     acc[:, CC:CC + 1])
                        v.drain().then_inc(P.sV.h, 1)
                        P.sV.v += 1
                    if src is S.q:
                        fwx = P.view(S.ao, 0, 128, T, F32)
                        slx = _xn_ap(P)[:, oc * T:(oc + 1) * T]
                        v.tensor_scalar(fwx, slx, S.sml[:, R1:R1 + 1], RC,
                                        OP.mult, OP.add)
                        v.drain()
                        v.tensor_scalar(slx, fwx, RC, S.sml[:, S1:S1 + 1],
                                        OP.subtract, OP.mult)
                        v.drain()

    with nc.Block() as b:
        @b.vector
        def _(v):
            v.tensor_add(S.nm2[:], S.nq2[:, 0:CC], S.nk2[:, 0:CC])
            v.drain()
            v.tensor_scalar_mul(S.nm2[:], S.nm2[:], -0.5)
            v.drain().then_inc(P.sV.h, 1)
            P.sV.v += 1

        @b.scalar
        def _(s):
            s.wait_ge(P.sV.h, P.sV.v)
            s.dma_start(out=P.D.s_nm.rearrange("b a -> a b"),
                        in_=S.nm2[:],
                        allow_slow_non_contiguous=True).then_inc(P.sU.h, 16)
            P.sU.v += 16
            s.wait_ge(P.sU.h, P.sU.v)
            s.dma_start(out=S.negm[:, 0:H],
                        in_=P.D.s_nm.rearrange("b a -> (b a)").unsqueeze(0)[
                            0:1, 0:H].broadcast_to([128, H])).then_inc(P.sU.h, 16)
            P.sU.v += 16
            s.wait_ge(P.sU.h, P.sU.v)


def _b6_upgate(P):
    nc, S, D, view = P.nc, P.S, P.D, P.view
    CC, C, T, IC = P.CC, P.C, P.T, P.IC
    nf = max(1, T // 1024)
    wld = lambda p: view(S.ao, p * 4 * C, 128, C, F32)
    wtmp = view(S.ao, 8 * C, 128, C, F32)
    wtmp2 = view(S.ao, 12 * C, 128, C, F32)
    wqc = lambda p: view(S.ao, 16 * C + p * 2 * C, 128, C, BF16)
    wt = lambda p: view(S.ao, 20 * C + p * 2 * C, 128, C, BF16)
    cbuf = lambda p: view(S.ao, 24 * C + p * 2 * T, 128, T, BF16)

    for wdram, spill, mcol in ((D.wg, D.s_gate, GT_MAX), (D.wu, D.s_up, UP_MAX)):
        bV, bS, bT = P.sV.v, P.sS.v, P.sT.v
        bLp = [P.sLa.v, P.sLb.v]
        bGp = [P.sGa.v, P.sGb.v]
        v_quant = lambda oc: bV + 2 * oc + 1
        v_red = lambda oc: bV + 2 * oc + 2
        t_transp = lambda oc: bT + oc * (1 + nf) + 1
        t_fill = lambda oc, fi: bT + oc * (1 + nf) + 1 + fi + 1
        s_wt = lambda oc: bS + oc * (1 + nf) + 1
        s_fill = lambda oc, fi: bS + oc * (1 + nf) + 1 + fi + 1
        g_spill = lambda oc: (P.sGp[oc % 2].h, bGp[oc % 2] + 16 * (oc // 2 + 1))

        with nc.Block() as b:
            @b.sync
            def _(s):
                for oc in range(IC):
                    if oc >= 2:
                        s.wait_ge(P.sV.h, v_quant(oc - 2))
                    s.dma_start(out=wld(oc % 2),
                                in_=wdram[oc * 128:(oc + 1) * 128, :]).then_inc(P.sLp[oc % 2].h, 16)
                    P.sLp[oc % 2].v += 16

            @b.vector
            def _(v):
                for oc in range(IC):
                    v.wait_ge(P.sLp[oc % 2].h, bLp[oc % 2] + 16 * (oc // 2 + 1))
                    if oc >= 2:
                        v.wait_ge(P.sT.h, t_transp(oc - 2))
                    _quant_blk(P, v, wld(oc % 2), wtmp, wtmp2, wqc(oc % 2), C)
                    v.drain().then_inc(P.sV.h, 1)
                    P.sV.v += 1
                    v.wait_ge(P.sS.h, s_fill(oc, nf - 1))
                    v.reduce_max(S.sml[:, TMP0:TMP0 + 1], cbuf(oc % 2), axis=AX,
                                 apply_absolute_value=True)
                    v.drain()
                    v.tensor_max(S.sml[:, mcol:mcol + 1], S.sml[:, mcol:mcol + 1],
                                 S.sml[:, TMP0:TMP0 + 1])
                    v.drain().then_inc(P.sV.h, 1)
                    P.sV.v += 1

            @b.tensor
            def _(t):
                for oc in range(IC):
                    t.wait_ge(P.sV.h, v_quant(oc))
                    if oc >= 2:
                        t.wait_ge(P.sS.h, s_wt(oc - 2))
                    ps = (S.psc if oc % 2 == 0 else S.psd)[:].bitcast(BF16)
                    for ic in range(CC):
                        inst = t.transpose(ps[:, ic * 128:(ic + 1) * 128],
                                    wqc(oc % 2)[:, ic * 128:(ic + 1) * 128],
                                    S.ident_b[:])
                    inst.then_inc(P.sT.h, 1)
                    P.sT.v += 1
                    t.wait_ge(P.sS.h, s_wt(oc))
                    for fi, t0 in enumerate(range(0, T, 1024)):
                        fw = min(1024, T - t0)
                        pm = S.psa if (oc * nf + fi) % 2 == 0 else S.psb
                        g = oc * nf + fi
                        if g >= 2:
                            po, pf = divmod(g - 2, nf)
                            t.wait_ge(P.sS.h, s_fill(po, pf))
                        for sl in range(0, fw, 512):
                            sw = min(512, fw - sl)
                            for ic in range(CC):
                                inst = t.matmul(pm[:, sl:sl + sw],
                                         wt(oc % 2)[:, ic * 128:(ic + 1) * 128],
                                         _xn_ap(P)[:, ic * T + t0 + sl:ic * T + t0 + sl + sw],
                                         start=(ic == 0), stop=(ic == CC - 1))
                        inst.then_inc(P.sT.h, 1)
                        P.sT.v += 1

            @b.scalar
            def _(s):
                for oc in range(IC):
                    s.wait_ge(P.sT.h, t_transp(oc))
                    pt = (S.psc if oc % 2 == 0 else S.psd)[:].bitcast(BF16)[:, 0:C]
                    s.activation(wt(oc % 2), pt, AF.Identity).then_inc(P.sS.h, 1)
                    P.sS.v += 1
                    if oc >= 2:
                        s.wait_ge(*g_spill(oc - 2))
                        s.wait_ge(P.sV.h, v_red(oc - 2))
                    for fi, t0 in enumerate(range(0, T, 1024)):
                        fw = min(1024, T - t0)
                        pm = S.psa if (oc * nf + fi) % 2 == 0 else S.psb
                        s.wait_ge(P.sT.h, t_fill(oc, fi))
                        s.activation(cbuf(oc % 2)[:, t0:t0 + fw], pm[:, 0:fw],
                                     AF.Identity).then_inc(P.sS.h, 1)
                        P.sS.v += 1

            @b.gpsimd
            def _(g):
                for oc in range(IC):
                    if oc >= 2:
                        g.wait_ge(*g_spill(oc - 2))
                    g.wait_ge(P.sS.h, s_fill(oc, nf - 1))
                    g.dma_start(out=spill[oc:oc + 1, :, :].squeeze(0), in_=cbuf(oc % 2)
                                ).then_inc(P.sGp[oc % 2].h, 16)
                    P.sGp[oc % 2].v += 16
                g.wait_ge(P.sGa.h, P.sGa.v)
                g.wait_ge(P.sGb.h, P.sGb.v)


def _b6b_wprep(P):
    nc, S, D, view = P.nc, P.S, P.D, P.view
    CC, C, I, IC = P.CC, P.C, P.I, P.IC
    wld = lambda p: view(S.ao, p * 4 * C, 128, C, F32)
    wtmp = view(S.ao, 8 * C, 128, C, F32)
    wqc = lambda p: view(S.ao, 12 * C + p * 2 * C, 128, C, BF16)
    wt = lambda p: view(S.ao, 16 * C + p * 2 * C, 128, C, BF16)
    bV, bS, bT = P.sV.v, P.sS.v, P.sT.v
    bLp = [P.sLa.v, P.sLb.v]
    bGp = [P.sGa.v, P.sGb.v]
    g_sp = lambda oc: (P.sGp[oc % 2].h, bGp[oc % 2] + 16 * (oc // 2 + 1))
    with nc.Block() as b:
        @b.sync
        def _(s):
            for oc in range(CC):
                if oc >= 2:
                    s.wait_ge(P.sV.h, bV + oc - 1)
                s.dma_start(out=wld(oc % 2),
                            in_=D.wp[oc * 128:(oc + 1) * 128, :]).then_inc(P.sLp[oc % 2].h, 16)
                P.sLp[oc % 2].v += 16

        @b.vector
        def _(v):
            for oc in range(CC):
                v.wait_ge(P.sLp[oc % 2].h, bLp[oc % 2] + 16 * (oc // 2 + 1))
                if oc >= 2:
                    v.wait_ge(P.sT.h, bT + oc - 1)
                _quant_row(P, v, wld(oc % 2), wtmp, wqc(oc % 2), None)
                v.drain().then_inc(P.sV.h, 1)
                P.sV.v += 1

        @b.tensor
        def _(t):
            for oc in range(CC):
                t.wait_ge(P.sV.h, bV + oc + 1)
                if oc >= 2:
                    t.wait_ge(P.sS.h, bS + oc - 1)
                ps = (S.psc if oc % 2 == 0 else S.psd)[:].bitcast(BF16)
                for ic in range(CC):
                    inst = t.transpose(ps[:, ic * 128:(ic + 1) * 128],
                                wqc(oc % 2)[:, ic * 128:(ic + 1) * 128],
                                S.ident_b[:])
                inst.then_inc(P.sT.h, 1)
                P.sT.v += 1

        @b.scalar
        def _(s):
            for oc in range(CC):
                s.wait_ge(P.sT.h, bT + oc + 1)
                if oc >= 2:
                    s.wait_ge(*g_sp(oc - 2))
                pt = (S.psc if oc % 2 == 0 else S.psd)[:].bitcast(BF16)[:, 0:C]
                s.activation(wt(oc % 2), pt, AF.Identity).then_inc(P.sS.h, 1)
                P.sS.v += 1

        @b.gpsimd
        def _(g):
            for oc in range(CC):
                if oc >= 2:
                    g.wait_ge(*g_sp(oc - 2))
                g.wait_ge(P.sS.h, bS + oc + 1)
                g.dma_start(out=D.s_wpt[:, :, oc * 128:(oc + 1) * 128].transpose([1, 0, 2]),
                            in_=wt(oc % 2).rearrange("p (i m) -> p i m", m=128)
                            ).then_inc(P.sGp[oc % 2].h, 16)
                P.sGp[oc % 2].v += 16
            g.wait_ge(P.sGa.h, P.sGa.v)
            g.wait_ge(P.sGb.h, P.sGb.v)

    IH = I // 2
    wld3 = view(S.ao, 0, 128, IH, F32)
    wtmp3 = view(S.ao, 2 * I, 128, IH, F32)
    wtmp3b = view(S.ao, 4 * I, 128, IH, F32)
    wqc3 = view(S.ao, 6 * I, 128, IH, BF16)
    tstage = view(S.ao, 7 * I, 128, IC * 128, BF16)
    bL, bV, bS, bT, bG = P.sL.v, P.sV.v, P.sS.v, P.sT.v, P.sG.v
    nh = IH // 128
    with nc.Block() as b:
        @b.sync
        def _(s):
            for oc in range(CC):
                for hv in range(2):
                    g = oc * 2 + hv
                    if g >= 1:
                        s.wait_ge(P.sV.h, bV + g)
                    s.dma_start(out=wld3,
                                in_=D.wd[oc * 128:(oc + 1) * 128, hv * IH:(hv + 1) * IH]
                                ).then_inc(P.sL.h, 16)
            P.sL.v += 32 * CC

        @b.vector
        def _(v):
            for oc in range(CC):
                for hv in range(2):
                    g = oc * 2 + hv
                    v.wait_ge(P.sL.h, bL + 16 * (g + 1))
                    if g >= 1:
                        v.wait_ge(P.sT.h, bT + g)
                    _quant_blk(P, v, wld3, wtmp3, wtmp3b, wqc3, IH)
                    v.drain().then_inc(P.sV.h, 1)
                    P.sV.v += 1

        @b.tensor
        def _(t):
            for oc in range(CC):
                for hv in range(2):
                    g = oc * 2 + hv
                    t.wait_ge(P.sV.h, bV + g + 1)
                    if g >= 2:
                        t.wait_ge(P.sS.h, bS + g - 1)
                    ps = (S.psc if g % 2 == 0 else S.psd)[:].bitcast(BF16)
                    for j in range(nh):
                        inst = t.transpose(ps[:, j * 128:(j + 1) * 128],
                                    wqc3[:, j * 128:(j + 1) * 128],
                                    S.ident_b[:])
                    inst.then_inc(P.sT.h, 1)
                    P.sT.v += 1

        @b.scalar
        def _(s):
            for oc in range(CC):
                for hv in range(2):
                    g = oc * 2 + hv
                    s.wait_ge(P.sT.h, bT + g + 1)
                    if g >= 2:
                        s.wait_ge(P.sG.h, bG + 16 * oc)
                    ps = (S.psc if g % 2 == 0 else S.psd)[:].bitcast(BF16)[:, 0:nh * 128]
                    s.activation(tstage[:, hv * IH:hv * IH + nh * 128], ps,
                                 AF.Identity).then_inc(P.sS.h, 1)
                    P.sS.v += 1

        @b.gpsimd
        def _(g):
            for oc in range(CC):
                if oc >= 1:
                    g.wait_ge(P.sG.h, bG + 16 * oc)
                g.wait_ge(P.sS.h, bS + 2 * (oc + 1))
                out = D.s_wdt[:, :, oc:oc + 1, :].squeeze(2).transpose([1, 0, 2])
                g.dma_start(out=out, in_=tstage.rearrange("p (i j) -> p i j", j=128),
                            allow_slow_non_contiguous=True).then_inc(P.sG.h, 16)
                P.sG.v += 16
            g.wait_ge(P.sG.h, P.sG.v)

def _b7_attention(P):
    nc, S = P.nc, P.S
    TC, CC, C, T, H = P.TC, P.CC, P.C, P.T, P.H
    th = T // 2
    tph = th // 128

    ev_t, ev_s, ev_v = [], [], []
    for h in range(H):
        for f in range(2):
            for j in range(TC):
                ev_t.append(("sc", h, f, j))
                if j >= 1:
                    ev_t.append(("av", h, f, j - 1))
            ev_t.append(("av", h, f, TC - 1))
            ev_s.extend(("exp", h, f, j) for j in range(TC))
            ev_s.append(("xav", h, f))
        for tau in range(TC):
            ev_t.append(("t1", h, tau))
        for bi in range((TC + 7) // 8):
            ev_v.append(("dvb", h, bi))
        if h % 2 == 1:
            ev_t.append(("t2", h))
            ev_v.append(("aoev", h))
    bT, bS, bV = P.sT.v, P.sS.v, P.sV.v
    tick_t = {e: bT + i + 1 for i, e in enumerate(ev_t)}
    tick_s = {e: bS + i + 1 for i, e in enumerate(ev_s)}
    tick_v = {e: bV + i + 1 for i, e in enumerate(ev_v)}

    def prev_hf(h, f):
        if f == 1:
            return (h, 0)
        return (h - 1, 1) if h > 0 else None

    with nc.Block() as b:
        @b.tensor
        def _(t):
            for h in range(H):
                pc, hp = h // 2, h % 2
                for f in range(2):
                    prev = prev_hf(h, f)

                    def scores(j):
                        pm = S.psa if j % 2 == 0 else S.psb
                        if j >= 2:
                            t.wait_ge(P.sS.h, tick_s[("exp", h, f, j - 2)])
                        elif prev is not None:
                            t.wait_ge(P.sS.h, tick_s[("exp", prev[0], prev[1], TC - 1)])
                        inst = None
                        for sl in range(0, th, 512):
                            sw = min(512, th - sl)
                            inst = t.matmul(pm[:, sl:sl + sw],
                                     S.k[hp * 64:hp * 64 + 64,
                                         pc * T + j * 128:pc * T + (j + 1) * 128],
                                     S.q[hp * 64:hp * 64 + 64,
                                         pc * T + f * th + sl:pc * T + f * th + sl + sw],
                                     start=True, stop=True,
                                     tile_position=(hp * 64, 0))
                        inst.then_inc(P.sT.h, 1)
                        P.sT.v += 1

                    def av(j):
                        t.wait_ge(P.sS.h, tick_s[("exp", h, f, j)])
                        if j == 0 and prev is not None:
                            t.wait_ge(P.sV.h, tick_v[("xav", prev[0], prev[1])])
                        vj = S.v[:, j * H * 65 + h * 65:j * H * 65 + h * 65 + 65]
                        ptb = S.pt[:, (j % 2) * 1024:(j % 2) * 1024 + th]
                        inst = None
                        for sl in range(0, th, 512):
                            sw = min(512, th - sl)
                            inst = t.matmul(S.psd[0:65, sl:sl + sw], vj, ptb[:, sl:sl + sw],
                                     start=(j == 0), stop=(j == TC - 1))
                        inst.then_inc(P.sT.h, 1)
                        P.sT.v += 1

                    for j in range(TC):
                        scores(j)
                        if j >= 1:
                            av(j - 1)
                    av(TC - 1)
                for tau in range(TC):
                    f2, loc = divmod(tau, tph)
                    t.wait_ge(P.sV.h, tick_v[("xav", h, f2)])
                    if tau >= 2:
                        t.wait_ge(P.sV.h, tick_v[("dv", h, tau - 2)])
                    elif h > 0:
                        if h % 2 == 1:
                            t.wait_ge(P.sV.h, tick_v[("dv", h - 1, TC - 1)])
                        else:
                            t.wait_ge(P.sV.h, tick_v[("aoev", h - 1)])
                    inst = t.transpose(S.psc[:, (tau % 2) * 512:(tau % 2) * 512 + 65],
                                S.sav[0:65, f2 * 1024 + loc * 128:f2 * 1024 + (loc + 1) * 128],
                                S.ident_f[0:65, 0:65])
                    inst.then_inc(P.sT.h, 1)
                    P.sT.v += 1
                if h % 2 == 1:
                    t.wait_ge(P.sV.h, tick_v[("dv", h, TC - 1)])
                    psb2 = S.psc[:].bitcast(BF16)
                    inst = None
                    for tau in range(TC):
                        inst = t.transpose(psb2[:, tau * 128:(tau + 1) * 128],
                                    S.aot[:, tau * 128:(tau + 1) * 128],
                                    S.ident_b[:])
                    inst.then_inc(P.sT.h, 1)
                    P.sT.v += 1

        @b.scalar
        def _(s):
            for h in range(H):
                for f in range(2):
                    for j in range(TC):
                        pm = S.psa if j % 2 == 0 else S.psb
                        s.wait_ge(P.sT.h, tick_t[("sc", h, f, j)])
                        if j >= 2:
                            s.wait_ge(P.sT.h, tick_t[("av", h, f, j - 2)])
                        ptb = S.pt[:, (j % 2) * 1024:(j % 2) * 1024 + th]
                        s.activation(ptb, pm[:, 0:th], AF.Exp,
                                     bias=S.negm[:, h:h + 1]).then_inc(P.sS.h, 1)
                        P.sS.v += 1
                    s.wait_ge(P.sT.h, tick_t[("av", h, f, TC - 1)])
                    s.activation(S.sav[0:65, f * 1024:f * 1024 + th],
                                 S.psd[0:65, 0:th], AF.Identity)
                    s.drain().then_inc(P.sS.h, 1)
                    P.sS.v += 1

        @b.vector
        def _(v):
            for h in range(H):
                hp = h % 2
                for f in range(2):
                    v.wait_ge(P.sT.h, tick_t[("av", h, f, TC - 1)])
                    v.tensor_copy(S.sav[0:65, f * 1024:f * 1024 + th],
                                  S.psd[0:65, 0:th])
                    v.drain().then_inc(P.sV.h, 1)
                    P.sV.v += 1
                for tau in range(TC):
                    v.wait_ge(P.sT.h, tick_t[("t1", h, tau)])
                    slot = S.psc[:, (tau % 2) * 512:(tau % 2) * 512 + 65]
                    v.reciprocal(S.sig[:, tau:tau + 1], slot[:, 64:65])
                    v.drain()
                    v.tensor_scalar(
                        S.aot[:, tau * 128 + hp * 64:tau * 128 + hp * 64 + 64],
                        slot[:, 0:64], S.sig[:, tau:tau + 1], None, OP.mult)
                    v.drain().then_inc(P.sV.h, 1)
                    P.sV.v += 1
                if h % 2 == 1:
                    v.wait_ge(P.sT.h, tick_t[("t2", h)])
                    psb2 = S.psc[:].bitcast(BF16)
                    out = S.ao[:, (h // 2) * T:(h // 2) * T + TC * 128]
                    for c0 in range(0, TC * 128, 1024):
                        cw = min(1024, TC * 128 - c0)
                        v.tensor_copy(out[:, c0:c0 + cw], psb2[:, c0:c0 + cw])
                    v.drain().then_inc(P.sV.h, 1)
                    P.sV.v += 1


def _ffo_ap(P, tc):
    half = P.TC // 2
    if tc < half:
        return P.view(P.S.q, tc * P.C * 4, 128, P.C, F32)
    return P.view(P.S.k, (tc - half) * P.C * 4, 128, P.C, F32)


def _b8_chain1(P):
    nc, S, D, view = P.nc, P.S, P.D, P.view
    T, IC, TC, CC, C = P.T, P.IC, P.TC, P.CC, P.C
    HT = T // 2
    ar_bytes = S.arena.shape[1] * 2
    pt_bytes = S.pt.shape[1] * 2
    aot_bytes = S.aot.shape[1] * 2
    FD = 4 if (6 * T <= ar_bytes and 2 * T <= pt_bytes
               and 2 * T <= aot_bytes) else 2
    ID1 = FD
    if FD == 4:
        in1 = lambda p: view(S.v, p * 2 * T, 128, T, BF16)
        fw = lambda q: view(S.v, 8 * T + q * 2 * T, 128, HT, F32)
        fw2 = lambda q: (view(S.arena, q * 2 * T, 128, HT, F32) if q < 3 else
                         view(S.pt, 0, 128, HT, F32))
        fw3 = view(S.aot, 0, 128, HT, F32)
    else:
        in1 = lambda p: view(S.v, p * 2 * T, 128, T, BF16)
        fw = lambda q: view(S.v, 4 * T + q * 2 * T, 128, HT, F32)
        fw2 = lambda q: view(S.v, 8 * T + q * 2 * T, 128, HT, F32)
        fw3 = view(S.v, 12 * T, 128, HT, F32)
    wpt = view(S.xn, 0, 128, CC * C, BF16)
    NG = 2 * IC
    bV, bS, bT, bL = P.sV.v, P.sS.v, P.sT.v, P.sL.v
    lsem = [P.sLa, P.sLb, P.sW0, P.sW1][:ID1]
    bLp = [sm.v for sm in lsem]
    in1_ld = lambda ic: (lsem[ic % ID1].h, bLp[ic % ID1] + 16 * (ic // ID1 + 1))
    bGp = [P.sGa.v, P.sGb.v]
    g_sp = lambda ic: (P.sGp[ic % 2].h, bGp[ic % 2] + 16 * (ic // 2 + 1))

    v_tl = lambda g: bV + g + 1
    sev = []
    ppq = list(range(TC)) if not DBG["no_pproj"] else []
    for g in range(NG):
        sev += [("p1", g), ("p2", g), ("sig", g)]
        if g % 3 == 2 and ppq:
            sev.append(("ppe", ppq.pop(0)))
    while ppq:
        sev.append(("ppe", ppq.pop(0)))
    tick_sev = {e: bS + i + 1 for i, e in enumerate(sev)}

    with nc.Block() as b:
        @b.sync
        def _(s):
            def gld(ic):
                if ic >= ID1:
                    s.wait_ge(*g_sp(ic - ID1))
                s.dma_start(out=in1(ic % ID1),
                            in_=D.s_gate[ic:ic + 1, :, :].squeeze(0)
                            ).then_inc(lsem[ic % ID1].h, 16)
                lsem[ic % ID1].v += 16
            for ic in range(min(ID1, IC)):
                gld(ic)
            if not DBG["no_pproj"]:
                for oc in range(CC):
                    s.dma_start(out=wpt[:, oc * C:(oc + 1) * C],
                                in_=D.s_wpt[oc:oc + 1, :, :].squeeze(0)
                                ).then_inc(P.sL.h, 16)
                P.sL.v += 16 * CC
            for ic in range(ID1, IC):
                gld(ic)

        @b.tensor
        def _(t):
            if DBG["no_pproj"]:
                return
            t.wait_ge(P.sL.h, bL + 16 * CC)
            for tc in range(TC):
                pm = S.psa if tc % 2 == 0 else S.psb
                if tc >= 2:
                    t.wait_ge(P.sS.h, tick_sev[("ppe", tc - 2)])
                inst = None
                for sl in range(0, C, 512):
                    sw = min(512, C - sl)
                    for pc in range(CC):
                        inst = t.matmul(pm[:, sl:sl + sw],
                                        S.ao[:, pc * T + tc * 128:pc * T + (tc + 1) * 128],
                                        wpt[:, pc * C + sl:pc * C + sl + sw],
                                        start=(pc == 0), stop=False)
                    inst = t.matmul(pm[:, sl:sl + sw], S.ones1[:],
                                    S.bpr[0:1, sl:sl + sw], start=False, stop=True)
                inst.then_inc(P.sT.h, 1)
                P.sT.v += 1

        @b.vector
        def _(v):
            for g in range(NG):
                ic, hv = divmod(g, 2)
                v.wait_ge(P.sS.h, tick_sev[("sig", g)])
                v.tensor_scalar(fw3, fw(g % FD), 65536.0, RC, OP.mult, OP.add)
                v.drain()
                sl = in1(ic % ID1)[:, hv * HT:(hv + 1) * HT]
                v.scalar_tensor_tensor(sl, fw3, RC, fw2(g % FD), OP.subtract, OP.mult)
                v.drain()
                v.reduce_max(S.sml[:, TMP0:TMP0 + 1], sl, axis=AX,
                             apply_absolute_value=True)
                v.drain()
                v.tensor_max(S.sml[:, PR_MAX:PR_MAX + 1], S.sml[:, PR_MAX:PR_MAX + 1],
                             S.sml[:, TMP0:TMP0 + 1])
                v.drain().then_inc(P.sV.h, 1)
                P.sV.v += 1
            v.tensor_scalar_mul(S.sml[:, PR_TRUE:PR_TRUE + 1],
                                S.sml[:, PR_MAX:PR_MAX + 1], INV16)
            v.drain().then_inc(P.sV.h, 1)
            P.sV.v += 1
            if not DBG["standalone_a4"]:
                v.wait_ge(P.sX.h, P.sX.v + 16)
                v.reduce_max(S.row[0:1, 0:1], S.row[0:1, 128:256], axis=AX)
                v.drain().then_inc(P.sV.h, 1)
                P.sV.v += 1

        @b.scalar
        def _(s):
            for e in sev:
                if e[0] == "p1":
                    g = e[1]
                    ic, hv = divmod(g, 2)
                    if hv == 0:
                        s.wait_ge(*in1_ld(ic))
                    if g >= FD:
                        s.wait_ge(P.sV.h, v_tl(g - FD))
                    sl = in1(ic % ID1)[:, hv * HT:(hv + 1) * HT]
                    s.activation(fw(g % FD), sl, AF.Identity,
                                 bias=S.c_rc[:], scale=S.sml[:, R3:R3 + 1])
                    s.drain().then_inc(P.sS.h, 1)
                    P.sS.v += 1
                elif e[0] == "p2":
                    g = e[1]
                    s.activation(fw2(g % FD), fw(g % FD), AF.Identity,
                                 bias=S.sml[:, NRC3:NRC3 + 1],
                                 scale=S.sml[:, S3:S3 + 1])
                    s.drain().then_inc(P.sS.h, 1)
                    P.sS.v += 1
                elif e[0] == "sig":
                    g = e[1]
                    s.activation(fw(g % FD), fw2(g % FD), AF.Sigmoid)
                    s.drain().then_inc(P.sS.h, 1)
                    P.sS.v += 1
                else:
                    tc = e[1]
                    pm = S.psa if tc % 2 == 0 else S.psb
                    s.wait_ge(P.sT.h, bT + tc + 1)
                    s.activation(_ffo_ap(P, tc), pm[:, 0:C],
                                 AF.Identity).then_inc(P.sS.h, 1)
                    P.sS.v += 1

        @b.gpsimd
        def _(g):
            for ic in range(IC):
                if ic >= 2:
                    g.wait_ge(*g_sp(ic - 2))
                g.wait_ge(P.sV.h, v_tl(2 * ic + 1))
                g.dma_start(out=D.s_prod[ic:ic + 1, :, :].squeeze(0),
                            in_=in1(ic % ID1)).then_inc(P.sGp[ic % 2].h, 16)
                P.sGp[ic % 2].v += 16
            if DBG["standalone_a4"]:
                g.wait_ge(P.sGa.h, P.sGa.v)
                g.wait_ge(P.sGb.h, P.sGb.v)
                return
            g.wait_ge(P.sV.h, bV + NG + 1)
            g.dma_start(out=S.row[0:1, 128:256],
                        in_=S.sml[:, PR_TRUE:PR_TRUE + 1]).then_inc(P.sX.h, 16)
            P.sX.v += 16
            g.wait_ge(P.sV.h, bV + NG + 2)
            cin, cout = D.cc[2]
            g.dma_start(out=cin[:], in_=S.row[0:1, 0:128]).then_inc(P.sX.h, 16)
            P.sX.v += 16
            g.wait_ge(P.sX.h, P.sX.v)
            g.collective_compute(
                "AllReduce", OP.max, replica_groups=[list(range(P.NCORES))],
                ins=[cin[:]], outs=[cout[:]]).then_inc(P.sC.h, 1)
            P.sC.v += 1
            g.wait_ge(P.sC.h, P.sC.v)
            g.dma_start(out=S.sml[:, PR_TRUE:PR_TRUE + 1],
                        in_=cout[0:1, 0:1].broadcast_to([128, 1])).then_inc(P.sX.h, 16)
            P.sX.v += 16
            g.wait_ge(P.sX.h, P.sX.v)
            g.wait_ge(P.sGa.h, P.sGa.v)
            g.wait_ge(P.sGb.h, P.sGb.v)


def _wdtr_ap(P, ic):
    C = P.C
    if ic < P.wdt_nao:
        return P.view(P.S.ao, ic * 2 * C, 128, C, BF16)
    return P.view(P.S.xn, (ic - P.wdt_nao) * 2 * C, 128, C, BF16)


def _b9_chain2(P):
    nc, S, D, view = P.nc, P.S, P.D, P.view
    T, IC = P.T, P.IC
    C = P.C
    HT = T // 2
    NG = 2 * IC
    xn_bytes = S.xn.shape[1] * 2
    ar_bytes = S.arena.shape[1] * 2
    pt_bytes = S.pt.shape[1] * 2
    DEEP = (2 * P.CC * C + 8 * T <= xn_bytes and 6 * T <= ar_bytes
            and 2 * T <= pt_bytes)
    FD = 4 if DEEP else 2
    IN2D = 4 if DEEP else 2
    ID1 = 4 if DEEP else 2
    if DEEP:
        in1 = lambda p: view(S.v, p * 2 * T, 128, T, BF16)
        in2 = lambda p: view(S.xn, 2 * P.CC * C + p * 2 * T, 128, T, BF16)
        fwA = lambda q: view(S.v, 8 * T + q * 2 * T, 128, HT, F32)
        fw2 = lambda q: (view(S.arena, q * 2 * T, 128, HT, F32) if q < 3 else
                         view(S.pt, 0, 128, HT, F32))
    else:
        in1 = lambda p: view(S.v, p * 2 * T, 128, T, BF16)
        in2 = lambda p: view(S.v, 4 * T + p * 2 * T, 128, T, BF16)
        fwA = lambda q: view(S.v, 8 * T + q * 2 * T, 128, HT, F32)
        fw2 = lambda q: view(S.v, 12 * T + q * 2 * T, 128, HT, F32)
    bV, bS = P.sV.v, P.sS.v
    lsem = [P.sLa, P.sLb, P.sL, P.sU][:ID1]
    bLp = [sm.v for sm in lsem]
    in1_ld = lambda ic: (lsem[ic % ID1].h, bLp[ic % ID1] + 16 * (ic // ID1 + 1))
    bGp = [P.sGa.v, P.sGb.v]
    v_tl = lambda g: bV + g + 1
    g_sp = lambda ic: (P.sGp[ic % 2].h, bGp[ic % 2] + 32 * (ic // 2 + 1))
    ao_el = S.ao.shape[1]
    P.wdt_nao = min(IC, ao_el // C)
    assert (IC - P.wdt_nao) * 2 * C <= 2 * P.CC * C, (IC, P.wdt_nao)
    P.wdt_ld = g_sp
    wsem2 = [P.sW0, P.sW1, P.sW2, P.sW3][:IN2D]
    bW2 = [sm.v for sm in wsem2]
    in2_ld = lambda ic: (wsem2[ic % IN2D].h, bW2[ic % IN2D] + 16 * (ic // IN2D + 1))

    npre = min(IN2D - 1, IC) if IN2D > 2 else min(2, IC)
    sev = []
    for g in range(NG):
        if g == 0:
            for ic0 in range(npre):
                sev.append(("in2", ic0))
        sev += [("p1", g), ("p2", g)]
        ic_n = g // 2 + npre
        if g % 2 == 1 and ic_n < IC:
            sev.append(("in2", ic_n))
    tick_sev = {}
    tick = bS
    for e in sev:
        if e[0] != "in2":
            tick += 1
            tick_sev[e] = tick

    with nc.Block() as b:
        @b.vector
        def _(v):
            v.tensor_tensor(S.sml[:, 18:19], S.sml[:, S4:S4 + 1],
                            S.sml[:, S2:S2 + 1], OP.mult)
            v.drain()
            v.tensor_scalar_mul(S.sml[:, NRC42:NRC42 + 1], S.sml[:, 18:19], -RC)
            v.drain()

    with nc.Block() as b:
        @b.sync
        def _(s):
            for ic in range(IC):
                if ic >= ID1:
                    s.wait_ge(*g_sp(ic - ID1))
                s.dma_start(out=in1(ic % ID1),
                            in_=D.s_prod[ic:ic + 1, :, :].squeeze(0)
                            ).then_inc(lsem[ic % ID1].h, 16)
                lsem[ic % ID1].v += 16

        @b.scalar
        def _(s):
            for e in sev:
                if e[0] == "p1":
                    g = e[1]
                    ic, hv = divmod(g, 2)
                    if hv == 0:
                        s.wait_ge(*in1_ld(ic))
                    if g >= FD:
                        s.wait_ge(P.sV.h, v_tl(g - FD))
                    sl = in1(ic % ID1)[:, hv * HT:(hv + 1) * HT]
                    s.activation(fwA(g % FD), sl, AF.Identity,
                                 bias=S.c_rc[:], scale=S.sml[:, R4S:R4S + 1])
                    s.drain().then_inc(P.sS.h, 1)
                    P.sS.v += 1
                elif e[0] == "p2":
                    g = e[1]
                    s.activation(fw2(g % FD), fwA(g % FD), AF.Identity,
                                 bias=S.sml[:, NRC42:NRC42 + 1],
                                 scale=S.sml[:, 18:19])
                    s.drain().then_inc(P.sS.h, 1)
                    P.sS.v += 1
                else:
                    ic = e[1]
                    if ic >= IN2D:
                        s.wait_ge(P.sV.h, v_tl(2 * (ic - IN2D) + 1))
                    s.dma_start(out=in2(ic % IN2D),
                                in_=D.s_up[ic:ic + 1, :, :].squeeze(0)
                                ).then_inc(wsem2[ic % IN2D].h, 16)
                    wsem2[ic % IN2D].v += 16

        @b.vector
        def _(v):
            for g in range(NG):
                ic, hv = divmod(g, 2)
                v.wait_ge(P.sS.h, tick_sev[("p2", g)])
                if hv == 0:
                    v.wait_ge(*in2_ld(ic))
                sl2 = in2(ic % IN2D)[:, hv * HT:(hv + 1) * HT]
                v.tensor_scalar(fwA(g % FD), sl2, S.sml[:, R2:R2 + 1], RC,
                                OP.mult, OP.add)
                v.drain()
                sl1 = in1(ic % ID1)[:, hv * HT:(hv + 1) * HT]
                v.scalar_tensor_tensor(sl1, fwA(g % FD), RC, fw2(g % FD),
                                       OP.subtract, OP.mult)
                v.drain()
                v.reduce_max(S.sml[:, TMP0:TMP0 + 1], sl1, axis=AX,
                             apply_absolute_value=True)
                v.drain()
                v.tensor_max(S.sml[:, OR_MAX:OR_MAX + 1], S.sml[:, OR_MAX:OR_MAX + 1],
                             S.sml[:, TMP0:TMP0 + 1])
                v.drain().then_inc(P.sV.h, 1)
                P.sV.v += 1
            if not DBG["standalone_a5"]:
                v.wait_ge(P.sY.h, P.sY.v + 16)
                v.reduce_max(S.row[0:1, 0:1], S.row[0:1, 128:256], axis=AX)
                v.drain().then_inc(P.sV.h, 1)
                P.sV.v += 1

        @b.gpsimd
        def _(g):
            for ic in range(IC):
                if ic >= 2:
                    g.wait_ge(*g_sp(ic - 2))
                g.wait_ge(P.sV.h, v_tl(2 * ic + 1))
                g.dma_start(out=D.s_orw[ic:ic + 1, :, :].squeeze(0),
                            in_=in1(ic % ID1)).then_inc(P.sGp[ic % 2].h, 16)
                P.sGp[ic % 2].v += 16
                g.dma_start(out=_wdtr_ap(P, ic),
                            in_=D.s_wdt[ic:ic + 1, :, :, :].squeeze(0).rearrange(
                                "p a b -> p (a b)")).then_inc(P.sGp[ic % 2].h, 16)
                P.sGp[ic % 2].v += 16
            if DBG["standalone_a5"]:
                g.wait_ge(P.sGa.h, P.sGa.v)
                g.wait_ge(P.sGb.h, P.sGb.v)
                return
            g.dma_start(out=S.row[0:1, 128:256],
                        in_=S.sml[:, OR_MAX:OR_MAX + 1]).then_inc(P.sY.h, 16)
            P.sY.v += 16
            g.wait_ge(P.sV.h, bV + NG + 1)
            cin, cout = D.cc[3]
            g.dma_start(out=cin[:], in_=S.row[0:1, 0:128]).then_inc(P.sY.h, 16)
            P.sY.v += 16
            g.wait_ge(P.sY.h, P.sY.v)
            g.collective_compute(
                "AllReduce", OP.max, replica_groups=[list(range(P.NCORES))],
                ins=[cin[:]], outs=[cout[:]]).then_inc(P.sC.h, 1)
            P.sC.v += 1
            g.wait_ge(P.sC.h, P.sC.v)
            g.dma_start(out=S.sml[:, OR_MAX:OR_MAX + 1],
                        in_=cout[0:1, 0:1].broadcast_to([128, 1])).then_inc(P.sY.h, 16)
            P.sY.v += 16
            g.wait_ge(P.sY.h, P.sY.v)
            g.wait_ge(P.sGa.h, P.sGa.v)
            g.wait_ge(P.sGb.h, P.sGb.v)


def _b10_down(P):
    nc, S, D, view = P.nc, P.S, P.D, P.view
    TC, CC, C, T, IC = P.TC, P.CC, P.C, P.T, P.IC
    TGW = min(512, T)
    ntg = T // TGW
    ng = TGW // 128
    NT = ntg * ng
    psl = [S.psa, S.psb, S.psc, S.psd]
    NOIN = 4
    oq = view(S.v, 0, 128, IC * TGW, BF16)
    oin = lambda p: view(S.v, 2 * IC * TGW + p * 2 * TGW, 128, TGW, BF16)
    fw5 = view(S.v, 2 * IC * TGW + 2 * NOIN * TGW, 128, TGW, F32)
    def scr(p):
        if p == 0:
            return view(S.pt, 0, 128, C, F32)
        if p == 1:
            return view(S.aot, 0, 128, C, F32)
        return view(S.arena, (p - 2) * 4 * C, 128, C, F32)
    outb = lambda p: view(S.xn, 2 * CC * C + p * 4 * C, 128, C, F32)
    xt = lambda p: S.xf[:, p * 1024:p * 1024 + C]
    P.ffo_ap = _ffo_ap

    bV, bS, bT = P.sV.v, P.sS.v, P.sT.v
    osem = [P.sLa, P.sLb, P.sL, P.sU]
    bLp = [sm.v for sm in osem]
    bOp = [P.sOa.v, P.sOb.v]

    ev = []
    for tg in range(ntg):
        pend = list(range(ng)) if tg >= 1 else []
        for ic in range(IC):
            ev.append(("q", tg, ic))
            if pend and ic >= 2 and ic % 2 == 0:
                ev.append(("o", tg - 1, pend.pop(0)))
        for tc4 in pend:
            ev.append(("o", tg - 1, tc4))
    for tc4 in range(ng):
        ev.append(("o", ntg - 1, tc4))
    tick = {e: bV + i + 1 for i, e in enumerate(ev)}
    v_oq = lambda tg, ic: tick[("q", tg, ic)]
    v_out = lambda k: tick[("o", k // ng, k % ng)]
    t_grp = lambda tg, ic: bT + tg * IC + ic + 1
    s_ev = lambda tg, tc4: bS + tg * ng + tc4 + 1
    ld_grp = lambda g: (osem[g % NOIN].h, bLp[g % NOIN] + 16 * (g // NOIN + 1))
    x_ld = lambda k: (P.sOp[k % 2].h,
                      bOp[k % 2] + 16 + (0 if k < 2 else 32 * (k // 2)))

    with nc.Block() as b:
        @b.sync
        def _(s):
            for tg in range(ntg):
                for ic in range(IC):
                    g = tg * IC + ic
                    if g >= NOIN:
                        pg, pi = divmod(g - NOIN, IC)
                        s.wait_ge(P.sV.h, v_oq(pg, pi))
                        s.wait_ge(P.sT.h, t_grp(pg, pi))
                    s.dma_start(out=oin(g % NOIN),
                                in_=D.s_orw[ic:ic + 1, :, tg * TGW:(tg + 1) * TGW].squeeze(0)
                                ).then_inc(osem[g % NOIN].h, 16)
                    osem[g % NOIN].v += 16

        @b.vector
        def _(v):
            for e in ev:
                if e[0] == "q":
                    tg, ic = e[1], e[2]
                    g = tg * IC + ic
                    v.wait_ge(*ld_grp(g))
                    if tg > 0:
                        v.wait_ge(P.sT.h, t_grp(tg - 1, ic))
                    v.tensor_scalar(fw5, oin(g % NOIN), S.sml[:, R5:R5 + 1], RC,
                                    OP.mult, OP.add)
                    v.drain()
                    v.tensor_scalar(oq[:, ic * TGW:(ic + 1) * TGW], fw5, RC,
                                    S.sml[:, S5:S5 + 1], OP.subtract, OP.mult)
                    v.drain().then_inc(P.sV.h, 1)
                    P.sV.v += 1
                else:
                    tg, tc4 = e[1], e[2]
                    k = tg * ng + tc4
                    v.wait_ge(P.sS.h, s_ev(tg, tc4))
                    v.wait_ge(*x_ld(k))
                    v.tensor_add(outb(k % 2), scr(k % 4), _ffo_ap(P, k))
                    v.drain()
                    v.tensor_add(outb(k % 2), outb(k % 2), xt(k % 2))
                    v.drain().then_inc(P.sV.h, 1)
                    P.sV.v += 1

        @b.tensor
        def _(t):
            for tg in range(ntg):
                for ic in range(IC):
                    g = tg * IC + ic
                    t.wait_ge(P.sV.h, v_oq(tg, ic))
                    if tg == 0:
                        t.wait_ge(*P.wdt_ld(ic))
                    if ic == 0 and tg > 0:
                        t.wait_ge(P.sS.h, s_ev(tg - 1, ng - 1))
                    for tc4 in range(ng):
                        for sl in range(0, C, 512):
                            sw = min(512, C - sl)
                            inst = t.matmul(psl[tc4][:, sl:sl + sw],
                                     oq[:, ic * TGW + tc4 * 128:ic * TGW + (tc4 + 1) * 128],
                                     _wdtr_ap(P, ic)[:, sl:sl + sw],
                                     start=(ic == 0), stop=(ic == IC - 1))
                    inst.then_inc(P.sT.h, 1)
                    P.sT.v += 1

        @b.scalar
        def _(s):
            for tg in range(ntg):
                for tc4 in range(ng):
                    k = tg * ng + tc4
                    s.wait_ge(P.sT.h, t_grp(tg, IC - 1))
                    if k >= 4:
                        s.wait_ge(P.sV.h, v_out(k - 4))
                    s.activation(scr(k % 4), psl[tc4][:, 0:C],
                                 AF.Identity).then_inc(P.sS.h, 1)
                    P.sS.v += 1

        @b.gpsimd
        def _(g):
            for k in range(min(2, NT)):
                g.dma_start(out=xt(k % 2),
                            in_=D.x[k * 128:(k + 1) * 128, :]).then_inc(P.sOp[k % 2].h, 16)
                P.sOp[k % 2].v += 16
            for k in range(NT):
                g.wait_ge(P.sV.h, v_out(k))
                g.dma_start(out=D.y[k * 128:(k + 1) * 128, :],
                            in_=outb(k % 2)).then_inc(P.sOp[k % 2].h, 16)
                P.sOp[k % 2].v += 16
                if k + 2 < NT:
                    g.dma_start(out=xt(k % 2),
                                in_=D.x[(k + 2) * 128:(k + 3) * 128, :]
                                ).then_inc(P.sOp[k % 2].h, 16)
                    P.sOp[k % 2].v += 16


def _b12_taps(P):
    nc, S = P.nc, P.S
    with nc.Block() as b:
        @b.gpsimd
        def _(g):
            tp = P.tap_t
            if "xnfm" in tp:
                g.dma_start(out=tp["xnfm"], in_=_xn_ap(P)).then_inc(P.sO.h, 16)
                P.sO.v += 16
            if "q" in tp:
                g.dma_start(out=tp["q"], in_=S.q[:, 0:P.CC * P.T]).then_inc(P.sO.h, 16)
                P.sO.v += 16
            if "k" in tp:
                g.dma_start(out=tp["k"], in_=S.k[:, 0:P.CC * P.T]).then_inc(P.sO.h, 16)
                P.sO.v += 16
            if "v" in tp:
                g.dma_start(out=tp["v"], in_=S.v[:, 0:P.TC * P.H * 65]).then_inc(P.sO.h, 16)
                P.sO.v += 16
            if "sml" in tp:
                g.dma_start(out=tp["sml"], in_=S.sml[:]).then_inc(P.sO.h, 16)
                P.sO.v += 16
            if "negm" in tp:
                g.dma_start(out=tp["negm"], in_=S.negm[:]).then_inc(P.sO.h, 16)
                P.sO.v += 16
            if "ao" in tp:
                g.dma_start(out=tp["ao"], in_=S.ao[:, 0:P.H2 * P.T]).then_inc(P.sO.h, 16)
                P.sO.v += 16
            if "aot" in tp:
                g.dma_start(out=tp["aot"], in_=S.aot[:, 0:P.TC * 128]).then_inc(P.sO.h, 16)
                P.sO.v += 16
            if "ffo" in tp:
                for tc in range(P.TC):
                    g.dma_start(out=tp["ffo"][tc * 128:(tc + 1) * 128, :],
                                in_=P.ffo_ap(P, tc)).then_inc(P.sO.h, 16)
                    P.sO.v += 16
            g.wait_ge(P.sO.h, P.sO.v)
            g.wait_ge(P.sOa.h, P.sOa.v)
            g.wait_ge(P.sOb.h, P.sOb.v)


_NC_CACHE = {}


def _get_nc():
    key = "full"
    if key not in _NC_CACHE:
        _NC_CACHE[key] = build_program(CFG_FULL)
    return _NC_CACHE[key]


def kernel(L0_attn_x, L0_attn_wq, L0_attn_bq, L0_attn_wk, L0_attn_bk,
           L0_attn_wv, L0_attn_bv, L0_attn_wp, L0_attn_bp,
           L0_ffn_w_gate, L0_ffn_w_up, L0_ffn_w_down):
    nc = _get_nc()
    n = CFG_FULL["NCORES"]
    x = np.ascontiguousarray(np.asarray(L0_attn_x, dtype=np.float32))
    shared = {
        "wq": np.ascontiguousarray(np.asarray(L0_attn_wq, dtype=np.float32)),
        "wk": np.ascontiguousarray(np.asarray(L0_attn_wk, dtype=np.float32)),
        "wv": np.ascontiguousarray(np.asarray(L0_attn_wv, dtype=np.float32)),
        "wp": np.ascontiguousarray(np.asarray(L0_attn_wp, dtype=np.float32)),
        "bq": np.ascontiguousarray(np.asarray(L0_attn_bq, dtype=np.float32)),
        "bk": np.ascontiguousarray(np.asarray(L0_attn_bk, dtype=np.float32)),
        "bv": np.ascontiguousarray(np.asarray(L0_attn_bv, dtype=np.float32)),
        "bp": np.ascontiguousarray(np.asarray(L0_attn_bp, dtype=np.float32)),
        "wg": np.ascontiguousarray(np.asarray(L0_ffn_w_gate, dtype=np.float32)),
        "wu": np.ascontiguousarray(np.asarray(L0_ffn_w_up, dtype=np.float32)),
        "wd": np.ascontiguousarray(np.asarray(L0_ffn_w_down, dtype=np.float32)),
    }
    in_maps = [dict(shared, x=np.ascontiguousarray(x[i])) for i in range(n)]
    res = run_bass_kernel_spmd(nc, in_maps, core_ids=list(range(n)))
    return np.stack([np.asarray(res.results[i]["y"], dtype=np.float32)
                     for i in range(n)], axis=0)

def _b67_mega(P):
    nc, S, D, view = P.nc, P.S, P.D, P.view
    TC, CC, C, T, H, IC, I = P.TC, P.CC, P.C, P.T, P.H, P.IC, P.I
    th = T // 2
    tph = th // 128
    IH = I // 2
    NF = max(1, T // 512)
    nhd = IH // 128

    wld = view(S.arena, 0, 128, C, F32)
    wtmp = view(S.arena, 4 * C, 128, C, F32)
    wqc = view(S.arena, 8 * C, 128, C, BF16)
    wt = view(S.arena, 10 * C, 128, C, BF16)
    cbuf = lambda p: view(S.xf, p * 2 * T, 128, T, BF16)
    wld3 = view(S.xn, 0, 128, IH, F32)
    wtmp3 = view(S.xn, 4 * IH, 128, IH, F32)
    wqc3 = view(S.xn, 8 * IH, 128, IH, BF16)
    tstage = view(S.xn, 10 * IH, 128, IC * 128, BF16)

    chunks = ([("ug", 0, oc) for oc in range(IC)] +
              [("ug", 1, oc) for oc in range(IC)] +
              [("wp", oc) for oc in range(CC)] +
              [("wd", oc, hv) for oc in range(CC) for hv in range(2)])
    NCH = len(chunks)
    n_ug = 2 * IC
    titems = []
    for k, ch in enumerate(chunks):
        if ch[0] == "ug":
            titems.append(("tr", k))
            for fi in range(NF):
                titems.append(("fl", k, fi))
        elif ch[0] == "wp":
            titems.append(("trp", k))
        else:
            titems.append(("trd", k))
    NSEG = 2 * H
    NTI = len(titems)
    seg_of_item = [min(NSEG - 1, i * NSEG // NTI) for i in range(NTI)]
    SEGC = [[] for _ in range(NSEG)]
    for i, it in enumerate(titems):
        SEGC[seg_of_item[i]].append(it)
    seg_of_chunk = {}
    for i, it in enumerate(titems):
        k = it[1]
        seg_of_chunk.setdefault(k, seg_of_item[i])

    ev_t, ev_s, ev_v = [], [], []
    for h in range(H):
        for f in range(2):
            for j in range(TC):
                ev_t.append(("sc", h, f, j))
                if j >= 1:
                    ev_t.append(("av", h, f, j - 1))
            ev_t.append(("av", h, f, TC - 1))
            ev_s.extend(("exp", h, f, j) for j in range(TC))
            ev_s.append(("xav", h, f))
        for tau in range(TC):
            ev_t.append(("t1", h, tau))
        for bi in range((TC + 7) // 8):
            ev_v.append(("dvb", h, bi))
        if h % 2 == 1:
            ev_t.append(("t2", h))
            ev_v.append(("aoev", h))
    bT, bS, bV = P.sT.v, P.sS.v, P.sV.v
    tick_t = {e: bT + i + 1 for i, e in enumerate(ev_t)}
    tick_s = {e: bS + i + 1 for i, e in enumerate(ev_s)}
    tick_v = {e: bV + i + 1 for i, e in enumerate(ev_v)}

    b6T = P.s6T.v
    tick_6t = {it: b6T + i + 1 for i, it in enumerate(titems)}
    def _vseg_items(s):
        out = []
        if s < 0:
            return [("q", 0)]
        for it in SEGC[s]:
            if it[0] == "tr":
                out.append(("we", it[1]))
                if it[1] + 1 < NCH:
                    out.append(("q", it[1] + 1))
            elif it[0] == "fl":
                out.append(("fe", it[1], it[2]))
                if it[2] == NF - 1:
                    out.append(("red", it[1]))
                    if it[1] == n_ug - 1:
                        out.append(("a23",))
            elif it[0] == "trp":
                out.append(("pe", it[1]))
                if it[1] + 1 < NCH:
                    out.append(("q", it[1] + 1))
            else:
                out.append(("de", it[1]))
                if it[1] + 1 < NCH:
                    out.append(("q", it[1] + 1))
        return out

    vitems = []
    for s in range(-1, NSEG):
        vitems.extend(_vseg_items(s))
    b6V = P.s6V.v
    tick_6v = {it: b6V + i + 1 for i, it in enumerate(vitems)}
    spills = []
    for k, ch in enumerate(chunks):
        if ch[0] == "ug" or ch[0] == "wp" or (ch[0] == "wd" and ch[2] == 1):
            spills.append(k)
    bGp = [P.sGa.v, P.sGb.v]
    spill_idx = {k: i for i, k in enumerate(spills)}
    def spill_tick(i):
        return (P.sGp[i % 2].h, bGp[i % 2] + 16 * (i // 2 + 1))
    bLa = P.sLa.v

    psc_b16 = S.psc[:].bitcast(BF16)
    psc_fill = S.psc[:, 512:1024]
    P._dbg = dict(tick_t=tick_t, tick_s=tick_s, tick_v=tick_v,
                  tick_6t=tick_6t, tick_6v=tick_6v)

    def chunk_dram(ch):
        if ch[0] == "ug":
            return (D.wg if ch[1] == 0 else D.wu), ch[2]
        if ch[0] == "wp":
            return D.wp, ch[1]
        return D.wd, ch[1]

    with nc.Block() as b:
        @b.sync
        def _(s):
            for k, ch in enumerate(chunks):
                if k > 0:
                    s.wait_ge(P.s6V.h, tick_6v[("q", k - 1)])
                if ch[0] == "wd" and ch == ("wd", 0, 0):
                    s.wait_ge(P.s6T.h, tick_6t[("fl", n_ug - 1, NF - 1)])
                wdram, oc = chunk_dram(ch)
                if ch[0] == "wd":
                    hv = ch[2]
                    s.dma_start(out=wld3,
                                in_=wdram[oc * 128:(oc + 1) * 128, hv * IH:(hv + 1) * IH]
                                ).then_inc(P.sLa.h, 16)
                else:
                    s.dma_start(out=wld,
                                in_=wdram[oc * 128:(oc + 1) * 128, :]).then_inc(P.sLa.h, 16)
                P.sLa.v += 16

        @b.tensor
        def _(t):
            flat_items = [(s, it) for s in range(NSEG) for it in SEGC[s]]
            fptr = [0]
            cur_seg = [0]

            def emit_chunk_item(it):
                if it[0] == "tr" or it[0] == "trp":
                    k = it[1]
                    t.wait_ge(P.s6V.h, tick_6v[("q", k)])
                    for kk in range(k - 1, -1, -1):
                        c0 = chunks[kk][0]
                        t.wait_ge(P.s6V.h, tick_6v[
                            ("we", kk) if c0 == "ug" else
                            (("pe", kk) if c0 == "wp" else ("de", kk))])
                        break
                    src = wqc
                    inst = None
                    for ic in range(CC):
                        inst = t.transpose(psc_b16[:, ic * 128:(ic + 1) * 128],
                                           src[:, ic * 128:(ic + 1) * 128], S.ident_b[:])
                    inst.then_inc(P.s6T.h, 1)
                    P.s6T.v += 1
                elif it[0] == "trd":
                    k = it[1]
                    t.wait_ge(P.s6V.h, tick_6v[("q", k)])
                    for kk in range(k - 1, -1, -1):
                        c0 = chunks[kk][0]
                        t.wait_ge(P.s6V.h, tick_6v[
                            ("we", kk) if c0 == "ug" else
                            (("pe", kk) if c0 == "wp" else ("de", kk))])
                        break
                    for kk in range(k - 1, -1, -1):
                        if chunks[kk][0] == "ug":
                            t.wait_ge(P.s6V.h, tick_6v[("fe", kk, NF - 1)])
                            break
                        if chunks[kk][0] == "wd":
                            break
                    inst = None
                    for j in range(nhd):
                        inst = t.transpose(psc_b16[:, j * 128:(j + 1) * 128],
                                           wqc3[:, j * 128:(j + 1) * 128], S.ident_b[:])
                    inst.then_inc(P.s6T.h, 1)
                    P.s6T.v += 1
                else:
                    _, k, fi = it
                    t.wait_ge(P.s6V.h, tick_6v[("we", k)])
                    if fi > 0:
                        t.wait_ge(P.s6V.h, tick_6v[("fe", k, fi - 1)])
                    elif k > 0 and chunks[k - 1][0] == "ug":
                        t.wait_ge(P.s6V.h, tick_6v[("fe", k - 1, NF - 1)])
                    t0_ = fi * 512
                    fwd = min(512, T - t0_)
                    inst = None
                    for ic in range(CC):
                        inst = t.matmul(psc_fill[:, 0:fwd],
                                        wt[:, ic * 128:(ic + 1) * 128],
                                        _xn_ap(P)[:, ic * T + t0_:ic * T + t0_ + fwd],
                                        start=(ic == 0), stop=(ic == CC - 1))
                    inst.then_inc(P.s6T.h, 1)
                    P.s6T.v += 1

            def feed(n=1):
                for _ in range(n):
                    if fptr[0] >= len(flat_items):
                        return
                    s_it, it = flat_items[fptr[0]]
                    if s_it > cur_seg[0]:
                        return
                    fptr[0] += 1
                    emit_chunk_item(it)

            def feed_seg(s):
                while fptr[0] < len(flat_items) and flat_items[fptr[0]][0] <= s:
                    emit_chunk_item(flat_items[fptr[0]][1])
                    fptr[0] += 1

            def prev_hf(h, f):
                if f == 1:
                    return (h, 0)
                return (h - 1, 1) if h > 0 else None

            for h in range(H):
                pc, hp = h // 2, h % 2
                for f in range(2):
                    prev = prev_hf(h, f)
                    for j in range(TC):
                        pm = S.psa if j % 2 == 0 else S.psb
                        if j >= 2:
                            t.wait_ge(P.sS.h, tick_s[("exp", h, f, j - 2)])
                        elif prev is not None:
                            t.wait_ge(P.sS.h, tick_s[("exp", prev[0], prev[1], TC - 1)])
                        inst = None
                        for sl in range(0, th, 512):
                            sw = min(512, th - sl)
                            inst = t.matmul(pm[:, sl:sl + sw],
                                            S.k[hp * 64:hp * 64 + 64,
                                                pc * T + j * 128:pc * T + (j + 1) * 128],
                                            S.q[hp * 64:hp * 64 + 64,
                                                pc * T + f * th + sl:pc * T + f * th + sl + sw],
                                            start=True, stop=True,
                                            tile_position=(hp * 64, 0))
                        inst.then_inc(P.sT.h, 1)
                        P.sT.v += 1
                        if j >= 1:
                            _mega_av(P, t, tick_s, tick_v, h, f, j - 1, prev)
                        feed(1)
                    _mega_av(P, t, tick_s, tick_v, h, f, TC - 1, prev)
                    feed_seg(cur_seg[0])
                    cur_seg[0] += 1
                for tau in range(TC):
                    f2, loc = divmod(tau, tph)
                    t.wait_ge(P.sS.h, tick_s[("xav", h, 1)])
                    if tau >= 8:
                        t.wait_ge(P.sV.h, tick_v[("dvb", h, tau // 8 - 1)])
                    slot = (tau % 8) * 128
                    inst = t.transpose(
                        S.psd[:, slot:slot + 65],
                        S.sav[0:65, f2 * 1024 + loc * 128:f2 * 1024 + (loc + 1) * 128],
                        S.ident_f[0:65, 0:65])
                    inst.then_inc(P.sT.h, 1)
                    P.sT.v += 1
                if h % 2 == 1:
                    t.wait_ge(P.sV.h, tick_v[("dvb", h, (TC + 7) // 8 - 1)])
                    psdb = S.psd[:].bitcast(BF16)
                    inst = None
                    for tau in range(TC):
                        inst = t.transpose(psdb[:, tau * 128:(tau + 1) * 128],
                                           S.aot[:, tau * 128:(tau + 1) * 128],
                                           S.ident_b[:])
                    inst.then_inc(P.sT.h, 1)
                    P.sT.v += 1
            feed_seg(NSEG)

        @b.scalar
        def _(s):
            for h in range(H):
                for f in range(2):
                    for j in range(TC):
                        pm = S.psa if j % 2 == 0 else S.psb
                        s.wait_ge(P.sT.h, tick_t[("sc", h, f, j)])
                        if j >= 2:
                            s.wait_ge(P.sT.h, tick_t[("av", h, f, j - 2)])
                        ptb = S.pt[:, (j % 2) * 1024:(j % 2) * 1024 + th]
                        s.activation(ptb, pm[:, 0:th], AF.Exp,
                                     bias=S.negm[:, h:h + 1]).then_inc(P.sS.h, 1)
                        P.sS.v += 1
                    s.wait_ge(P.sT.h, tick_t[("av", h, f, TC - 1)])
                    s.activation(S.sav[0:65, f * 1024:f * 1024 + th],
                                 S.psd[0:65, 0:th], AF.Identity)
                    s.drain().then_inc(P.sS.h, 1)
                    P.sS.v += 1

        @b.vector
        def _(v):
            vit = iter(vitems)

            def emit_v(it):
                if it[0] == "q":
                    k = it[1]
                    ch = chunks[k]
                    v.wait_ge(P.sLa.h, bLa + 16 * (k + 1))
                    if k > 0:
                        pit = ("tr", k - 1) if chunks[k - 1][0] == "ug" else (
                            ("trp", k - 1) if chunks[k - 1][0] == "wp" else ("trd", k - 1))
                        v.wait_ge(P.s6T.h, tick_6t[pit])
                    if ch[0] == "ug":
                        _quant_blk(P, v, wld, wtmp, wld, wqc, C)
                    elif ch[0] == "wp":
                        _quant_row(P, v, wld, wtmp, wqc, None)
                    else:
                        _quant_blk(P, v, wld3, wtmp3, wld3, wqc3, IH)
                    v.drain().then_inc(P.s6V.h, 1)
                    P.s6V.v += 1
                elif it[0] == "we":
                    k = it[1]
                    v.wait_ge(P.s6T.h, tick_6t[("tr", k)])
                    v.tensor_copy(wt, psc_b16[:, 0:C])
                    v.drain().then_inc(P.s6V.h, 1)
                    P.s6V.v += 1
                elif it[0] == "fe":
                    _, k, fi = it
                    v.wait_ge(P.s6T.h, tick_6t[("fl", k, fi)])
                    if fi == 0:
                        si = spill_idx[k]
                        if si >= 2:
                            v.wait_ge(*spill_tick(si - 2))
                    fwd = min(512, T - fi * 512)
                    v.tensor_copy(cbuf(spill_idx[k] % 2)[:, fi * 512:fi * 512 + fwd],
                                  psc_fill[:, 0:fwd])
                    v.drain().then_inc(P.s6V.h, 1)
                    P.s6V.v += 1
                elif it[0] == "red":
                    k = it[1]
                    mcol = GT_MAX if chunks[k][1] == 0 else UP_MAX
                    v.reduce_max(S.sml[:, TMP0:TMP0 + 1], cbuf(spill_idx[k] % 2),
                                 axis=AX, apply_absolute_value=True)
                    v.drain()
                    v.tensor_max(S.sml[:, mcol:mcol + 1], S.sml[:, mcol:mcol + 1],
                                 S.sml[:, TMP0:TMP0 + 1])
                    v.drain().then_inc(P.s6V.h, 1)
                    P.s6V.v += 1
                elif it[0] == "a23":
                    v.wait_ge(P.sX.h, 32)
                    v.reduce_max(S.row[0:1, 0:1], S.row[0:1, 128:256], axis=AX)
                    v.drain()
                    v.reduce_max(S.row[0:1, 1:2], S.row[0:1, 256:384], axis=AX)
                    v.drain().then_inc(P.s6V.h, 1)
                    P.s6V.v += 1
                elif it[0] == "pe":
                    k = it[1]
                    v.wait_ge(P.s6T.h, tick_6t[("trp", k)])
                    si = spill_idx[k]
                    if si >= 2:
                        v.wait_ge(*spill_tick(si - 2))
                    v.tensor_copy(cbuf(si % 2)[:, 0:C], psc_b16[:, 0:C])
                    v.drain().then_inc(P.s6V.h, 1)
                    P.s6V.v += 1
                else:
                    k = it[1]
                    ch = chunks[k]
                    v.wait_ge(P.s6T.h, tick_6t[("trd", k)])
                    if ch[2] == 0 and k > 0:
                        for kk in range(k - 1, -1, -1):
                            if chunks[kk][0] == "wd" and chunks[kk][2] == 1:
                                v.wait_ge(*spill_tick(spill_idx[kk]))
                                break
                    v.tensor_copy(tstage[:, ch[2] * IH:ch[2] * IH + nhd * 128],
                                  psc_b16[:, 0:nhd * 128])
                    v.drain().then_inc(P.s6V.h, 1)
                    P.s6V.v += 1

            vseg = {s: _vseg_items(s) for s in range(-1, NSEG)}

            for it in vseg[-1]:
                emit_v(it)
            seg = 0
            for h in range(H):
                hp = h % 2
                for f in range(2):
                    for it in vseg[seg]:
                        emit_v(it)
                    seg += 1
                for bi in range((TC + 7) // 8):
                    lo = bi * 8
                    bs = min(8, TC - lo)
                    v.wait_ge(P.sT.h, tick_t[("t1", h, lo + bs - 1)])
                    ps3 = S.psd[:, 0:bs * 128].rearrange("p (k o) -> p k o", o=128)
                    v.reciprocal(S.sig[:, lo:lo + bs], ps3[:, :, 64:65].squeeze(2))
                    v.drain()
                    for tau in range(lo, lo + bs):
                        v.tensor_scalar(
                            S.aot[:, tau * 128 + hp * 64:tau * 128 + hp * 64 + 64],
                            S.psd[:, (tau % 8) * 128:(tau % 8) * 128 + 64],
                            S.sig[:, tau:tau + 1], None, OP.mult)
                        v.drain()
                    v.drain().then_inc(P.sV.h, 1)
                    P.sV.v += 1
                if h % 2 == 1:
                    v.wait_ge(P.sT.h, tick_t[("t2", h)])
                    psdb = S.psd[:].bitcast(BF16)
                    out = S.ao[:, (h // 2) * T:(h // 2) * T + TC * 128]
                    for c0 in range(0, TC * 128, 1024):
                        cw = min(1024, TC * 128 - c0)
                        v.tensor_copy(out[:, c0:c0 + cw], psdb[:, c0:c0 + cw])
                    v.drain().then_inc(P.sV.h, 1)
                    P.sV.v += 1

        @b.gpsimd
        def _(g):
            did_a23 = False
            for si, k in enumerate(spills):
                ch = chunks[k]
                if si >= 2:
                    g.wait_ge(*spill_tick(si - 2))
                if ch[0] == "ug":
                    g.wait_ge(P.s6V.h, tick_6v[("fe", k, NF - 1)])
                    g.dma_start(out=(D.s_gate if ch[1] == 0 else D.s_up)[
                        ch[2]:ch[2] + 1, :, :].squeeze(0),
                        in_=cbuf(si % 2)).then_inc(P.sGp[si % 2].h, 16)
                elif ch[0] == "wp":
                    g.wait_ge(P.s6V.h, tick_6v[("pe", k)])
                    oc = ch[1]
                    g.dma_start(
                        out=D.s_wpt[:, :, oc * 128:(oc + 1) * 128].transpose([1, 0, 2]),
                        in_=cbuf(si % 2)[:, 0:C].rearrange("p (i m) -> p i m", m=128)
                        ).then_inc(P.sGp[si % 2].h, 16)
                else:
                    g.wait_ge(P.s6V.h, tick_6v[("de", k)])
                    oc = ch[1]
                    g.dma_start(
                        out=D.s_wdt[:, :, oc:oc + 1, :].squeeze(2).transpose([1, 0, 2]),
                        in_=tstage.rearrange("p (i j) -> p i j", j=128),
                        allow_slow_non_contiguous=True).then_inc(P.sGp[si % 2].h, 16)
                P.sGp[si % 2].v += 16
                if not did_a23 and ch == ("ug", 1, IC - 1):
                    did_a23 = True
                    g.wait_ge(P.s6V.h, tick_6v[("red", k)])
                    g.dma_start(out=S.row[0:1, 128:256],
                                in_=S.sml[:, UP_MAX:UP_MAX + 1]).then_inc(P.sX.h, 16)
                    g.dma_start(out=S.row[0:1, 256:384],
                                in_=S.sml[:, GT_MAX:GT_MAX + 1]).then_inc(P.sX.h, 16)
                    P.sX.v += 32
                    g.wait_ge(P.s6V.h, tick_6v[("a23",)])
                    cin, cout = D.cc[1]
                    g.dma_start(out=cin[:], in_=S.row[0:1, 0:128]).then_inc(P.sX.h, 16)
                    P.sX.v += 16
                    g.wait_ge(P.sX.h, P.sX.v)
                    g.collective_compute(
                        "AllReduce", OP.max, replica_groups=[list(range(P.NCORES))],
                        ins=[cin[:]], outs=[cout[:]]).then_inc(P.sC.h, 1)
                    P.sC.v += 1
                    g.wait_ge(P.sC.h, P.sC.v)
                    g.dma_start(out=S.sml[:, UP_MAX:UP_MAX + 1],
                                in_=cout[0:1, 0:1].broadcast_to([128, 1])).then_inc(P.sX.h, 16)
                    g.dma_start(out=S.sml[:, GT_MAX:GT_MAX + 1],
                                in_=cout[0:1, 1:2].broadcast_to([128, 1])).then_inc(P.sX.h, 16)
                    P.sX.v += 32
                    g.wait_ge(P.sX.h, P.sX.v)
            g.wait_ge(P.sGa.h, P.sGa.v)
            g.wait_ge(P.sGb.h, P.sGb.v)


def _mega_av(P, t, tick_s, tick_v, h, f, j, prev):
    S = P.S
    TC, T, H = P.TC, P.T, P.H
    th = T // 2
    t.wait_ge(P.sS.h, tick_s[("exp", h, f, j)])
    if j == 0:
        if prev is not None:
            t.wait_ge(P.sS.h, tick_s[("xav", prev[0], prev[1])])
        if f == 0 and h > 0:
            if (h - 1) % 2 == 1:
                t.wait_ge(P.sV.h, tick_v[("aoev", h - 1)])
            else:
                t.wait_ge(P.sV.h, tick_v[("dvb", h - 1, (TC + 7) // 8 - 1)])
    vj = S.v[:, j * H * 65 + h * 65:j * H * 65 + h * 65 + 65]
    ptb = S.pt[:, (j % 2) * 1024:(j % 2) * 1024 + th]
    inst = None
    for sl in range(0, th, 512):
        sw = min(512, th - sl)
        inst = t.matmul(S.psd[0:65, sl:sl + sw], vj, ptb[:, sl:sl + sw],
                        start=(j == 0), stop=(j == TC - 1))
    inst.then_inc(P.sT.h, 1)
    P.sT.v += 1


# version 30
# speedup vs baseline: 1.0375x; 1.0375x over previous
import contextlib
import math
from types import SimpleNamespace

import numpy as np

import concourse.bass as bass
import concourse.mybir as mybir
from concourse.bass_utils import run_bass_kernel_spmd

F32 = mybir.dt.float32
BF16 = mybir.dt.bfloat16
U8 = mybir.dt.uint8
AX = mybir.AxisListType.X
OP = mybir.AluOpType
AF = mybir.ActivationFunctionType

RC = float(np.float32(1.5 * 2 ** 23))
EPS = float(np.finfo(np.float32).eps)
SQG = float(np.float32(math.sqrt(0.125)))
CLP = 1e-12
INV16 = float(np.float32(2.0 ** -16))

CFG_FULL = dict(TC=16, CC=8, IC=24, H=16, NCORES=8)
CFG_SMALL = dict(TC=2, CC=2, IC=4, H=4, NCORES=2)
CFG_MED = dict(TC=8, CC=2, IC=4, H=4, NCORES=2)
CFG_MED6 = dict(TC=16, CC=8, IC=8, H=16, NCORES=2)


DBG = dict(standalone_a4=False, standalone_a5=False, no_pproj=False)


class Sem:
    def __init__(self, nc, es, name):
        self.h = es.enter_context(nc.semaphore(name))
        self.v = 0

    def inc(self, inst, n=1):
        inst.then_inc(self.h, n)
        self.v += n
        return self.v


XN_MAX, TMP0 = 0, 1
S1, R1 = 2, 3
UP_MAX, GT_MAX = 4, 5
S2, R2 = 6, 7
S3, R3 = 8, 9
PR_MAX, S4, R4 = 10, 11, 12
OR_MAX, S5, R5 = 13, 14, 15
PR_TRUE, R4S = 16, 17
NRC3, NRC42 = 19, 20


def build_program(cfg, taps=()):
    P = SimpleNamespace(**cfg)
    P.T, P.C, P.I = P.TC * 128, P.CC * 128, P.IC * 128
    P.H2 = P.H // 2
    assert P.H == 2 * P.CC
    P.taps = set(taps)

    nc = bass.Bass("TRN2", target_bir_lowering=False, debug=False)
    P.nc = nc
    es = contextlib.ExitStack()
    P.es = es
    T, C, I, TC, CC, IC, H = P.T, P.C, P.I, P.TC, P.CC, P.IC, P.H

    D = SimpleNamespace()
    P.D = D
    D.x = nc.dram_tensor("x", [T, C], F32, kind="ExternalInput").ap()
    for w in ("wq", "wk", "wv", "wp"):
        setattr(D, w, nc.dram_tensor(w, [C, C], F32, kind="ExternalInput").ap())
    for b2 in ("bq", "bk", "bv", "bp"):
        setattr(D, b2, nc.dram_tensor(b2, [C], F32, kind="ExternalInput").ap())
    D.wg = nc.dram_tensor("wg", [I, C], F32, kind="ExternalInput").ap()
    D.wu = nc.dram_tensor("wu", [I, C], F32, kind="ExternalInput").ap()
    D.wd = nc.dram_tensor("wd", [C, I], F32, kind="ExternalInput").ap()
    D.y = nc.dram_tensor("y", [T, C], F32, kind="ExternalOutput").ap()
    D.s_up = nc.dram_tensor("s_up", [IC, 128, T], BF16, kind="Internal").ap()
    D.s_gate = nc.dram_tensor("s_gate", [IC, 128, T], BF16, kind="Internal").ap()
    D.s_prod = nc.dram_tensor("s_prod", [IC, 128, T], BF16, kind="Internal").ap()
    D.s_orw = nc.dram_tensor("s_orw", [IC, 128, T], BF16, kind="Internal").ap()
    D.s_wdt = nc.dram_tensor("s_wdt", [IC, 128, CC, 128], BF16, kind="Internal").ap()
    D.s_wpt = nc.dram_tensor("s_wpt", [CC, 128, C], BF16, kind="Internal").ap()
    D.s_nm = nc.dram_tensor("s_nm", [CC, 2], F32, kind="Internal").ap()
    D.cc = [(nc.dram_tensor(f"cci{i}", [1, 128], F32, kind="Internal").ap(),
             nc.dram_tensor(f"cco{i}", [1, 128], F32, kind="Internal").ap())
            for i in range(4)]
    P.tap_t = {}
    for name, shape, dt in (
        ("xnfm", [128, CC * T], BF16), ("q", [128, CC * T], BF16),
        ("k", [128, CC * T], BF16), ("v", [128, TC * H * 65], BF16),
        ("sml", [128, 32], F32), ("negm", [128, H], F32),
        ("ao", [128, P.H2 * T], BF16), ("aot", [128, TC * 128], BF16),
        ("ffo", [T, C], F32),
    ):
        if name in P.taps:
            P.tap_t[name] = nc.dram_tensor("tp_" + name, shape, dt,
                                           kind="ExternalOutput").ap()

    SB = lambda n, sh, dt: es.enter_context(nc.sbuf_tensor(n, sh, dt))
    PS = lambda n, sh, dt: es.enter_context(nc.psum_tensor(n, sh, dt))
    S = SimpleNamespace()
    P.S = S

    def bmax(*sizes):
        return (max(sizes) + 1) // 2

    ao_scratch = max(20 * C,
                     2 * C * CC + 14 * C,
                     24 * C + 4 * T,
                     9 * I,
                     4 * min(2048, T))
    vt_scratch = max(16 * T,
                     IC * 512 * 2 + 2 * C * 2 + 2 * 512 * 2 + 512 * 4)
    xn_scratch = max(CC * C * 2 + 2 * C * 4, 7 * I)
    S.xn = SB("xn_fm", [128, max(CC * T, (xn_scratch + 1) // 2)], BF16)
    S.q = SB("q_fm", [128, max(CC * T, (TC * C * 4 // 2 + 1) // 2)], BF16)
    S.k = SB("k_fm", [128, max(CC * T, (TC * C * 4 // 2 + 1) // 2)], BF16)
    S.v = SB("v_tm", [128, max(TC * H * 65, (vt_scratch + 1) // 2)], BF16)
    S.ao = SB("ao_fm", [128, max(P.H2 * T, (ao_scratch + 1) // 2)], BF16)
    S.pt = SB("pt_buf", [128, max(2048, 2 * C, T)], BF16)
    S.sav = SB("stage_av", [65, 2048], F32)
    S.aot = SB("ao_tm", [128, max(TC * 128, 2 * C)], BF16)
    S.xf = SB("xf32", [128, 2048], F32)
    S.ident_f = SB("ident_f", [128, 128], F32)
    S.ident_b = SB("ident_b", [128, 128], BF16)
    S.iota_a = SB("iota_a", [128, 128], F32)
    S.iota_b = SB("iota_b", [128, 1], F32)
    S.ones1 = SB("ones1", [1, 128], BF16)
    S.oblk = SB("oblk", [128, 2], BF16)
    S.bqt = SB("bqt", [128, CC], F32)
    S.bkt = SB("bkt", [128, CC], F32)
    S.bvr = SB("bvr", [1, C], BF16)
    S.bpr = SB("bpr", [1, C], BF16)
    S.negm = SB("negm", [128, H], F32)
    S.nq2 = SB("nq2", [2, CC + 1], F32)
    S.nk2 = SB("nk2", [2, CC + 1], F32)
    S.nm2 = SB("nm2", [2, CC], F32)
    S.sml = SB("sml", [128, 32], F32)
    S.row = SB("row", [1, 512], F32)
    S.wsc = SB("wsc", [128, 96], F32)
    S.wsc2 = SB("wsc2", [128, 96], F32)
    S.lnag = SB("lnag", [128, 16], F32)
    S.c_eps = SB("c_eps", [128, 1], F32)
    S.c_rc = SB("c_rc", [128, 1], F32)
    S.sig = SB("sig", [128, TC], F32)
    S.arena = SB("arena", [128, 6144], BF16)

    S.psa = PS("psa", [128, 1024], F32)
    S.psb = PS("psb", [128, 1024], F32)
    S.psc = PS("psc", [128, 1024], F32)
    S.psd = PS("psd", [128, 1024], F32)

    def view(base, byte_off, part, free, dt):
        esz = mybir.dt.size(dt)
        bb = base[0:part, :].bitcast(U8)
        assert byte_off % 4 == 0 and byte_off + free * esz <= bb.shape[1], \
            (byte_off, free, esz, bb.shape)
        return bb[:, byte_off:byte_off + free * esz].bitcast(dt)

    P.view = view

    for name in ("sL", "sG", "sU", "sT", "sV", "sS", "sC", "sO",
                 "sLa", "sLb", "sGa", "sGb", "sOa", "sOb", "s6T", "s6V", "sX",
                 "sY", "sW0", "sW1", "sW2", "sW3"):
        setattr(P, name, Sem(nc, es, name))
    P.sLp = [P.sLa, P.sLb]
    P.sGp = [P.sGa, P.sGb]
    P.sOp = [P.sOa, P.sOb]

    _b0_init(P)
    _b1_ln(P)
    _allreduce(P, 0, [XN_MAX])
    _scales(P, [(XN_MAX, S1, R1, 32767.0, CLP, None)])
    _b3_qkv(P)
    _b4_norms(P)
    _b67_mega(P)
    _scales(P, [(UP_MAX, S2, R2, 32767.0, CLP, None),
                (GT_MAX, S3, R3, 32767.0, CLP, None)])
    _mulcol(P, NRC3, S3, -RC)
    _b8_chain1(P)
    if DBG["standalone_a4"]:
        _allreduce(P, 2, [PR_TRUE])
    _scales(P, [(PR_TRUE, S4, R4, 32767.0, CLP, (R4S, INV16))])
    _b9_chain2(P)
    if DBG["standalone_a5"]:
        _allreduce(P, 3, [OR_MAX])
    _scales(P, [(OR_MAX, S5, R5, 32767.0, CLP, None)])
    _b10_down(P)
    _b12_taps(P)

    es.close()
    return nc


def _b0_init(P):
    nc, S, D = P.nc, P.S, P.D
    with nc.Block() as b:
        @b.gpsimd
        def _(g):
            g.iota(S.iota_a[:], [[1, 128]], channel_multiplier=0,
                   allow_small_or_imprecise_dtypes=True)
            g.iota(S.iota_b[:], [[0, 1]], channel_multiplier=1,
                   allow_small_or_imprecise_dtypes=True)
            g.dma_start(out=S.bvr[:], in_=D.bv.unsqueeze(0)).then_inc(P.sG.h, 16)
            g.dma_start(out=S.bpr[:], in_=D.bp.unsqueeze(0)).then_inc(P.sG.h, 16)
            P.sG.v += 32
            g.wait_ge(P.sG.h, P.sG.v)

        @b.sync
        def _(s):
            s.dma_start(out=S.bqt[:], in_=D.bq.rearrange("(c p) -> p c", p=128),
                        allow_slow_non_contiguous=True).then_inc(P.sL.h, 16)
            s.dma_start(out=S.bkt[:], in_=D.bk.rearrange("(c p) -> p c", p=128),
                        allow_slow_non_contiguous=True).then_inc(P.sL.h, 16)
            P.sL.v += 32
            s.wait_ge(P.sL.h, P.sL.v)

        @b.vector
        def _(v):
            v.memset(S.ones1[:], 1.0)
            v.memset(S.oblk[:], 0.0)
            v.drain()
            v.memset(S.oblk[0:64, 0:1], 1.0)
            v.memset(S.oblk[64:128, 1:2], 1.0)
            v.memset(bass.AP(S.v, 64, [[S.v.shape[1], 128], [P.H * 65, P.TC], [65, P.H]]), 1.0)
            v.memset(S.sml[:], 0.0)
            v.memset(S.row[:], 0.0)
            v.memset(S.negm[:], 0.0)
            v.memset(S.nq2[:], 0.0)
            v.memset(S.nk2[:], 0.0)
            v.memset(S.sig[:], 0.0)
            v.memset(S.lnag[:], 0.0)
            v.memset(S.c_eps[:], 1e-5)
            v.memset(S.c_rc[:], RC)

    with nc.Block() as b:
        @b.vector
        def _(v):
            v.tensor_scalar(S.ident_f[:], S.iota_a[:], S.iota_b[:], None, OP.is_equal)
            v.drain()
            v.tensor_copy(S.ident_b[:], S.ident_f[:])
            v.tensor_scalar_mul(S.bqt[:], S.bqt[:], SQG)
            v.tensor_scalar_mul(S.bkt[:], S.bkt[:], SQG)


def _b1_ln(P):
    nc, S, D = P.nc, P.S, P.D
    TC, CC, C, T = P.TC, P.CC, P.C, P.T
    bV, bS, bT = P.sV.v, P.sS.v, P.sT.v
    bLp = [P.sLa.v, P.sLb.v]
    xnt = lambda par: P.view(S.pt, par * 2 * C, 128, C, BF16)

    with nc.Block() as b:
        @b.sync
        def _(s):
            for tc in range(0, TC, 2):
                if tc >= 2:
                    s.wait_ge(P.sV.h, bV + 2 * (tc - 1))
                s.dma_start(out=S.xf[:, (tc % 2) * 1024:(tc % 2) * 1024 + C],
                            in_=D.x[tc * 128:(tc + 1) * 128, :]).then_inc(P.sLp[tc % 2].h, 16)
                P.sLp[tc % 2].v += 16

        @b.vector
        def _(v):
            for tc in range(TC):
                sl = S.xf[:, (tc % 2) * 1024:(tc % 2) * 1024 + C]
                v.wait_ge(P.sLp[tc % 2].h, bLp[tc % 2] + 16 * (tc // 2 + 1))
                for hi, h0 in enumerate(range(0, C, 512)):
                    v.bn_stats(S.lnag[:, hi * 6:hi * 6 + 6],
                               sl[:, h0:h0 + min(512, C - h0)])
                v.drain()
                v.bn_aggr(S.lnag[:, 12:14], S.lnag[:, 0:6 * ((C + 511) // 512)])
                v.drain()
                v.tensor_scalar_mul(S.lnag[:, 14:15], S.lnag[:, 12:13], -1.0)
                v.drain().then_inc(P.sV.h, 1)
                P.sV.v += 1
                v.wait_ge(P.sS.h, bS + 2 * tc + 1)
                v.reciprocal(S.lnag[:, 15:16], S.lnag[:, 15:16])
                v.drain()
                v.tensor_scalar(xnt(tc % 2), sl, S.lnag[:, 14:15], S.lnag[:, 15:16],
                                OP.add, OP.mult)
                v.drain()
                v.reduce_max(S.sml[:, TMP0:TMP0 + 1], xnt(tc % 2), axis=AX,
                             apply_absolute_value=True)
                v.drain()
                v.tensor_max(S.sml[:, XN_MAX:XN_MAX + 1], S.sml[:, XN_MAX:XN_MAX + 1],
                             S.sml[:, TMP0:TMP0 + 1])
                v.drain().then_inc(P.sV.h, 1)
                P.sV.v += 1

        @b.scalar
        def _(s):
            def xld(tc):
                if tc >= TC:
                    return
                s.dma_start(out=S.xf[:, (tc % 2) * 1024:(tc % 2) * 1024 + C],
                            in_=D.x[tc * 128:(tc + 1) * 128, :]).then_inc(P.sLp[tc % 2].h, 16)
                P.sLp[tc % 2].v += 16
            if TC > 1:
                xld(1)
            for tc in range(TC):
                s.wait_ge(P.sV.h, bV + 2 * tc + 1)
                s.activation(S.lnag[:, 15:16], S.lnag[:, 13:14], AF.Sqrt,
                             bias=S.c_eps[:]).then_inc(P.sS.h, 1)
                P.sS.v += 1
                s.wait_ge(P.sT.h, bT + tc + 1)
                pv = (S.psa if tc % 2 == 0 else S.psb)[:].bitcast(BF16)[:, 0:C]
                inst = None
                for j in range(CC):
                    inst = s.activation(
                        S.xn[:, j * T + tc * 128:j * T + (tc + 1) * 128],
                        pv[:, j * 128:(j + 1) * 128], AF.Identity)
                inst.then_inc(P.sS.h, 1)
                P.sS.v += 1
                if tc % 2 == 1:
                    xld(tc + 2)

        @b.tensor
        def _(t):
            for tc in range(TC):
                t.wait_ge(P.sV.h, bV + 2 * tc + 2)
                if tc >= 2:
                    t.wait_ge(P.sS.h, bS + 2 * (tc - 1))
                ps = (S.psa if tc % 2 == 0 else S.psb)[:].bitcast(BF16)
                for ic in range(CC):
                    inst = t.transpose(ps[:, ic * 128:(ic + 1) * 128],
                                xnt(tc % 2)[:, ic * 128:(ic + 1) * 128],
                                S.ident_b[:])
                inst.then_inc(P.sT.h, 1)
                P.sT.v += 1


def _allreduce(P, idx, cols):
    nc, S = P.nc, P.S
    cin, cout = P.D.cc[idx]
    with nc.Block() as b:
        @b.scalar
        def _(s):
            for j, col in enumerate(cols):
                s.dma_start(out=S.row[0:1, 128 + j * 128:256 + j * 128],
                            in_=S.sml[:, col:col + 1]).then_inc(P.sU.h, 16)
                P.sU.v += 16

        @b.vector
        def _(v):
            v.wait_ge(P.sU.h, P.sU.v)
            for j, col in enumerate(cols):
                v.reduce_max(S.row[0:1, j:j + 1],
                             S.row[0:1, 128 + j * 128:256 + j * 128], axis=AX)
            v.drain()
            v.drain().then_inc(P.sV.h, 1)
            P.sV.v += 1

        @b.gpsimd
        def _(g):
            g.wait_ge(P.sV.h, P.sV.v)
            g.dma_start(out=cin[:], in_=S.row[0:1, 0:128]).then_inc(P.sG.h, 16)
            P.sG.v += 16
            g.wait_ge(P.sG.h, P.sG.v)
            g.collective_compute(
                "AllReduce", OP.max, replica_groups=[list(range(P.NCORES))],
                ins=[cin[:]], outs=[cout[:]],
            ).then_inc(P.sC.h, 1)
            P.sC.v += 1
            g.wait_ge(P.sC.h, P.sC.v)
            for j, col in enumerate(cols):
                g.dma_start(out=S.sml[:, col:col + 1],
                            in_=cout[0:1, j:j + 1].broadcast_to([128, 1])).then_inc(P.sG.h, 16)
                P.sG.v += 16
            g.wait_ge(P.sG.h, P.sG.v)


def _scales(P, specs):
    nc, S = P.nc, P.S
    with nc.Block() as b:
        @b.vector
        def _(v):
            for g_col, s_col, r_col, qmax, clamp, extra in specs:
                v.tensor_scalar(S.sml[:, s_col:s_col + 1], S.sml[:, g_col:g_col + 1],
                                float(np.float32(1.0) / np.float32(qmax)), clamp,
                                OP.mult, OP.max)
                v.drain()
                v.reciprocal(S.sml[:, r_col:r_col + 1], S.sml[:, s_col:s_col + 1])
                v.drain()
                if extra is not None:
                    ec, em = extra
                    v.tensor_scalar_mul(S.sml[:, ec:ec + 1], S.sml[:, r_col:r_col + 1], em)
                    v.drain()


def _mulcol(P, dst, srcc, m):
    nc, S = P.nc, P.S
    with nc.Block() as b:
        @b.vector
        def _(v):
            v.tensor_scalar_mul(S.sml[:, dst:dst + 1], S.sml[:, srcc:srcc + 1], m)
            v.drain()


def _quant_row(P, v, wld_ap, wtmp_ap, wqc_ap, extra_scale):
    S = P.S
    v.reduce_max(S.wsc[:, 0:1], wld_ap, axis=AX, apply_absolute_value=True)
    v.drain()
    v.tensor_scalar(S.wsc[:, 1:2], S.wsc[:, 0:1], float(np.float32(1.0) / np.float32(127.0)),
                    EPS, OP.mult, OP.max)
    v.drain()
    v.reciprocal(S.wsc[:, 2:3], S.wsc[:, 1:2])
    if extra_scale is not None:
        v.drain()
        v.tensor_scalar_mul(S.wsc[:, 1:2], S.wsc[:, 1:2], extra_scale)
    v.drain()
    v.tensor_scalar(wtmp_ap, wld_ap, S.wsc[:, 2:3], RC, OP.mult, OP.add)
    v.drain()
    v.tensor_scalar(wqc_ap, wtmp_ap, RC, S.wsc[:, 1:2], OP.subtract, OP.mult)
    v.drain()


def _quant_blk(P, v, wld_ap, wtmp_ap, wtmp2_ap, wqc_ap, width):
    S = P.S
    nb = width // 32
    w3 = wld_ap.rearrange("p (b k) -> p b k", k=32)
    v.tensor_reduce(S.wsc[:, 0:nb].unsqueeze(2), w3, axis=AX, op=OP.max,
                    apply_absolute_value=True)
    v.drain()
    v.tensor_scalar(S.wsc[:, 0:nb], S.wsc[:, 0:nb], float(np.float32(1.0) / np.float32(7.0)),
                    CLP, OP.mult, OP.max)
    v.drain()
    v.reciprocal(S.wsc2[:, 0:nb], S.wsc[:, 0:nb])
    v.drain()
    v.tensor_tensor(wtmp_ap.rearrange("p (b k) -> p b k", k=32), w3,
                    S.wsc2[:, 0:nb].unsqueeze(2).broadcast_to([128, nb, 32]), OP.mult)
    v.drain()
    v.tensor_scalar(wtmp2_ap, wtmp_ap, RC, RC, OP.add, OP.subtract)
    v.drain()
    v.tensor_tensor(wqc_ap.rearrange("p (b k) -> p b k", k=32),
                    wtmp2_ap.rearrange("p (b k) -> p b k", k=32),
                    S.wsc[:, 0:nb].unsqueeze(2).broadcast_to([128, nb, 32]), OP.mult)
    v.drain()


def _xn_ap(P):
    return P.S.xn[:, 0:P.CC * P.T]


def _b3_qkv(P):
    nc, S, D, view = P.nc, P.S, P.D, P.view
    TC, CC, C, T, H = P.TC, P.CC, P.C, P.T, P.H
    nf = max(1, T // 1024)

    for wdram, dst, bias in ((D.wq, S.q, S.bqt), (D.wk, S.k, S.bkt)):
        wld = lambda p: view(S.ao, p * 4 * C, 128, C, F32)
        wtmp = view(S.ao, 8 * C, 128, C, F32)
        wqc = lambda p: view(S.ao, 12 * C + p * 2 * C, 128, C, BF16)
        wt = lambda p: view(S.ao, 16 * C + p * 2 * C, 128, C, BF16)
        bV, bS, bT = P.sV.v, P.sS.v, P.sT.v
        bLp = [P.sLa.v, P.sLb.v]
        v_quant = lambda oc: bV + oc + 1
        t_transp = lambda oc: bT + oc * (1 + nf) + 1
        t_fill = lambda oc, fi: bT + oc * (1 + nf) + 1 + fi + 1
        s_wt = lambda oc: bS + oc * (1 + nf) + 1
        s_fill = lambda oc, fi: bS + oc * (1 + nf) + 1 + fi + 1

        with nc.Block() as b:
            @b.sync
            def _(s):
                for oc in range(0, CC, 2):
                    if oc >= 2:
                        s.wait_ge(P.sV.h, v_quant(oc - 2))
                    s.dma_start(out=wld(oc % 2),
                                in_=wdram[oc * 128:(oc + 1) * 128, :]).then_inc(P.sLp[oc % 2].h, 16)
                    P.sLp[oc % 2].v += 16

            @b.vector
            def _(v):
                for oc in range(CC):
                    v.wait_ge(P.sLp[oc % 2].h, bLp[oc % 2] + 16 * (oc // 2 + 1))
                    if oc >= 2:
                        v.wait_ge(P.sT.h, t_transp(oc - 2))
                    _quant_row(P, v, wld(oc % 2), wtmp, wqc(oc % 2), SQG)
                    v.drain().then_inc(P.sV.h, 1)
                    P.sV.v += 1

            @b.tensor
            def _(t):
                for oc in range(CC):
                    t.wait_ge(P.sV.h, v_quant(oc))
                    if oc >= 2:
                        t.wait_ge(P.sS.h, s_wt(oc - 2))
                    ps = (S.psc if oc % 2 == 0 else S.psd)[:].bitcast(BF16)
                    for ic in range(CC):
                        inst = t.transpose(ps[:, ic * 128:(ic + 1) * 128],
                                    wqc(oc % 2)[:, ic * 128:(ic + 1) * 128],
                                    S.ident_b[:])
                    inst.then_inc(P.sT.h, 1)
                    P.sT.v += 1
                    t.wait_ge(P.sS.h, s_wt(oc))
                    for fi, t0 in enumerate(range(0, T, 1024)):
                        fw = min(1024, T - t0)
                        pm = S.psa if (oc * nf + fi) % 2 == 0 else S.psb
                        g = oc * nf + fi
                        if g >= 2:
                            po, pf = divmod(g - 2, nf)
                            t.wait_ge(P.sS.h, s_fill(po, pf))
                        for sl in range(0, fw, 512):
                            sw = min(512, fw - sl)
                            for ic in range(CC):
                                inst = t.matmul(pm[:, sl:sl + sw],
                                         wt(oc % 2)[:, ic * 128:(ic + 1) * 128],
                                         _xn_ap(P)[:, ic * T + t0 + sl:ic * T + t0 + sl + sw],
                                         start=(ic == 0), stop=(ic == CC - 1))
                        inst.then_inc(P.sT.h, 1)
                        P.sT.v += 1

            @b.scalar
            def _(s):
                def wload(oc):
                    if oc >= CC:
                        return
                    s.dma_start(out=wld(oc % 2),
                                in_=wdram[oc * 128:(oc + 1) * 128, :]
                                ).then_inc(P.sLp[oc % 2].h, 16)
                    P.sLp[oc % 2].v += 16
                if CC > 1:
                    wload(1)
                for oc in range(CC):
                    s.wait_ge(P.sT.h, t_transp(oc))
                    pt = (S.psc if oc % 2 == 0 else S.psd)[:].bitcast(BF16)[:, 0:C]
                    s.activation(wt(oc % 2), pt, AF.Identity).then_inc(P.sS.h, 1)
                    P.sS.v += 1
                    if oc % 2 == 1:
                        wload(oc + 2)
                    for fi, t0 in enumerate(range(0, T, 1024)):
                        fw = min(1024, T - t0)
                        pm = S.psa if (oc * nf + fi) % 2 == 0 else S.psb
                        s.wait_ge(P.sT.h, t_fill(oc, fi))
                        s.activation(dst[:, oc * T + t0:oc * T + t0 + fw], pm[:, 0:fw],
                                     AF.Identity, bias=bias[:, oc:oc + 1]).then_inc(P.sS.h, 1)
                        P.sS.v += 1

    wvt = view(S.ao, 0, 128, CC * C, BF16)
    wld1 = lambda p: view(S.ao, 2 * CC * C + p * 4 * C, 128, C, F32)
    wtmp1 = view(S.ao, 2 * CC * C + 8 * C, 128, C, F32)
    wqc1 = view(S.ao, 2 * CC * C + 12 * C, 128, C, BF16)
    bL, bV, bS, bT = P.sL.v, P.sV.v, P.sS.v, P.sT.v
    bU = P.sU.v
    vlsem = [P.sL, P.sU]
    bVL = [bL, bU]
    with nc.Block() as b:
        @b.sync
        def _(s):
            for oc in range(0, CC, 2):
                if oc >= 2:
                    s.wait_ge(P.sV.h, bV + oc - 1)
                s.dma_start(out=wld1(oc % 2),
                            in_=D.wv[oc * 128:(oc + 1) * 128, :]
                            ).then_inc(vlsem[oc % 2].h, 16)
                vlsem[oc % 2].v += 16

        @b.vector
        def _(v):
            for oc in range(CC):
                v.wait_ge(vlsem[oc % 2].h, bVL[oc % 2] + 16 * (oc // 2 + 1))
                if oc >= 1:
                    v.wait_ge(P.sT.h, bT + oc)
                _quant_row(P, v, wld1(oc % 2), wtmp1, wqc1, None)
                v.drain().then_inc(P.sV.h, 1)
                P.sV.v += 1

        @b.tensor
        def _(t):
            for oc in range(CC):
                t.wait_ge(P.sV.h, bV + oc + 1)
                if oc >= 2:
                    t.wait_ge(P.sS.h, bS + oc - 1)
                ps = (S.psc if oc % 2 == 0 else S.psd)[:].bitcast(BF16)
                for ic in range(CC):
                    inst = t.transpose(ps[:, ic * 128:(ic + 1) * 128],
                                wqc1[:, ic * 128:(ic + 1) * 128],
                                S.ident_b[:])
                inst.then_inc(P.sT.h, 1)
                P.sT.v += 1
            t.wait_ge(P.sS.h, bS + CC)
            for tc in range(TC):
                pm = S.psa if tc % 2 == 0 else S.psb
                if tc >= 2:
                    t.wait_ge(P.sS.h, bS + CC + tc - 1)
                for sl in range(0, C, 512):
                    sw = min(512, C - sl)
                    for ic in range(CC):
                        inst = t.matmul(pm[:, sl:sl + sw],
                                 _xn_ap(P)[:, ic * T + tc * 128:ic * T + (tc + 1) * 128],
                                 wvt[:, ic * C + sl:ic * C + sl + sw],
                                 start=(ic == 0), stop=False)
                    inst = t.matmul(pm[:, sl:sl + sw], S.ones1[:],
                             S.bvr[0:1, sl:sl + sw], start=False, stop=True)
                inst.then_inc(P.sT.h, 1)
                P.sT.v += 1

        @b.scalar
        def _(s):
            def vload(oc):
                if oc >= CC:
                    return
                s.dma_start(out=wld1(oc % 2),
                            in_=D.wv[oc * 128:(oc + 1) * 128, :]
                            ).then_inc(vlsem[oc % 2].h, 16)
                vlsem[oc % 2].v += 16
            if CC > 1:
                vload(1)
            for oc in range(CC):
                s.wait_ge(P.sT.h, bT + oc + 1)
                pt = (S.psc if oc % 2 == 0 else S.psd)[:].bitcast(BF16)[:, 0:C]
                inst = None
                for j in range(CC):
                    inst = s.activation(wvt[:, j * C + oc * 128:j * C + (oc + 1) * 128],
                                        pt[:, j * 128:(j + 1) * 128], AF.Identity)
                inst.then_inc(P.sS.h, 1)
                P.sS.v += 1
                if oc % 2 == 1:
                    vload(oc + 2)
            for tc in range(TC):
                s.wait_ge(P.sT.h, bT + CC + tc + 1)
                pm = S.psa if tc % 2 == 0 else S.psb
                out = S.v[:, tc * H * 65:(tc + 1) * H * 65].rearrange(
                    "p (h e) -> p h e", e=65)[:, :, 0:64]
                s.activation(out, pm[:, 0:C].rearrange("p (h d) -> p h d", d=64),
                             AF.Identity).then_inc(P.sS.h, 1)
                P.sS.v += 1


def _b4_norms(P):
    nc, S = P.nc, P.S
    TC, CC, C, T, H = P.TC, P.CC, P.C, P.T, P.H
    sq = lambda p: P.view(S.pt if p == 0 else S.aot, 0, 128, T, BF16)
    for src, acc in ((S.q, S.nq2), (S.k, S.nk2)):
        bV, bT, bS = P.sV.v, P.sT.v, P.sS.v
        nfill = (T + 511) // 512
        s_sq = lambda oc: bS + oc + 1
        v_red = lambda oc, fi: bV + oc * nfill + fi + 1
        t_fill = lambda oc, fi: bT + oc * nfill + fi + 1
        with nc.Block() as b:
            @b.scalar
            def _(s):
                for oc in range(CC):
                    if oc >= 2:
                        s.wait_ge(P.sT.h, t_fill(oc - 2, nfill - 1))
                    s.activation(sq(oc % 2), src[:, oc * T:(oc + 1) * T],
                                 AF.Square)
                    s.drain().then_inc(P.sS.h, 1)
                    P.sS.v += 1

            @b.tensor
            def _(t):
                for oc in range(CC):
                    t.wait_ge(P.sS.h, s_sq(oc))
                    for fi, t0 in enumerate(range(0, T, 512)):
                        sw = min(512, T - t0)
                        g = oc * nfill + fi
                        pm = S.psc if g % 2 == 0 else S.psd
                        if g >= 2:
                            po, pf = divmod(g - 2, nfill)
                            t.wait_ge(P.sV.h, v_red(po, pf))
                        inst = t.matmul(pm[0:2, 0:sw], S.oblk[:],
                                        sq(oc % 2)[:, t0:t0 + sw],
                                        start=True, stop=True)
                        inst.then_inc(P.sT.h, 1)
                        P.sT.v += 1

            @b.vector
            def _(v):
                for oc in range(CC):
                    for fi in range(nfill):
                        v.wait_ge(P.sT.h, t_fill(oc, fi))
                        pm = S.psc if (oc * nfill + fi) % 2 == 0 else S.psd
                        v.reduce_max(acc[:, CC:CC + 1],
                                     pm[0:2, 0:min(512, T - fi * 512)], axis=AX)
                        v.drain()
                        v.tensor_max(acc[:, oc:oc + 1], acc[:, oc:oc + 1],
                                     acc[:, CC:CC + 1])
                        v.drain().then_inc(P.sV.h, 1)
                        P.sV.v += 1
                    if src is S.q:
                        fwx = P.view(S.ao, 0, 128, T, F32)
                        slx = _xn_ap(P)[:, oc * T:(oc + 1) * T]
                        v.tensor_scalar(fwx, slx, S.sml[:, R1:R1 + 1], RC,
                                        OP.mult, OP.add)
                        v.drain()
                        v.tensor_scalar(slx, fwx, RC, S.sml[:, S1:S1 + 1],
                                        OP.subtract, OP.mult)
                        v.drain()

    with nc.Block() as b:
        @b.vector
        def _(v):
            v.tensor_add(S.nm2[:], S.nq2[:, 0:CC], S.nk2[:, 0:CC])
            v.drain()
            v.tensor_scalar_mul(S.nm2[:], S.nm2[:], -0.5)
            v.drain().then_inc(P.sV.h, 1)
            P.sV.v += 1

        @b.scalar
        def _(s):
            s.wait_ge(P.sV.h, P.sV.v)
            s.dma_start(out=P.D.s_nm.rearrange("b a -> a b"),
                        in_=S.nm2[:],
                        allow_slow_non_contiguous=True).then_inc(P.sU.h, 16)
            P.sU.v += 16
            s.wait_ge(P.sU.h, P.sU.v)
            s.dma_start(out=S.negm[:, 0:H],
                        in_=P.D.s_nm.rearrange("b a -> (b a)").unsqueeze(0)[
                            0:1, 0:H].broadcast_to([128, H])).then_inc(P.sU.h, 16)
            P.sU.v += 16
            s.wait_ge(P.sU.h, P.sU.v)


def _b6_upgate(P):
    nc, S, D, view = P.nc, P.S, P.D, P.view
    CC, C, T, IC = P.CC, P.C, P.T, P.IC
    nf = max(1, T // 1024)
    wld = lambda p: view(S.ao, p * 4 * C, 128, C, F32)
    wtmp = view(S.ao, 8 * C, 128, C, F32)
    wtmp2 = view(S.ao, 12 * C, 128, C, F32)
    wqc = lambda p: view(S.ao, 16 * C + p * 2 * C, 128, C, BF16)
    wt = lambda p: view(S.ao, 20 * C + p * 2 * C, 128, C, BF16)
    cbuf = lambda p: view(S.ao, 24 * C + p * 2 * T, 128, T, BF16)

    for wdram, spill, mcol in ((D.wg, D.s_gate, GT_MAX), (D.wu, D.s_up, UP_MAX)):
        bV, bS, bT = P.sV.v, P.sS.v, P.sT.v
        bLp = [P.sLa.v, P.sLb.v]
        bGp = [P.sGa.v, P.sGb.v]
        v_quant = lambda oc: bV + 2 * oc + 1
        v_red = lambda oc: bV + 2 * oc + 2
        t_transp = lambda oc: bT + oc * (1 + nf) + 1
        t_fill = lambda oc, fi: bT + oc * (1 + nf) + 1 + fi + 1
        s_wt = lambda oc: bS + oc * (1 + nf) + 1
        s_fill = lambda oc, fi: bS + oc * (1 + nf) + 1 + fi + 1
        g_spill = lambda oc: (P.sGp[oc % 2].h, bGp[oc % 2] + 16 * (oc // 2 + 1))

        with nc.Block() as b:
            @b.sync
            def _(s):
                for oc in range(IC):
                    if oc >= 2:
                        s.wait_ge(P.sV.h, v_quant(oc - 2))
                    s.dma_start(out=wld(oc % 2),
                                in_=wdram[oc * 128:(oc + 1) * 128, :]).then_inc(P.sLp[oc % 2].h, 16)
                    P.sLp[oc % 2].v += 16

            @b.vector
            def _(v):
                for oc in range(IC):
                    v.wait_ge(P.sLp[oc % 2].h, bLp[oc % 2] + 16 * (oc // 2 + 1))
                    if oc >= 2:
                        v.wait_ge(P.sT.h, t_transp(oc - 2))
                    _quant_blk(P, v, wld(oc % 2), wtmp, wtmp2, wqc(oc % 2), C)
                    v.drain().then_inc(P.sV.h, 1)
                    P.sV.v += 1
                    v.wait_ge(P.sS.h, s_fill(oc, nf - 1))
                    v.reduce_max(S.sml[:, TMP0:TMP0 + 1], cbuf(oc % 2), axis=AX,
                                 apply_absolute_value=True)
                    v.drain()
                    v.tensor_max(S.sml[:, mcol:mcol + 1], S.sml[:, mcol:mcol + 1],
                                 S.sml[:, TMP0:TMP0 + 1])
                    v.drain().then_inc(P.sV.h, 1)
                    P.sV.v += 1

            @b.tensor
            def _(t):
                for oc in range(IC):
                    t.wait_ge(P.sV.h, v_quant(oc))
                    if oc >= 2:
                        t.wait_ge(P.sS.h, s_wt(oc - 2))
                    ps = (S.psc if oc % 2 == 0 else S.psd)[:].bitcast(BF16)
                    for ic in range(CC):
                        inst = t.transpose(ps[:, ic * 128:(ic + 1) * 128],
                                    wqc(oc % 2)[:, ic * 128:(ic + 1) * 128],
                                    S.ident_b[:])
                    inst.then_inc(P.sT.h, 1)
                    P.sT.v += 1
                    t.wait_ge(P.sS.h, s_wt(oc))
                    for fi, t0 in enumerate(range(0, T, 1024)):
                        fw = min(1024, T - t0)
                        pm = S.psa if (oc * nf + fi) % 2 == 0 else S.psb
                        g = oc * nf + fi
                        if g >= 2:
                            po, pf = divmod(g - 2, nf)
                            t.wait_ge(P.sS.h, s_fill(po, pf))
                        for sl in range(0, fw, 512):
                            sw = min(512, fw - sl)
                            for ic in range(CC):
                                inst = t.matmul(pm[:, sl:sl + sw],
                                         wt(oc % 2)[:, ic * 128:(ic + 1) * 128],
                                         _xn_ap(P)[:, ic * T + t0 + sl:ic * T + t0 + sl + sw],
                                         start=(ic == 0), stop=(ic == CC - 1))
                        inst.then_inc(P.sT.h, 1)
                        P.sT.v += 1

            @b.scalar
            def _(s):
                for oc in range(IC):
                    s.wait_ge(P.sT.h, t_transp(oc))
                    pt = (S.psc if oc % 2 == 0 else S.psd)[:].bitcast(BF16)[:, 0:C]
                    s.activation(wt(oc % 2), pt, AF.Identity).then_inc(P.sS.h, 1)
                    P.sS.v += 1
                    if oc >= 2:
                        s.wait_ge(*g_spill(oc - 2))
                        s.wait_ge(P.sV.h, v_red(oc - 2))
                    for fi, t0 in enumerate(range(0, T, 1024)):
                        fw = min(1024, T - t0)
                        pm = S.psa if (oc * nf + fi) % 2 == 0 else S.psb
                        s.wait_ge(P.sT.h, t_fill(oc, fi))
                        s.activation(cbuf(oc % 2)[:, t0:t0 + fw], pm[:, 0:fw],
                                     AF.Identity).then_inc(P.sS.h, 1)
                        P.sS.v += 1

            @b.gpsimd
            def _(g):
                for oc in range(IC):
                    if oc >= 2:
                        g.wait_ge(*g_spill(oc - 2))
                    g.wait_ge(P.sS.h, s_fill(oc, nf - 1))
                    g.dma_start(out=spill[oc:oc + 1, :, :].squeeze(0), in_=cbuf(oc % 2)
                                ).then_inc(P.sGp[oc % 2].h, 16)
                    P.sGp[oc % 2].v += 16
                g.wait_ge(P.sGa.h, P.sGa.v)
                g.wait_ge(P.sGb.h, P.sGb.v)


def _b6b_wprep(P):
    nc, S, D, view = P.nc, P.S, P.D, P.view
    CC, C, I, IC = P.CC, P.C, P.I, P.IC
    wld = lambda p: view(S.ao, p * 4 * C, 128, C, F32)
    wtmp = view(S.ao, 8 * C, 128, C, F32)
    wqc = lambda p: view(S.ao, 12 * C + p * 2 * C, 128, C, BF16)
    wt = lambda p: view(S.ao, 16 * C + p * 2 * C, 128, C, BF16)
    bV, bS, bT = P.sV.v, P.sS.v, P.sT.v
    bLp = [P.sLa.v, P.sLb.v]
    bGp = [P.sGa.v, P.sGb.v]
    g_sp = lambda oc: (P.sGp[oc % 2].h, bGp[oc % 2] + 16 * (oc // 2 + 1))
    with nc.Block() as b:
        @b.sync
        def _(s):
            for oc in range(CC):
                if oc >= 2:
                    s.wait_ge(P.sV.h, bV + oc - 1)
                s.dma_start(out=wld(oc % 2),
                            in_=D.wp[oc * 128:(oc + 1) * 128, :]).then_inc(P.sLp[oc % 2].h, 16)
                P.sLp[oc % 2].v += 16

        @b.vector
        def _(v):
            for oc in range(CC):
                v.wait_ge(P.sLp[oc % 2].h, bLp[oc % 2] + 16 * (oc // 2 + 1))
                if oc >= 2:
                    v.wait_ge(P.sT.h, bT + oc - 1)
                _quant_row(P, v, wld(oc % 2), wtmp, wqc(oc % 2), None)
                v.drain().then_inc(P.sV.h, 1)
                P.sV.v += 1

        @b.tensor
        def _(t):
            for oc in range(CC):
                t.wait_ge(P.sV.h, bV + oc + 1)
                if oc >= 2:
                    t.wait_ge(P.sS.h, bS + oc - 1)
                ps = (S.psc if oc % 2 == 0 else S.psd)[:].bitcast(BF16)
                for ic in range(CC):
                    inst = t.transpose(ps[:, ic * 128:(ic + 1) * 128],
                                wqc(oc % 2)[:, ic * 128:(ic + 1) * 128],
                                S.ident_b[:])
                inst.then_inc(P.sT.h, 1)
                P.sT.v += 1

        @b.scalar
        def _(s):
            for oc in range(CC):
                s.wait_ge(P.sT.h, bT + oc + 1)
                if oc >= 2:
                    s.wait_ge(*g_sp(oc - 2))
                pt = (S.psc if oc % 2 == 0 else S.psd)[:].bitcast(BF16)[:, 0:C]
                s.activation(wt(oc % 2), pt, AF.Identity).then_inc(P.sS.h, 1)
                P.sS.v += 1

        @b.gpsimd
        def _(g):
            for oc in range(CC):
                if oc >= 2:
                    g.wait_ge(*g_sp(oc - 2))
                g.wait_ge(P.sS.h, bS + oc + 1)
                g.dma_start(out=D.s_wpt[:, :, oc * 128:(oc + 1) * 128].transpose([1, 0, 2]),
                            in_=wt(oc % 2).rearrange("p (i m) -> p i m", m=128)
                            ).then_inc(P.sGp[oc % 2].h, 16)
                P.sGp[oc % 2].v += 16
            g.wait_ge(P.sGa.h, P.sGa.v)
            g.wait_ge(P.sGb.h, P.sGb.v)

    IH = I // 2
    wld3 = view(S.ao, 0, 128, IH, F32)
    wtmp3 = view(S.ao, 2 * I, 128, IH, F32)
    wtmp3b = view(S.ao, 4 * I, 128, IH, F32)
    wqc3 = view(S.ao, 6 * I, 128, IH, BF16)
    tstage = view(S.ao, 7 * I, 128, IC * 128, BF16)
    bL, bV, bS, bT, bG = P.sL.v, P.sV.v, P.sS.v, P.sT.v, P.sG.v
    nh = IH // 128
    with nc.Block() as b:
        @b.sync
        def _(s):
            for oc in range(CC):
                for hv in range(2):
                    g = oc * 2 + hv
                    if g >= 1:
                        s.wait_ge(P.sV.h, bV + g)
                    s.dma_start(out=wld3,
                                in_=D.wd[oc * 128:(oc + 1) * 128, hv * IH:(hv + 1) * IH]
                                ).then_inc(P.sL.h, 16)
            P.sL.v += 32 * CC

        @b.vector
        def _(v):
            for oc in range(CC):
                for hv in range(2):
                    g = oc * 2 + hv
                    v.wait_ge(P.sL.h, bL + 16 * (g + 1))
                    if g >= 1:
                        v.wait_ge(P.sT.h, bT + g)
                    _quant_blk(P, v, wld3, wtmp3, wtmp3b, wqc3, IH)
                    v.drain().then_inc(P.sV.h, 1)
                    P.sV.v += 1

        @b.tensor
        def _(t):
            for oc in range(CC):
                for hv in range(2):
                    g = oc * 2 + hv
                    t.wait_ge(P.sV.h, bV + g + 1)
                    if g >= 2:
                        t.wait_ge(P.sS.h, bS + g - 1)
                    ps = (S.psc if g % 2 == 0 else S.psd)[:].bitcast(BF16)
                    for j in range(nh):
                        inst = t.transpose(ps[:, j * 128:(j + 1) * 128],
                                    wqc3[:, j * 128:(j + 1) * 128],
                                    S.ident_b[:])
                    inst.then_inc(P.sT.h, 1)
                    P.sT.v += 1

        @b.scalar
        def _(s):
            for oc in range(CC):
                for hv in range(2):
                    g = oc * 2 + hv
                    s.wait_ge(P.sT.h, bT + g + 1)
                    if g >= 2:
                        s.wait_ge(P.sG.h, bG + 16 * oc)
                    ps = (S.psc if g % 2 == 0 else S.psd)[:].bitcast(BF16)[:, 0:nh * 128]
                    s.activation(tstage[:, hv * IH:hv * IH + nh * 128], ps,
                                 AF.Identity).then_inc(P.sS.h, 1)
                    P.sS.v += 1

        @b.gpsimd
        def _(g):
            for oc in range(CC):
                if oc >= 1:
                    g.wait_ge(P.sG.h, bG + 16 * oc)
                g.wait_ge(P.sS.h, bS + 2 * (oc + 1))
                out = D.s_wdt[:, :, oc:oc + 1, :].squeeze(2).transpose([1, 0, 2])
                g.dma_start(out=out, in_=tstage.rearrange("p (i j) -> p i j", j=128),
                            allow_slow_non_contiguous=True).then_inc(P.sG.h, 16)
                P.sG.v += 16
            g.wait_ge(P.sG.h, P.sG.v)

def _b7_attention(P):
    nc, S = P.nc, P.S
    TC, CC, C, T, H = P.TC, P.CC, P.C, P.T, P.H
    th = T // 2
    tph = th // 128

    ev_t, ev_s, ev_v = [], [], []
    for h in range(H):
        for f in range(2):
            for j in range(TC):
                ev_t.append(("sc", h, f, j))
                if j >= 1:
                    ev_t.append(("av", h, f, j - 1))
            ev_t.append(("av", h, f, TC - 1))
            ev_s.extend(("exp", h, f, j) for j in range(TC))
            ev_s.append(("xav", h, f))
        for tau in range(TC):
            ev_t.append(("t1", h, tau))
        for bi in range((TC + 3) // 4):
            ev_v.append(("dvb", h, bi))
        if h % 2 == 1:
            ev_t.append(("t2", h))
            ev_v.append(("aoev", h))
    bT, bS, bV = P.sT.v, P.sS.v, P.sV.v
    tick_t = {e: bT + i + 1 for i, e in enumerate(ev_t)}
    tick_s = {e: bS + i + 1 for i, e in enumerate(ev_s)}
    tick_v = {e: bV + i + 1 for i, e in enumerate(ev_v)}

    def prev_hf(h, f):
        if f == 1:
            return (h, 0)
        return (h - 1, 1) if h > 0 else None

    with nc.Block() as b:
        @b.tensor
        def _(t):
            for h in range(H):
                pc, hp = h // 2, h % 2
                for f in range(2):
                    prev = prev_hf(h, f)

                    def scores(j):
                        pm = S.psa if j % 2 == 0 else S.psb
                        if j >= 2:
                            t.wait_ge(P.sS.h, tick_s[("exp", h, f, j - 2)])
                        elif prev is not None:
                            t.wait_ge(P.sS.h, tick_s[("exp", prev[0], prev[1], TC - 1)])
                        inst = None
                        for sl in range(0, th, 512):
                            sw = min(512, th - sl)
                            inst = t.matmul(pm[:, sl:sl + sw],
                                     S.k[hp * 64:hp * 64 + 64,
                                         pc * T + j * 128:pc * T + (j + 1) * 128],
                                     S.q[hp * 64:hp * 64 + 64,
                                         pc * T + f * th + sl:pc * T + f * th + sl + sw],
                                     start=True, stop=True,
                                     tile_position=(hp * 64, 0))
                        inst.then_inc(P.sT.h, 1)
                        P.sT.v += 1

                    def av(j):
                        t.wait_ge(P.sS.h, tick_s[("exp", h, f, j)])
                        if j == 0 and prev is not None:
                            t.wait_ge(P.sV.h, tick_v[("xav", prev[0], prev[1])])
                        vj = S.v[:, j * H * 65 + h * 65:j * H * 65 + h * 65 + 65]
                        ptb = S.pt[:, (j % 2) * 1024:(j % 2) * 1024 + th]
                        inst = None
                        for sl in range(0, th, 512):
                            sw = min(512, th - sl)
                            inst = t.matmul(S.psd[0:65, sl:sl + sw], vj, ptb[:, sl:sl + sw],
                                     start=(j == 0), stop=(j == TC - 1))
                        inst.then_inc(P.sT.h, 1)
                        P.sT.v += 1

                    for j in range(TC):
                        scores(j)
                        if j >= 1:
                            av(j - 1)
                    av(TC - 1)
                for tau in range(TC):
                    f2, loc = divmod(tau, tph)
                    t.wait_ge(P.sV.h, tick_v[("xav", h, f2)])
                    if tau >= 2:
                        t.wait_ge(P.sV.h, tick_v[("dv", h, tau - 2)])
                    elif h > 0:
                        if h % 2 == 1:
                            t.wait_ge(P.sV.h, tick_v[("dv", h - 1, TC - 1)])
                        else:
                            t.wait_ge(P.sV.h, tick_v[("aoev", h - 1)])
                    inst = t.transpose(S.psc[:, (tau % 2) * 512:(tau % 2) * 512 + 65],
                                S.sav[0:65, f2 * 1024 + loc * 128:f2 * 1024 + (loc + 1) * 128],
                                S.ident_f[0:65, 0:65])
                    inst.then_inc(P.sT.h, 1)
                    P.sT.v += 1
                if h % 2 == 1:
                    t.wait_ge(P.sV.h, tick_v[("dv", h, TC - 1)])
                    psb2 = S.psc[:].bitcast(BF16)
                    inst = None
                    for tau in range(TC):
                        inst = t.transpose(psb2[:, tau * 128:(tau + 1) * 128],
                                    S.aot[:, tau * 128:(tau + 1) * 128],
                                    S.ident_b[:])
                    inst.then_inc(P.sT.h, 1)
                    P.sT.v += 1

        @b.scalar
        def _(s):
            for h in range(H):
                for f in range(2):
                    for j in range(TC):
                        pm = S.psa if j % 2 == 0 else S.psb
                        s.wait_ge(P.sT.h, tick_t[("sc", h, f, j)])
                        if j >= 2:
                            s.wait_ge(P.sT.h, tick_t[("av", h, f, j - 2)])
                        ptb = S.pt[:, (j % 2) * 1024:(j % 2) * 1024 + th]
                        s.activation(ptb, pm[:, 0:th], AF.Exp,
                                     bias=S.negm[:, h:h + 1]).then_inc(P.sS.h, 1)
                        P.sS.v += 1
                    s.wait_ge(P.sT.h, tick_t[("av", h, f, TC - 1)])
                    s.activation(S.sav[0:65, f * 1024:f * 1024 + th],
                                 S.psd[0:65, 0:th], AF.Identity)
                    s.drain().then_inc(P.sS.h, 1)
                    P.sS.v += 1

        @b.vector
        def _(v):
            for h in range(H):
                hp = h % 2
                for f in range(2):
                    v.wait_ge(P.sT.h, tick_t[("av", h, f, TC - 1)])
                    v.tensor_copy(S.sav[0:65, f * 1024:f * 1024 + th],
                                  S.psd[0:65, 0:th])
                    v.drain().then_inc(P.sV.h, 1)
                    P.sV.v += 1
                for tau in range(TC):
                    v.wait_ge(P.sT.h, tick_t[("t1", h, tau)])
                    slot = S.psc[:, (tau % 2) * 512:(tau % 2) * 512 + 65]
                    v.reciprocal(S.sig[:, tau:tau + 1], slot[:, 64:65])
                    v.drain()
                    v.tensor_scalar(
                        S.aot[:, tau * 128 + hp * 64:tau * 128 + hp * 64 + 64],
                        slot[:, 0:64], S.sig[:, tau:tau + 1], None, OP.mult)
                    v.drain().then_inc(P.sV.h, 1)
                    P.sV.v += 1
                if h % 2 == 1:
                    v.wait_ge(P.sT.h, tick_t[("t2", h)])
                    psb2 = S.psc[:].bitcast(BF16)
                    out = S.ao[:, (h // 2) * T:(h // 2) * T + TC * 128]
                    for c0 in range(0, TC * 128, 1024):
                        cw = min(1024, TC * 128 - c0)
                        v.tensor_copy(out[:, c0:c0 + cw], psb2[:, c0:c0 + cw])
                    v.drain().then_inc(P.sV.h, 1)
                    P.sV.v += 1


def _ffo_ap(P, tc):
    half = P.TC // 2
    if tc < half:
        return P.view(P.S.q, tc * P.C * 4, 128, P.C, F32)
    return P.view(P.S.k, (tc - half) * P.C * 4, 128, P.C, F32)


def _b8_chain1(P):
    nc, S, D, view = P.nc, P.S, P.D, P.view
    T, IC, TC, CC, C = P.T, P.IC, P.TC, P.CC, P.C
    HT = T // 2
    ar_bytes = S.arena.shape[1] * 2
    pt_bytes = S.pt.shape[1] * 2
    aot_bytes = S.aot.shape[1] * 2
    FD = 4 if (6 * T <= ar_bytes and 2 * T <= pt_bytes
               and 2 * T <= aot_bytes) else 2
    ID1 = FD
    if FD == 4:
        in1 = lambda p: view(S.v, p * 2 * T, 128, T, BF16)
        fw = lambda q: view(S.v, 8 * T + q * 2 * T, 128, HT, F32)
        fw2 = lambda q: (view(S.arena, q * 2 * T, 128, HT, F32) if q < 3 else
                         view(S.pt, 0, 128, HT, F32))
        fw3 = view(S.aot, 0, 128, HT, F32)
    else:
        in1 = lambda p: view(S.v, p * 2 * T, 128, T, BF16)
        fw = lambda q: view(S.v, 4 * T + q * 2 * T, 128, HT, F32)
        fw2 = lambda q: view(S.v, 8 * T + q * 2 * T, 128, HT, F32)
        fw3 = view(S.v, 12 * T, 128, HT, F32)
    wpt = view(S.xn, 0, 128, CC * C, BF16)
    NG = 2 * IC
    bV, bS, bT, bL = P.sV.v, P.sS.v, P.sT.v, P.sL.v
    lsem = [P.sLa, P.sLb, P.sW0, P.sW1][:ID1]
    bLp = [sm.v for sm in lsem]
    in1_ld = lambda ic: (lsem[ic % ID1].h, bLp[ic % ID1] + 16 * (ic // ID1 + 1))
    bGp = [P.sGa.v, P.sGb.v]
    g_sp = lambda ic: (P.sGp[ic % 2].h, bGp[ic % 2] + 16 * (ic // 2 + 1))

    v_tl = lambda g: bV + g + 1
    sev = []
    ppq = list(range(TC)) if not DBG["no_pproj"] else []
    for g in range(NG):
        sev += [("p1", g), ("p2", g), ("sig", g)]
        if g % 3 == 2 and ppq:
            sev.append(("ppe", ppq.pop(0)))
    while ppq:
        sev.append(("ppe", ppq.pop(0)))
    tick_sev = {e: bS + i + 1 for i, e in enumerate(sev)}

    with nc.Block() as b:
        @b.sync
        def _(s):
            def gld(ic):
                if ic >= ID1:
                    s.wait_ge(*g_sp(ic - ID1))
                s.dma_start(out=in1(ic % ID1),
                            in_=D.s_gate[ic:ic + 1, :, :].squeeze(0)
                            ).then_inc(lsem[ic % ID1].h, 16)
                lsem[ic % ID1].v += 16
            for ic in range(min(ID1, IC)):
                gld(ic)
            if not DBG["no_pproj"]:
                for oc in range(CC):
                    s.dma_start(out=wpt[:, oc * C:(oc + 1) * C],
                                in_=D.s_wpt[oc:oc + 1, :, :].squeeze(0)
                                ).then_inc(P.sL.h, 16)
                P.sL.v += 16 * CC
            for ic in range(ID1, IC):
                gld(ic)

        @b.tensor
        def _(t):
            if DBG["no_pproj"]:
                return
            t.wait_ge(P.sL.h, bL + 16 * CC)
            for tc in range(TC):
                pm = S.psa if tc % 2 == 0 else S.psb
                if tc >= 2:
                    t.wait_ge(P.sS.h, tick_sev[("ppe", tc - 2)])
                inst = None
                for sl in range(0, C, 512):
                    sw = min(512, C - sl)
                    for pc in range(CC):
                        inst = t.matmul(pm[:, sl:sl + sw],
                                        S.ao[:, pc * T + tc * 128:pc * T + (tc + 1) * 128],
                                        wpt[:, pc * C + sl:pc * C + sl + sw],
                                        start=(pc == 0), stop=False)
                    inst = t.matmul(pm[:, sl:sl + sw], S.ones1[:],
                                    S.bpr[0:1, sl:sl + sw], start=False, stop=True)
                inst.then_inc(P.sT.h, 1)
                P.sT.v += 1

        @b.vector
        def _(v):
            for g in range(NG):
                ic, hv = divmod(g, 2)
                v.wait_ge(P.sS.h, tick_sev[("sig", g)])
                v.tensor_scalar(fw3, fw(g % FD), 65536.0, RC, OP.mult, OP.add)
                v.drain()
                sl = in1(ic % ID1)[:, hv * HT:(hv + 1) * HT]
                v.scalar_tensor_tensor(sl, fw3, RC, fw2(g % FD), OP.subtract, OP.mult)
                v.drain()
                v.reduce_max(S.sml[:, TMP0:TMP0 + 1], sl, axis=AX,
                             apply_absolute_value=True)
                v.drain()
                v.tensor_max(S.sml[:, PR_MAX:PR_MAX + 1], S.sml[:, PR_MAX:PR_MAX + 1],
                             S.sml[:, TMP0:TMP0 + 1])
                v.drain().then_inc(P.sV.h, 1)
                P.sV.v += 1
            v.tensor_scalar_mul(S.sml[:, PR_TRUE:PR_TRUE + 1],
                                S.sml[:, PR_MAX:PR_MAX + 1], INV16)
            v.drain().then_inc(P.sV.h, 1)
            P.sV.v += 1
            if not DBG["standalone_a4"]:
                v.wait_ge(P.sX.h, P.sX.v + 16)
                v.reduce_max(S.row[0:1, 0:1], S.row[0:1, 128:256], axis=AX)
                v.drain().then_inc(P.sV.h, 1)
                P.sV.v += 1

        @b.scalar
        def _(s):
            for e in sev:
                if e[0] == "p1":
                    g = e[1]
                    ic, hv = divmod(g, 2)
                    if hv == 0:
                        s.wait_ge(*in1_ld(ic))
                    if g >= FD:
                        s.wait_ge(P.sV.h, v_tl(g - FD))
                    sl = in1(ic % ID1)[:, hv * HT:(hv + 1) * HT]
                    s.activation(fw(g % FD), sl, AF.Identity,
                                 bias=S.c_rc[:], scale=S.sml[:, R3:R3 + 1])
                    s.drain().then_inc(P.sS.h, 1)
                    P.sS.v += 1
                elif e[0] == "p2":
                    g = e[1]
                    s.activation(fw2(g % FD), fw(g % FD), AF.Identity,
                                 bias=S.sml[:, NRC3:NRC3 + 1],
                                 scale=S.sml[:, S3:S3 + 1])
                    s.drain().then_inc(P.sS.h, 1)
                    P.sS.v += 1
                elif e[0] == "sig":
                    g = e[1]
                    s.activation(fw(g % FD), fw2(g % FD), AF.Sigmoid)
                    s.drain().then_inc(P.sS.h, 1)
                    P.sS.v += 1
                else:
                    tc = e[1]
                    pm = S.psa if tc % 2 == 0 else S.psb
                    s.wait_ge(P.sT.h, bT + tc + 1)
                    s.activation(_ffo_ap(P, tc), pm[:, 0:C],
                                 AF.Identity).then_inc(P.sS.h, 1)
                    P.sS.v += 1

        @b.gpsimd
        def _(g):
            for ic in range(IC):
                if ic >= 2:
                    g.wait_ge(*g_sp(ic - 2))
                g.wait_ge(P.sV.h, v_tl(2 * ic + 1))
                g.dma_start(out=D.s_prod[ic:ic + 1, :, :].squeeze(0),
                            in_=in1(ic % ID1)).then_inc(P.sGp[ic % 2].h, 16)
                P.sGp[ic % 2].v += 16
            if DBG["standalone_a4"]:
                g.wait_ge(P.sGa.h, P.sGa.v)
                g.wait_ge(P.sGb.h, P.sGb.v)
                return
            g.wait_ge(P.sV.h, bV + NG + 1)
            g.dma_start(out=S.row[0:1, 128:256],
                        in_=S.sml[:, PR_TRUE:PR_TRUE + 1]).then_inc(P.sX.h, 16)
            P.sX.v += 16
            g.wait_ge(P.sV.h, bV + NG + 2)
            cin, cout = D.cc[2]
            g.dma_start(out=cin[:], in_=S.row[0:1, 0:128]).then_inc(P.sX.h, 16)
            P.sX.v += 16
            g.wait_ge(P.sX.h, P.sX.v)
            g.collective_compute(
                "AllReduce", OP.max, replica_groups=[list(range(P.NCORES))],
                ins=[cin[:]], outs=[cout[:]]).then_inc(P.sC.h, 1)
            P.sC.v += 1
            g.wait_ge(P.sC.h, P.sC.v)
            g.dma_start(out=S.sml[:, PR_TRUE:PR_TRUE + 1],
                        in_=cout[0:1, 0:1].broadcast_to([128, 1])).then_inc(P.sX.h, 16)
            P.sX.v += 16
            g.wait_ge(P.sX.h, P.sX.v)
            g.wait_ge(P.sGa.h, P.sGa.v)
            g.wait_ge(P.sGb.h, P.sGb.v)


def _wdtr_ap(P, ic):
    C = P.C
    if ic < P.wdt_nao:
        return P.view(P.S.ao, ic * 2 * C, 128, C, BF16)
    return P.view(P.S.xn, (ic - P.wdt_nao) * 2 * C, 128, C, BF16)


def _b9_chain2(P):
    nc, S, D, view = P.nc, P.S, P.D, P.view
    T, IC = P.T, P.IC
    C = P.C
    HT = T // 2
    NG = 2 * IC
    xn_bytes = S.xn.shape[1] * 2
    ar_bytes = S.arena.shape[1] * 2
    pt_bytes = S.pt.shape[1] * 2
    DEEP = (2 * P.CC * C + 8 * T <= xn_bytes and 6 * T <= ar_bytes
            and 2 * T <= pt_bytes)
    FD = 4 if DEEP else 2
    IN2D = 4 if DEEP else 2
    ID1 = 4 if DEEP else 2
    if DEEP:
        in1 = lambda p: view(S.v, p * 2 * T, 128, T, BF16)
        in2 = lambda p: view(S.xn, 2 * P.CC * C + p * 2 * T, 128, T, BF16)
        fwA = lambda q: view(S.v, 8 * T + q * 2 * T, 128, HT, F32)
        fw2 = lambda q: (view(S.arena, q * 2 * T, 128, HT, F32) if q < 3 else
                         view(S.pt, 0, 128, HT, F32))
    else:
        in1 = lambda p: view(S.v, p * 2 * T, 128, T, BF16)
        in2 = lambda p: view(S.v, 4 * T + p * 2 * T, 128, T, BF16)
        fwA = lambda q: view(S.v, 8 * T + q * 2 * T, 128, HT, F32)
        fw2 = lambda q: view(S.v, 12 * T + q * 2 * T, 128, HT, F32)
    bV, bS = P.sV.v, P.sS.v
    lsem = [P.sLa, P.sLb, P.sL, P.sU][:ID1]
    bLp = [sm.v for sm in lsem]
    in1_ld = lambda ic: (lsem[ic % ID1].h, bLp[ic % ID1] + 16 * (ic // ID1 + 1))
    bGp = [P.sGa.v, P.sGb.v]
    v_tl = lambda g: bV + g + 1
    g_sp = lambda ic: (P.sGp[ic % 2].h, bGp[ic % 2] + 32 * (ic // 2 + 1))
    ao_el = S.ao.shape[1]
    P.wdt_nao = min(IC, ao_el // C)
    assert (IC - P.wdt_nao) * 2 * C <= 2 * P.CC * C, (IC, P.wdt_nao)
    P.wdt_ld = g_sp
    wsem2 = [P.sW0, P.sW1, P.sW2, P.sW3][:IN2D]
    bW2 = [sm.v for sm in wsem2]
    in2_ld = lambda ic: (wsem2[ic % IN2D].h, bW2[ic % IN2D] + 16 * (ic // IN2D + 1))

    npre = min(IN2D - 1, IC) if IN2D > 2 else min(2, IC)
    sev = []
    for g in range(NG):
        if g == 0:
            for ic0 in range(npre):
                sev.append(("in2", ic0))
        sev += [("p1", g), ("p2", g)]
        ic_n = g // 2 + npre
        if g % 2 == 1 and ic_n < IC:
            sev.append(("in2", ic_n))
    tick_sev = {}
    tick = bS
    for e in sev:
        if e[0] != "in2":
            tick += 1
            tick_sev[e] = tick

    with nc.Block() as b:
        @b.vector
        def _(v):
            v.tensor_tensor(S.sml[:, 18:19], S.sml[:, S4:S4 + 1],
                            S.sml[:, S2:S2 + 1], OP.mult)
            v.drain()
            v.tensor_scalar_mul(S.sml[:, NRC42:NRC42 + 1], S.sml[:, 18:19], -RC)
            v.drain()

    with nc.Block() as b:
        @b.sync
        def _(s):
            for ic in range(IC):
                if ic >= ID1:
                    s.wait_ge(*g_sp(ic - ID1))
                s.dma_start(out=in1(ic % ID1),
                            in_=D.s_prod[ic:ic + 1, :, :].squeeze(0)
                            ).then_inc(lsem[ic % ID1].h, 16)
                lsem[ic % ID1].v += 16

        @b.scalar
        def _(s):
            for e in sev:
                if e[0] == "p1":
                    g = e[1]
                    ic, hv = divmod(g, 2)
                    if hv == 0:
                        s.wait_ge(*in1_ld(ic))
                    if g >= FD:
                        s.wait_ge(P.sV.h, v_tl(g - FD))
                    sl = in1(ic % ID1)[:, hv * HT:(hv + 1) * HT]
                    s.activation(fwA(g % FD), sl, AF.Identity,
                                 bias=S.c_rc[:], scale=S.sml[:, R4S:R4S + 1])
                    s.drain().then_inc(P.sS.h, 1)
                    P.sS.v += 1
                elif e[0] == "p2":
                    g = e[1]
                    s.activation(fw2(g % FD), fwA(g % FD), AF.Identity,
                                 bias=S.sml[:, NRC42:NRC42 + 1],
                                 scale=S.sml[:, 18:19])
                    s.drain().then_inc(P.sS.h, 1)
                    P.sS.v += 1
                else:
                    ic = e[1]
                    if ic >= IN2D:
                        s.wait_ge(P.sV.h, v_tl(2 * (ic - IN2D) + 1))
                    s.dma_start(out=in2(ic % IN2D),
                                in_=D.s_up[ic:ic + 1, :, :].squeeze(0)
                                ).then_inc(wsem2[ic % IN2D].h, 16)
                    wsem2[ic % IN2D].v += 16

        @b.vector
        def _(v):
            for g in range(NG):
                ic, hv = divmod(g, 2)
                v.wait_ge(P.sS.h, tick_sev[("p2", g)])
                if hv == 0:
                    v.wait_ge(*in2_ld(ic))
                sl2 = in2(ic % IN2D)[:, hv * HT:(hv + 1) * HT]
                v.tensor_scalar(fwA(g % FD), sl2, S.sml[:, R2:R2 + 1], RC,
                                OP.mult, OP.add)
                v.drain()
                sl1 = in1(ic % ID1)[:, hv * HT:(hv + 1) * HT]
                v.scalar_tensor_tensor(sl1, fwA(g % FD), RC, fw2(g % FD),
                                       OP.subtract, OP.mult)
                v.drain()
                v.reduce_max(S.sml[:, TMP0:TMP0 + 1], sl1, axis=AX,
                             apply_absolute_value=True)
                v.drain()
                v.tensor_max(S.sml[:, OR_MAX:OR_MAX + 1], S.sml[:, OR_MAX:OR_MAX + 1],
                             S.sml[:, TMP0:TMP0 + 1])
                v.drain().then_inc(P.sV.h, 1)
                P.sV.v += 1
            if not DBG["standalone_a5"]:
                v.wait_ge(P.sY.h, P.sY.v + 16)
                v.reduce_max(S.row[0:1, 0:1], S.row[0:1, 128:256], axis=AX)
                v.drain().then_inc(P.sV.h, 1)
                P.sV.v += 1

        @b.gpsimd
        def _(g):
            for ic in range(IC):
                if ic >= 2:
                    g.wait_ge(*g_sp(ic - 2))
                g.wait_ge(P.sV.h, v_tl(2 * ic + 1))
                g.dma_start(out=D.s_orw[ic:ic + 1, :, :].squeeze(0),
                            in_=in1(ic % ID1)).then_inc(P.sGp[ic % 2].h, 16)
                P.sGp[ic % 2].v += 16
                g.dma_start(out=_wdtr_ap(P, ic),
                            in_=D.s_wdt[ic:ic + 1, :, :, :].squeeze(0).rearrange(
                                "p a b -> p (a b)")).then_inc(P.sGp[ic % 2].h, 16)
                P.sGp[ic % 2].v += 16
            if DBG["standalone_a5"]:
                g.wait_ge(P.sGa.h, P.sGa.v)
                g.wait_ge(P.sGb.h, P.sGb.v)
                return
            g.dma_start(out=S.row[0:1, 128:256],
                        in_=S.sml[:, OR_MAX:OR_MAX + 1]).then_inc(P.sY.h, 16)
            P.sY.v += 16
            g.wait_ge(P.sV.h, bV + NG + 1)
            cin, cout = D.cc[3]
            g.dma_start(out=cin[:], in_=S.row[0:1, 0:128]).then_inc(P.sY.h, 16)
            P.sY.v += 16
            g.wait_ge(P.sY.h, P.sY.v)
            g.collective_compute(
                "AllReduce", OP.max, replica_groups=[list(range(P.NCORES))],
                ins=[cin[:]], outs=[cout[:]]).then_inc(P.sC.h, 1)
            P.sC.v += 1
            g.wait_ge(P.sC.h, P.sC.v)
            g.dma_start(out=S.sml[:, OR_MAX:OR_MAX + 1],
                        in_=cout[0:1, 0:1].broadcast_to([128, 1])).then_inc(P.sY.h, 16)
            P.sY.v += 16
            g.wait_ge(P.sY.h, P.sY.v)
            g.wait_ge(P.sGa.h, P.sGa.v)
            g.wait_ge(P.sGb.h, P.sGb.v)


def _b10_down(P):
    nc, S, D, view = P.nc, P.S, P.D, P.view
    TC, CC, C, T, IC = P.TC, P.CC, P.C, P.T, P.IC
    TGW = min(512, T)
    ntg = T // TGW
    ng = TGW // 128
    NT = ntg * ng
    psl = [S.psa, S.psb, S.psc, S.psd]
    NOIN = 4
    oq = view(S.v, 0, 128, IC * TGW, BF16)
    oin = lambda p: view(S.v, 2 * IC * TGW + p * 2 * TGW, 128, TGW, BF16)
    fw5 = view(S.v, 2 * IC * TGW + 2 * NOIN * TGW, 128, TGW, F32)
    def scr(p):
        if p == 0:
            return view(S.pt, 0, 128, C, F32)
        if p == 1:
            return view(S.aot, 0, 128, C, F32)
        return view(S.arena, (p - 2) * 4 * C, 128, C, F32)
    outb = lambda p: view(S.xn, 2 * CC * C + p * 4 * C, 128, C, F32)
    xt = lambda p: S.xf[:, p * 1024:p * 1024 + C]
    P.ffo_ap = _ffo_ap

    bV, bS, bT = P.sV.v, P.sS.v, P.sT.v
    osem = [P.sLa, P.sLb, P.sL, P.sU]
    bLp = [sm.v for sm in osem]
    bOp = [P.sOa.v, P.sOb.v]

    ev = []
    for tg in range(ntg):
        pend = list(range(ng)) if tg >= 1 else []
        for ic in range(IC):
            ev.append(("q", tg, ic))
            if pend and ic >= 2 and ic % 2 == 0:
                ev.append(("o", tg - 1, pend.pop(0)))
        for tc4 in pend:
            ev.append(("o", tg - 1, tc4))
    for tc4 in range(ng):
        ev.append(("o", ntg - 1, tc4))
    tick = {e: bV + i + 1 for i, e in enumerate(ev)}
    v_oq = lambda tg, ic: tick[("q", tg, ic)]
    v_out = lambda k: tick[("o", k // ng, k % ng)]
    t_grp = lambda tg, ic: bT + tg * IC + ic + 1
    s_ev = lambda tg, tc4: bS + tg * ng + tc4 + 1
    ld_grp = lambda g: (osem[g % NOIN].h, bLp[g % NOIN] + 16 * (g // NOIN + 1))
    x_ld = lambda k: (P.sOp[k % 2].h,
                      bOp[k % 2] + 16 + (0 if k < 2 else 32 * (k // 2)))

    with nc.Block() as b:
        @b.sync
        def _(s):
            for tg in range(ntg):
                for ic in range(IC):
                    g = tg * IC + ic
                    if g >= NOIN:
                        pg, pi = divmod(g - NOIN, IC)
                        s.wait_ge(P.sV.h, v_oq(pg, pi))
                        s.wait_ge(P.sT.h, t_grp(pg, pi))
                    s.dma_start(out=oin(g % NOIN),
                                in_=D.s_orw[ic:ic + 1, :, tg * TGW:(tg + 1) * TGW].squeeze(0)
                                ).then_inc(osem[g % NOIN].h, 16)
                    osem[g % NOIN].v += 16

        @b.vector
        def _(v):
            for e in ev:
                if e[0] == "q":
                    tg, ic = e[1], e[2]
                    g = tg * IC + ic
                    v.wait_ge(*ld_grp(g))
                    if tg > 0:
                        v.wait_ge(P.sT.h, t_grp(tg - 1, ic))
                    v.tensor_scalar(fw5, oin(g % NOIN), S.sml[:, R5:R5 + 1], RC,
                                    OP.mult, OP.add)
                    v.drain()
                    v.tensor_scalar(oq[:, ic * TGW:(ic + 1) * TGW], fw5, RC,
                                    S.sml[:, S5:S5 + 1], OP.subtract, OP.mult)
                    v.drain().then_inc(P.sV.h, 1)
                    P.sV.v += 1
                else:
                    tg, tc4 = e[1], e[2]
                    k = tg * ng + tc4
                    v.wait_ge(P.sS.h, s_ev(tg, tc4))
                    v.wait_ge(*x_ld(k))
                    v.tensor_add(outb(k % 2), scr(k % 4), _ffo_ap(P, k))
                    v.drain()
                    v.tensor_add(outb(k % 2), outb(k % 2), xt(k % 2))
                    v.drain().then_inc(P.sV.h, 1)
                    P.sV.v += 1

        @b.tensor
        def _(t):
            for tg in range(ntg):
                for ic in range(IC):
                    g = tg * IC + ic
                    t.wait_ge(P.sV.h, v_oq(tg, ic))
                    if tg == 0:
                        t.wait_ge(*P.wdt_ld(ic))
                    if ic == 0 and tg > 0:
                        t.wait_ge(P.sS.h, s_ev(tg - 1, ng - 1))
                    for tc4 in range(ng):
                        for sl in range(0, C, 512):
                            sw = min(512, C - sl)
                            inst = t.matmul(psl[tc4][:, sl:sl + sw],
                                     oq[:, ic * TGW + tc4 * 128:ic * TGW + (tc4 + 1) * 128],
                                     _wdtr_ap(P, ic)[:, sl:sl + sw],
                                     start=(ic == 0), stop=(ic == IC - 1))
                    inst.then_inc(P.sT.h, 1)
                    P.sT.v += 1

        @b.scalar
        def _(s):
            for tg in range(ntg):
                for tc4 in range(ng):
                    k = tg * ng + tc4
                    s.wait_ge(P.sT.h, t_grp(tg, IC - 1))
                    if k >= 4:
                        s.wait_ge(P.sV.h, v_out(k - 4))
                    s.activation(scr(k % 4), psl[tc4][:, 0:C],
                                 AF.Identity).then_inc(P.sS.h, 1)
                    P.sS.v += 1

        @b.gpsimd
        def _(g):
            for k in range(min(2, NT)):
                g.dma_start(out=xt(k % 2),
                            in_=D.x[k * 128:(k + 1) * 128, :]).then_inc(P.sOp[k % 2].h, 16)
                P.sOp[k % 2].v += 16
            for k in range(NT):
                g.wait_ge(P.sV.h, v_out(k))
                g.dma_start(out=D.y[k * 128:(k + 1) * 128, :],
                            in_=outb(k % 2)).then_inc(P.sOp[k % 2].h, 16)
                P.sOp[k % 2].v += 16
                if k + 2 < NT:
                    g.dma_start(out=xt(k % 2),
                                in_=D.x[(k + 2) * 128:(k + 3) * 128, :]
                                ).then_inc(P.sOp[k % 2].h, 16)
                    P.sOp[k % 2].v += 16


def _b12_taps(P):
    nc, S = P.nc, P.S
    with nc.Block() as b:
        @b.gpsimd
        def _(g):
            tp = P.tap_t
            if "xnfm" in tp:
                g.dma_start(out=tp["xnfm"], in_=_xn_ap(P)).then_inc(P.sO.h, 16)
                P.sO.v += 16
            if "q" in tp:
                g.dma_start(out=tp["q"], in_=S.q[:, 0:P.CC * P.T]).then_inc(P.sO.h, 16)
                P.sO.v += 16
            if "k" in tp:
                g.dma_start(out=tp["k"], in_=S.k[:, 0:P.CC * P.T]).then_inc(P.sO.h, 16)
                P.sO.v += 16
            if "v" in tp:
                g.dma_start(out=tp["v"], in_=S.v[:, 0:P.TC * P.H * 65]).then_inc(P.sO.h, 16)
                P.sO.v += 16
            if "sml" in tp:
                g.dma_start(out=tp["sml"], in_=S.sml[:]).then_inc(P.sO.h, 16)
                P.sO.v += 16
            if "negm" in tp:
                g.dma_start(out=tp["negm"], in_=S.negm[:]).then_inc(P.sO.h, 16)
                P.sO.v += 16
            if "ao" in tp:
                g.dma_start(out=tp["ao"], in_=S.ao[:, 0:P.H2 * P.T]).then_inc(P.sO.h, 16)
                P.sO.v += 16
            if "aot" in tp:
                g.dma_start(out=tp["aot"], in_=S.aot[:, 0:P.TC * 128]).then_inc(P.sO.h, 16)
                P.sO.v += 16
            if "ffo" in tp:
                for tc in range(P.TC):
                    g.dma_start(out=tp["ffo"][tc * 128:(tc + 1) * 128, :],
                                in_=P.ffo_ap(P, tc)).then_inc(P.sO.h, 16)
                    P.sO.v += 16
            g.wait_ge(P.sO.h, P.sO.v)
            g.wait_ge(P.sOa.h, P.sOa.v)
            g.wait_ge(P.sOb.h, P.sOb.v)


_NC_CACHE = {}


def _get_nc():
    key = "full"
    if key not in _NC_CACHE:
        _NC_CACHE[key] = build_program(CFG_FULL)
    return _NC_CACHE[key]


def kernel(L0_attn_x, L0_attn_wq, L0_attn_bq, L0_attn_wk, L0_attn_bk,
           L0_attn_wv, L0_attn_bv, L0_attn_wp, L0_attn_bp,
           L0_ffn_w_gate, L0_ffn_w_up, L0_ffn_w_down):
    nc = _get_nc()
    n = CFG_FULL["NCORES"]
    x = np.ascontiguousarray(np.asarray(L0_attn_x, dtype=np.float32))
    shared = {
        "wq": np.ascontiguousarray(np.asarray(L0_attn_wq, dtype=np.float32)),
        "wk": np.ascontiguousarray(np.asarray(L0_attn_wk, dtype=np.float32)),
        "wv": np.ascontiguousarray(np.asarray(L0_attn_wv, dtype=np.float32)),
        "wp": np.ascontiguousarray(np.asarray(L0_attn_wp, dtype=np.float32)),
        "bq": np.ascontiguousarray(np.asarray(L0_attn_bq, dtype=np.float32)),
        "bk": np.ascontiguousarray(np.asarray(L0_attn_bk, dtype=np.float32)),
        "bv": np.ascontiguousarray(np.asarray(L0_attn_bv, dtype=np.float32)),
        "bp": np.ascontiguousarray(np.asarray(L0_attn_bp, dtype=np.float32)),
        "wg": np.ascontiguousarray(np.asarray(L0_ffn_w_gate, dtype=np.float32)),
        "wu": np.ascontiguousarray(np.asarray(L0_ffn_w_up, dtype=np.float32)),
        "wd": np.ascontiguousarray(np.asarray(L0_ffn_w_down, dtype=np.float32)),
    }
    in_maps = [dict(shared, x=np.ascontiguousarray(x[i])) for i in range(n)]
    res = run_bass_kernel_spmd(nc, in_maps, core_ids=list(range(n)))
    return np.stack([np.asarray(res.results[i]["y"], dtype=np.float32)
                     for i in range(n)], axis=0)

def _b67_mega(P):
    nc, S, D, view = P.nc, P.S, P.D, P.view
    TC, CC, C, T, H, IC, I = P.TC, P.CC, P.C, P.T, P.H, P.IC, P.I
    th = T // 2
    tph = th // 128
    IH = I // 2
    NF = max(1, T // 512)
    nhd = IH // 128

    wld = view(S.arena, 0, 128, C, F32)
    wtmp = view(S.arena, 4 * C, 128, C, F32)
    wqc = view(S.arena, 8 * C, 128, C, BF16)
    wt = view(S.arena, 10 * C, 128, C, BF16)
    cbuf = lambda p: view(S.xf, p * 2 * T, 128, T, BF16)
    wld3 = view(S.xn, 0, 128, IH, F32)
    wtmp3 = view(S.xn, 4 * IH, 128, IH, F32)
    wqc3 = view(S.xn, 8 * IH, 128, IH, BF16)
    tstage = view(S.xn, 10 * IH, 128, IC * 128, BF16)

    chunks = ([("ug", 0, oc) for oc in range(IC)] +
              [("ug", 1, oc) for oc in range(IC)] +
              [("wp", oc) for oc in range(CC)] +
              [("wd", oc, hv) for oc in range(CC) for hv in range(2)])
    NCH = len(chunks)
    n_ug = 2 * IC
    titems = []
    for k, ch in enumerate(chunks):
        if ch[0] == "ug":
            titems.append(("tr", k))
            for fi in range(NF):
                titems.append(("fl", k, fi))
        elif ch[0] == "wp":
            titems.append(("trp", k))
        else:
            titems.append(("trd", k))
    NSEG = 2 * H
    NTI = len(titems)
    seg_of_item = [min(NSEG - 1, i * NSEG // NTI) for i in range(NTI)]
    SEGC = [[] for _ in range(NSEG)]
    for i, it in enumerate(titems):
        SEGC[seg_of_item[i]].append(it)
    seg_of_chunk = {}
    for i, it in enumerate(titems):
        k = it[1]
        seg_of_chunk.setdefault(k, seg_of_item[i])

    ev_t, ev_s, ev_v = [], [], []
    for h in range(H):
        for f in range(2):
            for j in range(TC):
                ev_t.append(("sc", h, f, j))
                if j >= 1:
                    ev_t.append(("av", h, f, j - 1))
            ev_t.append(("av", h, f, TC - 1))
            ev_s.extend(("exp", h, f, j) for j in range(TC))
            ev_s.append(("xav", h, f))
        for tau in range(TC):
            ev_t.append(("t1", h, tau))
        for bi in range((TC + 3) // 4):
            ev_v.append(("dvb", h, bi))
        if h % 2 == 1:
            ev_t.append(("t2", h))
            ev_v.append(("aoev", h))
    bT, bS, bV = P.sT.v, P.sS.v, P.sV.v
    tick_t = {e: bT + i + 1 for i, e in enumerate(ev_t)}
    tick_s = {e: bS + i + 1 for i, e in enumerate(ev_s)}
    tick_v = {e: bV + i + 1 for i, e in enumerate(ev_v)}

    b6T = P.s6T.v
    tick_6t = {it: b6T + i + 1 for i, it in enumerate(titems)}
    def _vseg_items(s):
        out = []
        if s < 0:
            return [("q", 0)]
        for it in SEGC[s]:
            if it[0] == "tr":
                out.append(("we", it[1]))
                if it[1] + 1 < NCH:
                    out.append(("q", it[1] + 1))
            elif it[0] == "fl":
                out.append(("fe", it[1], it[2]))
                if it[2] == NF - 1:
                    out.append(("red", it[1]))
                    if it[1] == n_ug - 1:
                        out.append(("a23",))
            elif it[0] == "trp":
                out.append(("pe", it[1]))
                if it[1] + 1 < NCH:
                    out.append(("q", it[1] + 1))
            else:
                out.append(("de", it[1]))
                if it[1] + 1 < NCH:
                    out.append(("q", it[1] + 1))
        return out

    vitems = []
    for s in range(-1, NSEG):
        vitems.extend(_vseg_items(s))
    b6V = P.s6V.v
    tick_6v = {it: b6V + i + 1 for i, it in enumerate(vitems)}
    spills = []
    for k, ch in enumerate(chunks):
        if ch[0] == "ug" or ch[0] == "wp" or (ch[0] == "wd" and ch[2] == 1):
            spills.append(k)
    bGp = [P.sGa.v, P.sGb.v]
    spill_idx = {k: i for i, k in enumerate(spills)}
    def spill_tick(i):
        return (P.sGp[i % 2].h, bGp[i % 2] + 16 * (i // 2 + 1))
    bLa = P.sLa.v

    psc_b16 = S.psc[:].bitcast(BF16)
    psc_fill = S.psc[:, 512:1024]
    P._dbg = dict(tick_t=tick_t, tick_s=tick_s, tick_v=tick_v,
                  tick_6t=tick_6t, tick_6v=tick_6v)

    def chunk_dram(ch):
        if ch[0] == "ug":
            return (D.wg if ch[1] == 0 else D.wu), ch[2]
        if ch[0] == "wp":
            return D.wp, ch[1]
        return D.wd, ch[1]

    with nc.Block() as b:
        @b.sync
        def _(s):
            for k, ch in enumerate(chunks):
                if k > 0:
                    s.wait_ge(P.s6V.h, tick_6v[("q", k - 1)])
                if ch[0] == "wd" and ch == ("wd", 0, 0):
                    s.wait_ge(P.s6T.h, tick_6t[("fl", n_ug - 1, NF - 1)])
                wdram, oc = chunk_dram(ch)
                if ch[0] == "wd":
                    hv = ch[2]
                    s.dma_start(out=wld3,
                                in_=wdram[oc * 128:(oc + 1) * 128, hv * IH:(hv + 1) * IH]
                                ).then_inc(P.sLa.h, 16)
                else:
                    s.dma_start(out=wld,
                                in_=wdram[oc * 128:(oc + 1) * 128, :]).then_inc(P.sLa.h, 16)
                P.sLa.v += 16

        @b.tensor
        def _(t):
            flat_items = [(s, it) for s in range(NSEG) for it in SEGC[s]]
            fptr = [0]
            cur_seg = [0]

            def emit_chunk_item(it):
                if it[0] == "tr" or it[0] == "trp":
                    k = it[1]
                    t.wait_ge(P.s6V.h, tick_6v[("q", k)])
                    for kk in range(k - 1, -1, -1):
                        c0 = chunks[kk][0]
                        t.wait_ge(P.s6V.h, tick_6v[
                            ("we", kk) if c0 == "ug" else
                            (("pe", kk) if c0 == "wp" else ("de", kk))])
                        break
                    src = wqc
                    inst = None
                    for ic in range(CC):
                        inst = t.transpose(psc_b16[:, ic * 128:(ic + 1) * 128],
                                           src[:, ic * 128:(ic + 1) * 128], S.ident_b[:])
                    inst.then_inc(P.s6T.h, 1)
                    P.s6T.v += 1
                elif it[0] == "trd":
                    k = it[1]
                    t.wait_ge(P.s6V.h, tick_6v[("q", k)])
                    for kk in range(k - 1, -1, -1):
                        c0 = chunks[kk][0]
                        t.wait_ge(P.s6V.h, tick_6v[
                            ("we", kk) if c0 == "ug" else
                            (("pe", kk) if c0 == "wp" else ("de", kk))])
                        break
                    for kk in range(k - 1, -1, -1):
                        if chunks[kk][0] == "ug":
                            t.wait_ge(P.s6V.h, tick_6v[("fe", kk, NF - 1)])
                            break
                        if chunks[kk][0] == "wd":
                            break
                    inst = None
                    for j in range(nhd):
                        inst = t.transpose(psc_b16[:, j * 128:(j + 1) * 128],
                                           wqc3[:, j * 128:(j + 1) * 128], S.ident_b[:])
                    inst.then_inc(P.s6T.h, 1)
                    P.s6T.v += 1
                else:
                    _, k, fi = it
                    t.wait_ge(P.s6V.h, tick_6v[("we", k)])
                    if fi > 0:
                        t.wait_ge(P.s6V.h, tick_6v[("fe", k, fi - 1)])
                    elif k > 0 and chunks[k - 1][0] == "ug":
                        t.wait_ge(P.s6V.h, tick_6v[("fe", k - 1, NF - 1)])
                    t0_ = fi * 512
                    fwd = min(512, T - t0_)
                    inst = None
                    for ic in range(CC):
                        inst = t.matmul(psc_fill[:, 0:fwd],
                                        wt[:, ic * 128:(ic + 1) * 128],
                                        _xn_ap(P)[:, ic * T + t0_:ic * T + t0_ + fwd],
                                        start=(ic == 0), stop=(ic == CC - 1))
                    inst.then_inc(P.s6T.h, 1)
                    P.s6T.v += 1

            def feed(n=1):
                for _ in range(n):
                    if fptr[0] >= len(flat_items):
                        return
                    s_it, it = flat_items[fptr[0]]
                    if s_it > cur_seg[0]:
                        return
                    fptr[0] += 1
                    emit_chunk_item(it)

            def feed_seg(s):
                while fptr[0] < len(flat_items) and flat_items[fptr[0]][0] <= s:
                    emit_chunk_item(flat_items[fptr[0]][1])
                    fptr[0] += 1

            def prev_hf(h, f):
                if f == 1:
                    return (h, 0)
                return (h - 1, 1) if h > 0 else None

            for h in range(H):
                pc, hp = h // 2, h % 2
                for f in range(2):
                    prev = prev_hf(h, f)
                    for j in range(TC):
                        pm = S.psa if j % 2 == 0 else S.psb
                        if j >= 2:
                            t.wait_ge(P.sS.h, tick_s[("exp", h, f, j - 2)])
                        elif prev is not None:
                            t.wait_ge(P.sS.h, tick_s[("exp", prev[0], prev[1], TC - 1)])
                        inst = None
                        for sl in range(0, th, 512):
                            sw = min(512, th - sl)
                            inst = t.matmul(pm[:, sl:sl + sw],
                                            S.k[hp * 64:hp * 64 + 64,
                                                pc * T + j * 128:pc * T + (j + 1) * 128],
                                            S.q[hp * 64:hp * 64 + 64,
                                                pc * T + f * th + sl:pc * T + f * th + sl + sw],
                                            start=True, stop=True,
                                            tile_position=(hp * 64, 0))
                        inst.then_inc(P.sT.h, 1)
                        P.sT.v += 1
                        if j >= 1:
                            _mega_av(P, t, tick_s, tick_v, h, f, j - 1, prev)
                        feed(1)
                    _mega_av(P, t, tick_s, tick_v, h, f, TC - 1, prev)
                    feed(2)
                    cur_seg[0] += 1
                for tau in range(TC):
                    f2, loc = divmod(tau, tph)
                    t.wait_ge(P.sS.h, tick_s[("xav", h, 1)])
                    if tau >= 8:
                        t.wait_ge(P.sV.h, tick_v[("dvb", h, (tau - 8) // 4)])
                    slot = (tau % 8) * 128
                    inst = t.transpose(
                        S.psd[:, slot:slot + 65],
                        S.sav[0:65, f2 * 1024 + loc * 128:f2 * 1024 + (loc + 1) * 128],
                        S.ident_f[0:65, 0:65])
                    inst.then_inc(P.sT.h, 1)
                    P.sT.v += 1
                if h % 2 == 1:
                    t.wait_ge(P.sV.h, tick_v[("dvb", h, (TC + 3) // 4 - 1)])
                    psdb = S.psd[:].bitcast(BF16)
                    inst = None
                    for tau in range(TC):
                        inst = t.transpose(psdb[:, tau * 128:(tau + 1) * 128],
                                           S.aot[:, tau * 128:(tau + 1) * 128],
                                           S.ident_b[:])
                    inst.then_inc(P.sT.h, 1)
                    P.sT.v += 1
            feed_seg(NSEG)

        @b.scalar
        def _(s):
            for h in range(H):
                for f in range(2):
                    for j in range(TC):
                        pm = S.psa if j % 2 == 0 else S.psb
                        s.wait_ge(P.sT.h, tick_t[("sc", h, f, j)])
                        if j >= 2:
                            s.wait_ge(P.sT.h, tick_t[("av", h, f, j - 2)])
                        ptb = S.pt[:, (j % 2) * 1024:(j % 2) * 1024 + th]
                        s.activation(ptb, pm[:, 0:th], AF.Exp,
                                     bias=S.negm[:, h:h + 1]).then_inc(P.sS.h, 1)
                        P.sS.v += 1
                    s.wait_ge(P.sT.h, tick_t[("av", h, f, TC - 1)])
                    s.activation(S.sav[0:65, f * 1024:f * 1024 + th],
                                 S.psd[0:65, 0:th], AF.Identity)
                    s.drain().then_inc(P.sS.h, 1)
                    P.sS.v += 1

        @b.vector
        def _(v):
            vit = iter(vitems)

            def emit_v(it):
                if it[0] == "q":
                    k = it[1]
                    ch = chunks[k]
                    v.wait_ge(P.sLa.h, bLa + 16 * (k + 1))
                    if k > 0:
                        pit = ("tr", k - 1) if chunks[k - 1][0] == "ug" else (
                            ("trp", k - 1) if chunks[k - 1][0] == "wp" else ("trd", k - 1))
                        v.wait_ge(P.s6T.h, tick_6t[pit])
                    if ch[0] == "ug":
                        _quant_blk(P, v, wld, wtmp, wld, wqc, C)
                    elif ch[0] == "wp":
                        _quant_row(P, v, wld, wtmp, wqc, None)
                    else:
                        _quant_blk(P, v, wld3, wtmp3, wld3, wqc3, IH)
                    v.drain().then_inc(P.s6V.h, 1)
                    P.s6V.v += 1
                elif it[0] == "we":
                    k = it[1]
                    v.wait_ge(P.s6T.h, tick_6t[("tr", k)])
                    v.tensor_copy(wt, psc_b16[:, 0:C])
                    v.drain().then_inc(P.s6V.h, 1)
                    P.s6V.v += 1
                elif it[0] == "fe":
                    _, k, fi = it
                    v.wait_ge(P.s6T.h, tick_6t[("fl", k, fi)])
                    if fi == 0:
                        si = spill_idx[k]
                        if si >= 2:
                            v.wait_ge(*spill_tick(si - 2))
                    fwd = min(512, T - fi * 512)
                    v.tensor_copy(cbuf(spill_idx[k] % 2)[:, fi * 512:fi * 512 + fwd],
                                  psc_fill[:, 0:fwd])
                    v.drain().then_inc(P.s6V.h, 1)
                    P.s6V.v += 1
                elif it[0] == "red":
                    k = it[1]
                    mcol = GT_MAX if chunks[k][1] == 0 else UP_MAX
                    v.reduce_max(S.sml[:, TMP0:TMP0 + 1], cbuf(spill_idx[k] % 2),
                                 axis=AX, apply_absolute_value=True)
                    v.drain()
                    v.tensor_max(S.sml[:, mcol:mcol + 1], S.sml[:, mcol:mcol + 1],
                                 S.sml[:, TMP0:TMP0 + 1])
                    v.drain().then_inc(P.s6V.h, 1)
                    P.s6V.v += 1
                elif it[0] == "a23":
                    v.wait_ge(P.sX.h, 32)
                    v.reduce_max(S.row[0:1, 0:1], S.row[0:1, 128:256], axis=AX)
                    v.drain()
                    v.reduce_max(S.row[0:1, 1:2], S.row[0:1, 256:384], axis=AX)
                    v.drain().then_inc(P.s6V.h, 1)
                    P.s6V.v += 1
                elif it[0] == "pe":
                    k = it[1]
                    v.wait_ge(P.s6T.h, tick_6t[("trp", k)])
                    si = spill_idx[k]
                    if si >= 2:
                        v.wait_ge(*spill_tick(si - 2))
                    v.tensor_copy(cbuf(si % 2)[:, 0:C], psc_b16[:, 0:C])
                    v.drain().then_inc(P.s6V.h, 1)
                    P.s6V.v += 1
                else:
                    k = it[1]
                    ch = chunks[k]
                    v.wait_ge(P.s6T.h, tick_6t[("trd", k)])
                    if ch[2] == 0 and k > 0:
                        for kk in range(k - 1, -1, -1):
                            if chunks[kk][0] == "wd" and chunks[kk][2] == 1:
                                v.wait_ge(*spill_tick(spill_idx[kk]))
                                break
                    v.tensor_copy(tstage[:, ch[2] * IH:ch[2] * IH + nhd * 128],
                                  psc_b16[:, 0:nhd * 128])
                    v.drain().then_inc(P.s6V.h, 1)
                    P.s6V.v += 1

            vseg = {s: _vseg_items(s) for s in range(-1, NSEG)}

            for it in vseg[-1]:
                emit_v(it)
            seg = 0
            for h in range(H):
                hp = h % 2
                for f in range(2):
                    for it in vseg[seg]:
                        emit_v(it)
                    seg += 1
                for bi in range((TC + 3) // 4):
                    lo = bi * 4
                    bs = min(4, TC - lo)
                    v.wait_ge(P.sT.h, tick_t[("t1", h, lo + bs - 1)])
                    ps3 = S.psd[:, (lo % 8) * 128:(lo % 8) * 128 + bs * 128
                                ].rearrange("p (k o) -> p k o", o=128)
                    v.reciprocal(S.sig[:, lo:lo + bs], ps3[:, :, 64:65].squeeze(2))
                    v.drain()
                    for tau in range(lo, lo + bs):
                        v.tensor_scalar(
                            S.aot[:, tau * 128 + hp * 64:tau * 128 + hp * 64 + 64],
                            S.psd[:, (tau % 8) * 128:(tau % 8) * 128 + 64],
                            S.sig[:, tau:tau + 1], None, OP.mult)
                        v.drain()
                    v.drain().then_inc(P.sV.h, 1)
                    P.sV.v += 1
                if h % 2 == 1:
                    v.wait_ge(P.sT.h, tick_t[("t2", h)])
                    psdb = S.psd[:].bitcast(BF16)
                    out = S.ao[:, (h // 2) * T:(h // 2) * T + TC * 128]
                    for c0 in range(0, TC * 128, 1024):
                        cw = min(1024, TC * 128 - c0)
                        v.tensor_copy(out[:, c0:c0 + cw], psdb[:, c0:c0 + cw])
                    v.drain().then_inc(P.sV.h, 1)
                    P.sV.v += 1

        @b.gpsimd
        def _(g):
            did_a23 = False
            for si, k in enumerate(spills):
                ch = chunks[k]
                if si >= 2:
                    g.wait_ge(*spill_tick(si - 2))
                if ch[0] == "ug":
                    g.wait_ge(P.s6V.h, tick_6v[("fe", k, NF - 1)])
                    g.dma_start(out=(D.s_gate if ch[1] == 0 else D.s_up)[
                        ch[2]:ch[2] + 1, :, :].squeeze(0),
                        in_=cbuf(si % 2)).then_inc(P.sGp[si % 2].h, 16)
                elif ch[0] == "wp":
                    g.wait_ge(P.s6V.h, tick_6v[("pe", k)])
                    oc = ch[1]
                    g.dma_start(
                        out=D.s_wpt[:, :, oc * 128:(oc + 1) * 128].transpose([1, 0, 2]),
                        in_=cbuf(si % 2)[:, 0:C].rearrange("p (i m) -> p i m", m=128)
                        ).then_inc(P.sGp[si % 2].h, 16)
                else:
                    g.wait_ge(P.s6V.h, tick_6v[("de", k)])
                    oc = ch[1]
                    g.dma_start(
                        out=D.s_wdt[:, :, oc:oc + 1, :].squeeze(2).transpose([1, 0, 2]),
                        in_=tstage.rearrange("p (i j) -> p i j", j=128),
                        allow_slow_non_contiguous=True).then_inc(P.sGp[si % 2].h, 16)
                P.sGp[si % 2].v += 16
                if not did_a23 and ch == ("ug", 1, IC - 1):
                    did_a23 = True
                    g.wait_ge(P.s6V.h, tick_6v[("red", k)])
                    g.dma_start(out=S.row[0:1, 128:256],
                                in_=S.sml[:, UP_MAX:UP_MAX + 1]).then_inc(P.sX.h, 16)
                    g.dma_start(out=S.row[0:1, 256:384],
                                in_=S.sml[:, GT_MAX:GT_MAX + 1]).then_inc(P.sX.h, 16)
                    P.sX.v += 32
                    g.wait_ge(P.s6V.h, tick_6v[("a23",)])
                    cin, cout = D.cc[1]
                    g.dma_start(out=cin[:], in_=S.row[0:1, 0:128]).then_inc(P.sX.h, 16)
                    P.sX.v += 16
                    g.wait_ge(P.sX.h, P.sX.v)
                    g.collective_compute(
                        "AllReduce", OP.max, replica_groups=[list(range(P.NCORES))],
                        ins=[cin[:]], outs=[cout[:]]).then_inc(P.sC.h, 1)
                    P.sC.v += 1
                    g.wait_ge(P.sC.h, P.sC.v)
                    g.dma_start(out=S.sml[:, UP_MAX:UP_MAX + 1],
                                in_=cout[0:1, 0:1].broadcast_to([128, 1])).then_inc(P.sX.h, 16)
                    g.dma_start(out=S.sml[:, GT_MAX:GT_MAX + 1],
                                in_=cout[0:1, 1:2].broadcast_to([128, 1])).then_inc(P.sX.h, 16)
                    P.sX.v += 32
                    g.wait_ge(P.sX.h, P.sX.v)
            g.wait_ge(P.sGa.h, P.sGa.v)
            g.wait_ge(P.sGb.h, P.sGb.v)


def _mega_av(P, t, tick_s, tick_v, h, f, j, prev):
    S = P.S
    TC, T, H = P.TC, P.T, P.H
    th = T // 2
    t.wait_ge(P.sS.h, tick_s[("exp", h, f, j)])
    if j == 0:
        if prev is not None:
            t.wait_ge(P.sS.h, tick_s[("xav", prev[0], prev[1])])
        if f == 0 and h > 0:
            if (h - 1) % 2 == 1:
                t.wait_ge(P.sV.h, tick_v[("aoev", h - 1)])
            else:
                t.wait_ge(P.sV.h, tick_v[("dvb", h - 1, (TC + 3) // 4 - 1)])
    vj = S.v[:, j * H * 65 + h * 65:j * H * 65 + h * 65 + 65]
    ptb = S.pt[:, (j % 2) * 1024:(j % 2) * 1024 + th]
    inst = None
    for sl in range(0, th, 512):
        sw = min(512, th - sl)
        inst = t.matmul(S.psd[0:65, sl:sl + sw], vj, ptb[:, sl:sl + sw],
                        start=(j == 0), stop=(j == TC - 1))
    inst.then_inc(P.sT.h, 1)
    P.sT.v += 1
